# version 05
# baseline (speedup 1.0000x reference)
import numpy as np
import ml_dtypes
from contextlib import ExitStack
import concourse.bass as bass
import concourse.mybir as mybir
from concourse.bass_utils import run_bass_kernel_spmd

F32 = mybir.dt.float32
BF16 = mybir.dt.bfloat16
U8 = mybir.dt.uint8
AF = mybir.ActivationFunctionType
ALU = mybir.AluOpType

NCORES = 8
B, T, D = 16, 2048, 1024
SEQ = 2
TOK = SEQ * T
FCH = D // 128
TSEG = 512
NSEG = TOK // TSEG
MAGIC = 12582912.0

ENGINES = ("sync", "tensor", "vector", "scalar", "gpsimd")



def tile_lhsT(W):
    K, M = W.shape
    assert K % 128 == 0 and M % 128 == 0, (K, M)
    t = W.reshape(K // 128, 128, M // 128, 128).transpose(2, 1, 0, 3)
    return np.ascontiguousarray(t.reshape(M // 128, 128, K))


def bias_tiles(b):
    M = b.shape[0]
    return np.ascontiguousarray(np.asarray(b).reshape(M // 128, 128).T.astype(np.float32))


def feat_major(x):
    x = np.asarray(x).reshape(-1, x.shape[-1])
    return np.ascontiguousarray(x.T.astype(np.float32))


def host_sel():
    a = np.zeros((128, 8, 8), dtype=np.float32)
    for j in range(8):
        a[:, j, j] = 1.0
    return a.reshape(128, 64).astype(ml_dtypes.bfloat16)


def host_rowsel():
    a = np.zeros((8, 4, 128), dtype=np.float32)
    for j in range(4):
        a[j, j, :] = 1.0
    return a.reshape(8, 512)


def host_eye():
    return np.eye(128, dtype=np.float32).astype(ml_dtypes.bfloat16)


AUX_ZERO, AUX_ONE, AUX_EPS5, AUX_EPS3, AUX_WKVEPS, AUX_HALFPI = 0, 1, 2, 3, 4, 5
AUXC = 8

def host_aux():
    a = np.zeros((128, AUXC), dtype=np.float32)
    a[:, AUX_ONE] = 1.0
    a[:, AUX_EPS5] = 1e-5
    a[:, AUX_EPS3] = 1e-3
    a[:, AUX_WKVEPS] = 1e-8
    a[:, AUX_HALFPI] = np.pi / 2
    return a



class Prog:
    def __init__(self, nc, es):
        self.nc, self.es = nc, es
        self.uid = 0
        self.S = {}
        self.C = {}
        for s in ("dma", "dmo", "pe", "dve", "act", "pool", "bar", "sp",
                  "dq0", "dq1", "dq2", "dq3", "dq4", "dq5", "dq6", "dq7"):
            self.S[s] = es.enter_context(nc.semaphore(s))
            self.C[s] = 0
        self.X = None
        self.aux = None
        self.sel = None
        self.rowsel = None
        self.dq_rr = 0
        self.dq_sw_rr = 0
        self.blk = None
        self.lastjoin = {e: {} for e in ENGINES}

    def block(self):
        if self.blk is not None:
            return _SharedBlk(self)
        return self.nc.Block()

    def take_dq(self, k, sw=False):
        if sw:
            out = [f"dq{6 + (self.dq_sw_rr + i) % 2}" for i in range(k)]
            self.dq_sw_rr = (self.dq_sw_rr + k) % 2
            return out
        out = [f"dq{(self.dq_rr + i) % 6}" for i in range(k)]
        self.dq_rr = (self.dq_rr + k) % 6
        return out

    def name(self, pfx):
        self.uid += 1
        return f"{pfx}{self.uid}"

    def sbuf(self, es, shape, dtype):
        return es.enter_context(self.nc.sbuf_tensor(self.name("t"), shape, dtype))

    def psum(self, es, shape, dtype=F32):
        return es.enter_context(self.nc.psum_tensor(self.name("p"), shape, dtype))

    def inc(self, instr, s, n=1):
        instr.then_inc(self.S[s], n)
        self.C[s] += n

    def fence(self, q, s):
        q.wait_ge(self.S[s], self.C[s])

    def barrier(p, engines=ENGINES):
        nc = p.nc
        ALLSEMS = ("sp", "pe", "dve", "act", "pool", "dma", "dmo",
                   "dq0", "dq1", "dq2", "dq3", "dq4", "dq5", "dq6", "dq7")
        snap = {nm: p.C[nm] for nm in ALLSEMS}

        def _emit(blk):
            for eng in engines:
                lj = p.lastjoin[eng]
                todo = [(nm, snap[nm]) for nm in ALLSEMS
                        if snap[nm] > lj.get(nm, 0)]
                if not todo:
                    continue

                def mk(items):
                    def _(q):
                        for nm, v in items:
                            q.wait_ge(p.S[nm], v)
                    return _
                getattr(blk, eng)(mk(todo))
                for nm, v in todo:
                    lj[nm] = v

        if p.blk is not None:
            _emit(p.blk)
        else:
            with nc.Block() as blk:
                _emit(blk)


class _SharedBlk:
    def __init__(self, p):
        self.p = p

    def __enter__(self):
        return self.p.blk

    def __exit__(self, exc_type, exc_val, exc_tb):
        if exc_type is None:
            self.p.barrier()
        return False


def emit_setup(p: Prog, x_in, aux_in, sel_in, rowsel_in, eye_in=None, eyef_in=None, load_x=True):
    nc = p.nc
    es = p.es
    p.X = p.sbuf(es, [128, FCH * TOK], F32)
    p.aux = p.sbuf(es, [128, AUXC], F32)
    p.sel = p.sbuf(es, [128, 64], BF16)
    p.rowsel = p.sbuf(es, [8, 512], F32)
    p.eye = p.sbuf(es, [128, 128], BF16) if eye_in is not None else None
    p.eyef = p.sbuf(es, [128, 128], F32) if eyef_in is not None else None
    with p.block() as blk:
        @blk.sync
        def _(s):
            if load_x:
                p.inc(s.dma_start(out=p.X[:].rearrange("p (c t) -> p c t", c=FCH), in_=x_in.rearrange("(c p) t -> p c t", p=128)), "dma", 16)
            p.inc(s.dma_start(out=p.aux[:], in_=aux_in), "dma", 16)
            p.inc(s.dma_start(out=p.sel[:], in_=sel_in), "dma", 16)
            p.inc(s.dma_start(out=p.rowsel[:], in_=rowsel_in), "dma", 16)
            if eye_in is not None:
                p.inc(s.dma_start(out=p.eye[:], in_=eye_in), "dma", 16)
            if eyef_in is not None:
                p.inc(s.dma_start(out=p.eyef[:], in_=eyef_in), "dma", 16)
        @blk.vector
        def _(v):
            v.wait_ge(p.S["dma"], p.C["dma"])
    p.barrier()


def emit_store_x(p: Prog, x_out):
    nc = p.nc
    p.barrier()
    with p.block() as blk:
        @blk.sync
        def _(s):
            p.inc(s.dma_start(out=x_out.rearrange("(c p) t -> p c t", p=128), in_=p.X[:].rearrange("p (c t) -> p c t", c=FCH)), "dmo", 16)
            s.wait_ge(p.S["dmo"], p.C["dmo"])



EPS_COL = {1e-5: AUX_EPS5, 1e-3: AUX_EPS3}

def emit_ln(p: Prog, src, dst=None, dst2=None, acc_into=None, eps=1e-5,
            nch=FCH, src_stride=None, src_tok0=0, ntok=TOK,
            dst_stride=None, dst_tok0=0, gamma=None, beta=None, gb_col0=0, beta_col0=None,
            scratch=None):
    nc = p.nc
    sstride = src_stride if src_stride is not None else TOK
    dstride = dst_stride if dst_stride is not None else sstride
    seglen = min(TSEG, ntok)
    nseg = ntok // seglen
    assert ntok % seglen == 0 and nseg <= 4
    invn = 1.0 / (nch * 128)
    epsap = p.aux[0:8, EPS_COL[eps]:EPS_COL[eps] + 1]
    if beta_col0 is None:
        beta_col0 = gb_col0

    plain = (gamma is None and dst2 is None and acc_into is None and dst is not None)
    gsimple = (gamma is not None and dst is not None and dst2 is None and acc_into is None)
    nwr = (0 if (plain or gsimple) else
           (1 if dst is not None else 0) + (1 if dst2 is not None else 0)
           + (1 if acc_into is not None else 0))
    chain_len = 2 + (1 if gamma is not None else 0) + nwr
    napply = nch * chain_len

    with ExitStack() as es:
        s1 = p.psum(es, [128, 512])
        s2 = p.psum(es, [128, 512])
        bc = [p.psum(es, [128, 1024]) for _ in range(2)]
        nscr = 1 if plain else (3 if gamma is not None else 2)
        if scratch is not None:
            o = 0
            scr = [scratch[:, o + i * seglen:o + (i + 1) * seglen] for i in range(nscr)]
            o += nscr * seglen
            rows = scratch[0:8, o:o + 3 * seglen]
            o += 3 * seglen
            h = seglen // 2
            xb = [scratch[:, o + i * h:o + (i + 1) * h].bitcast(BF16) for i in range(2)]
            o += 2 * h
            xq = [scratch[:, o + i * h:o + (i + 1) * h].bitcast(BF16) for i in range(2)]
        else:
            xb = [p.sbuf(es, [128, seglen], BF16) for _ in range(2)]
            xq = [p.sbuf(es, [128, seglen], BF16) for _ in range(2)]
            rows = p.sbuf(es, [8, 1536], F32)
            scr = [p.sbuf(es, [128, seglen], F32) for _ in range(nscr)]
        X1 = rows[0:8, 0:seglen]
        X2 = rows[0:8, seglen:2 * seglen]
        X3 = rows[0:8, 2 * seglen:3 * seglen]
        R_A, R_B = X3, X2
        base = dict(p.C)
        N = nseg * nch

        def src_ap(c, g):
            t0 = src_tok0 + g * seglen
            return src[:, c * sstride + t0: c * sstride + t0 + seglen]

        def dst_ap(buf, c, g):
            t0 = dst_tok0 + g * seglen
            return buf[:, c * dstride + t0: c * dstride + t0 + seglen]

        with p.block() as blk:
            @blk.scalar
            def _(a):
                for i in range(N):
                    if i >= 2:
                        a.wait_ge(p.S["pe"], base["pe"] + i - 1)
                    p.inc(a.activation(xq[i % 2][:], src_ap(i % nch, i // nch), AF.Square), "act")
                a.wait_ge(p.S["dve"], base["dve"] + N + 3)
                p.inc(a.activation(X2, X3, AF.Sqrt, bias=epsap, scale=1.0), "act")

            @blk.tensor
            def _(t):
                for i in range(N):
                    g = i // nch
                    t.wait_ge(p.S["dve"], base["dve"] + i + 1)
                    t.wait_ge(p.S["act"], base["act"] + i + 1)
                    first, last = (i == 0), (i == N - 1)
                    selj = p.sel[:, g * 8:(g + 1) * 8]
                    t.matmul(s1[0:8, 0:seglen], selj, xb[i % 2][:], start=first, stop=last)
                    p.inc(t.matmul(s2[0:8, 0:seglen], selj, xq[i % 2][:], start=first, stop=last), "pe")
                for g in range(nseg):
                    t.wait_ge(p.S["dve"], base["dve"] + N + 5 + (0 if g < 2 else (g - 1) * napply))
                    rj = p.rowsel[0:8, g * 128:(g + 1) * 128]
                    p.inc(t.matmul(bc[g % 2][:, 0:seglen], rj, X3, start=True, stop=True), "pe")
                    p.inc(t.matmul(bc[g % 2][:, 512:512 + seglen], rj, X2, start=True, stop=True), "pe")

            @blk.vector
            def _(v):
                for i in range(N):
                    if i >= 2:
                        v.wait_ge(p.S["pe"], base["pe"] + i - 1)
                    p.inc(v.tensor_copy(xb[i % 2][:], src_ap(i % nch, i // nch)), "dve")
                v.wait_ge(p.S["pe"], base["pe"] + N)
                p.inc(v.tensor_scalar(X1, s1[0:8, 0:seglen], invn, None, ALU.mult), "dve")
                p.fence(v, "dve")
                p.inc(v.tensor_tensor(X2, X1, X1, ALU.mult), "dve")
                p.fence(v, "dve")
                p.inc(v.scalar_tensor_tensor(X3, s2[0:8, 0:seglen], invn, X2, ALU.mult, ALU.subtract), "dve")
                v.wait_ge(p.S["act"], base["act"] + N + 1)
                p.inc(v.reciprocal(X3, X2), "dve")
                p.fence(v, "dve")
                p.inc(v.scalar_tensor_tensor(X2, X1, -1.0, X3, ALU.mult, ALU.mult), "dve")
                for g in range(nseg):
                    v.wait_ge(p.S["pe"], base["pe"] + N + 2 * (g + 1))
                    A = bc[g % 2][:, 0:seglen]
                    Bv = bc[g % 2][:, 512:512 + seglen]
                    for c in range(nch):
                        p.fence(v, "dve")
                        p.inc(v.tensor_tensor(scr[0][:], src_ap(c, g), A, ALU.mult), "dve")
                        p.fence(v, "dve")
                        if plain:
                            p.inc(v.tensor_tensor(dst_ap(dst, c, g), scr[0][:], Bv, ALU.add), "dve")
                            continue
                        if gamma is not None:
                            p.inc(v.tensor_tensor(scr[1][:], scr[0][:], Bv, ALU.add), "dve")
                            p.fence(v, "dve")
                            tgt = dst_ap(dst, c, g) if gsimple else scr[2][:]
                            p.inc(v.tensor_scalar(tgt, scr[1][:],
                                                  gamma[:, gb_col0 + c:gb_col0 + c + 1],
                                                  beta[:, beta_col0 + c:beta_col0 + c + 1], ALU.mult, ALU.add), "dve")
                            if gsimple:
                                continue
                            r = scr[2]
                        else:
                            p.inc(v.tensor_tensor(scr[1][:], scr[0][:], Bv, ALU.add), "dve")
                            r = scr[1]
                        p.fence(v, "dve")
                        writes = []
                        if dst is not None:
                            writes.append(("copy", dst))
                        if dst2 is not None:
                            writes.append(("copy", dst2))
                        if acc_into is not None:
                            writes.append(("acc", acc_into))
                        assert writes
                        for wi, (kind, buf) in enumerate(writes):
                            if kind == "copy":
                                ins = v.tensor_copy(dst_ap(buf, c, g), r[:])
                            else:
                                ins = v.tensor_tensor(dst_ap(buf, c, g), dst_ap(buf, c, g), r[:], ALU.add)
                            if wi == len(writes) - 1:
                                p.inc(ins, "dve")



ENG2SEM = {"vector": "dve", "scalar": "act", "gpsimd": "pool", "tensor": "pe"}

def emit_gemm(p: Prog, w_oc_ap, och, kch, rhs_fn, epi, njobs=1,
              wdtype=F32, epi_eng="scalar", cast_eng=None, wcast=None, kcw=8,
              wraw_mult=1, nbuf=2):
    nc = p.nc
    kcw = min(kcw, kch)
    assert kch % kcw == 0
    nw = kch // kcw
    L = rhs_fn(0, 0).shape[-1]
    assert L <= 512 and njobs <= 2
    if cast_eng is None:
        cast_eng = "vector" if epi_eng != "vector" else "gpsimd"
    assert cast_eng != epi_eng
    episem = ENG2SEM[epi_eng]
    castsem = ENG2SEM[cast_eng]
    with ExitStack() as es:
        wraw = [p.sbuf(es, [128, wraw_mult * kcw * 128], wdtype) for _ in range(nbuf)]
        wbf = [p.sbuf(es, [128, kcw * 128], BF16) for _ in range(nbuf)]
        ps = [p.psum(es, [128, 512]) for _ in range(2)]
        wq = p.take_dq(2)
        base = dict(p.C)

        with p.block() as blk:
            @blk.sync
            def _(s):
                for l in range(och * nw):
                    if l >= nbuf:
                        s.wait_ge(p.S["pe"], base["pe"] + l - nbuf + 1)
                    p.inc(s.dma_start(out=wraw[l % nbuf][:], in_=w_oc_ap(l // nw, l % nw)), wq[l % nbuf], 16)
                for b_ in range(nbuf):
                    s.wait_ge(p.S[wq[b_]], p.C[wq[b_]])

            def cast_sec(q):
                copyf = q.copy if cast_eng == "scalar" else q.tensor_copy
                for l in range(och * nw):
                    q.wait_ge(p.S[wq[l % nbuf]], base[wq[l % nbuf]] + 16 * (l // nbuf + 1))
                    if l >= nbuf:
                        q.wait_ge(p.S["pe"], base["pe"] + l - nbuf + 1)
                    if wcast is not None:
                        p.inc(wcast(q, wbf[l % nbuf][:], wraw[l % nbuf][:]), castsem)
                    else:
                        p.inc(copyf(wbf[l % nbuf][:], wraw[l % nbuf][:]), castsem)

            def epi_sec(q):
                for oc in range(och):
                    for j in range(njobs):
                        idx = oc * njobs + j
                        q.wait_ge(p.S["pe"], base["pe"] + (oc + 1) * nw)
                        p.inc(epi(q, oc, j, ps[idx % 2][:, 0:L]), episem)

            getattr(blk, cast_eng)(cast_sec)
            getattr(blk, epi_eng)(epi_sec)

            @blk.tensor
            def _(t):
                for oc in range(och):
                    for h in range(nw):
                        l = oc * nw + h
                        t.wait_ge(p.S[castsem], base[castsem] + l + 1)

                        for j in range(njobs):
                            idx = oc * njobs + j
                            if h == 0 and idx >= 2:
                                t.wait_ge(p.S[episem], base[episem] + idx - 1)
                            for k in range(kcw):
                                kc = h * kcw + k
                                p_ins = t.matmul(ps[idx % 2][:, 0:L],
                                                 wbf[l % nbuf][:, k * 128:(k + 1) * 128],
                                                 rhs_fn(kc, j),
                                                 start=(kc == 0), stop=(kc == kch - 1))
                        p.inc(p_ins, "pe")



def emit_l3ffn(p: Prog, w1_ap, b1_ap, w2_ap, b2_ap):
    nc = p.nc
    with ExitStack() as es:
        b1s = p.sbuf(es, [128, 32], F32)
        b2s = p.sbuf(es, [128, 8], F32)
        Hseg = p.sbuf(es, [128, FCH * TSEG], BF16)
        Mseg = p.sbuf(es, [128, 32 * TSEG], BF16)
        Yseg = p.sbuf(es, [128, FCH * TSEG], F32)
        with p.block() as blk:
            @blk.sync
            def _(s):
                p.inc(s.dma_start(out=b1s[:], in_=b1_ap), "dma", 16)
                p.inc(s.dma_start(out=b2s[:], in_=b2_ap), "dma", 16)
        p.barrier()
        for g in range(NSEG):
            emit_ln(p, p.X, dst=Hseg, ntok=TSEG, src_tok0=g * TSEG,
                    dst_stride=TSEG, dst_tok0=0)

            def epi1(a, oc, j, psv):
                return a.activation(Mseg[:, oc * TSEG:(oc + 1) * TSEG], psv,
                                    AF.Relu, bias=b1s[:, oc:oc + 1], scale=1.0)
            emit_gemm(p, lambda oc, h: w1_ap[oc, :, h * 1024:(h + 1) * 1024], 32, FCH,
                      lambda kc, j: Hseg[:, kc * TSEG:(kc + 1) * TSEG], epi1)

            def epi2(a, oc, j, psv):
                return a.activation(Yseg[:, oc * TSEG:(oc + 1) * TSEG], psv,
                                    AF.Identity, bias=b2s[:, oc:oc + 1], scale=1.0)
            emit_gemm(p, lambda oc, h: w2_ap[oc, :, h * 1024:(h + 1) * 1024], FCH, 32,
                      lambda kc, j: Mseg[:, kc * TSEG:(kc + 1) * TSEG], epi2)

            emit_ln(p, Yseg, acc_into=p.X, ntok=TSEG, src_stride=TSEG,
                    dst_stride=TOK, dst_tok0=g * TSEG)


def prep_l3ffn(ins):
    return {
        "L3f_w1": tile_lhsT(ins["L3_ffn_W1"].astype(np.float32)),
        "L3f_b1": bias_tiles(ins["L3_ffn_b1"]),
        "L3f_w2": tile_lhsT(ins["L3_ffn_W2"].astype(np.float32)),
        "L3f_b2": bias_tiles(ins["L3_ffn_b2"]),
    }


def decl_l3ffn(nc):
    return dict(
        w1_ap=nc.dram_tensor("L3f_w1", [32, 128, 1024], F32, kind="ExternalInput").ap(),
        b1_ap=nc.dram_tensor("L3f_b1", [128, 32], F32, kind="ExternalInput").ap(),
        w2_ap=nc.dram_tensor("L3f_w2", [8, 128, 4096], F32, kind="ExternalInput").ap(),
        b2_ap=nc.dram_tensor("L3f_b2", [128, 8], F32, kind="ExternalInput").ap(),
    )


def emit_l2ffn(p: Prog, k0_ap, k1_ap, k2_ap, s3_ap):
    nc = p.nc
    SEG = 256
    NSG = TOK // SEG
    with ExitStack() as es:
        sbc = p.sbuf(es, [128, 3], F32)
        s3t = p.sbuf(es, [1, 3], F32)
        psb = p.psum(es, [128, 512])
        Hseg = p.sbuf(es, [128, FCH * SEG], BF16)
        A1 = p.sbuf(es, [128, 32 * SEG], BF16)
        A2 = p.sbuf(es, [128, 32 * SEG], BF16)
        Yseg = p.sbuf(es, [128, FCH * SEG], F32)
        base = dict(p.C)
        with p.block() as blk:
            @blk.sync
            def _(s):
                p.inc(s.dma_start(out=s3t[:], in_=s3_ap), "dma", 16)
            @blk.tensor
            def _(t):
                t.wait_ge(p.S["dma"], p.C["dma"])
                p.inc(t.matmul(psb[:, 0:3], p.rowsel[0:1, 0:128], s3t[:], start=True, stop=True), "pe")
            @blk.vector
            def _(v):
                v.wait_ge(p.S["pe"], p.C["pe"])
                p.inc(v.tensor_scalar(sbc[:, 0:1], psb[:, 0:1], 4.0, None, ALU.mult), "dve")
                p.inc(v.tensor_scalar(sbc[:, 1:2], psb[:, 1:2], 2.0, None, ALU.mult), "dve")
                p.inc(v.tensor_scalar(sbc[:, 2:3], psb[:, 2:3], 1.0, None, ALU.mult), "dve")
        p.barrier()

        def subcast(q, dst, src):
            n = dst.shape[-1]
            return q.tensor_tensor(dst, src[:, 0:n], src[:, n:2 * n], ALU.subtract)

        def affcast(q, dst, src):
            return q.tensor_scalar(dst, src, 2.0, -1.0, ALU.mult, ALU.add)

        for g in range(NSG):
            emit_ln(p, p.X, dst=Hseg, ntok=SEG, src_tok0=g * SEG,
                    dst_stride=SEG, dst_tok0=0)

            def epi0(v, oc, j, psv):
                return v.tensor_scalar(A1[:, oc * SEG:(oc + 1) * SEG], psv,
                                       sbc[:, 0:1], 0.0, ALU.mult, ALU.max)
            emit_gemm(p, lambda oc, h: k0_ap[oc, :, :], 32, FCH,
                      lambda kc, j: Hseg[:, kc * SEG:(kc + 1) * SEG], epi0,
                      wdtype=U8, epi_eng="vector", cast_eng="gpsimd",
                      wcast=subcast, kcw=FCH, wraw_mult=2)

            def epi1(v, oc, j, psv):
                return v.tensor_scalar(A2[:, oc * SEG:(oc + 1) * SEG], psv,
                                       sbc[:, 1:2], 0.0, ALU.mult, ALU.max)
            emit_gemm(p, lambda oc, h: k1_ap[oc, :, h * 1024:(h + 1) * 1024], 32, 32,
                      lambda kc, j: A1[:, kc * SEG:(kc + 1) * SEG], epi1,
                      wdtype=U8, epi_eng="vector", cast_eng="gpsimd", wcast=affcast)

            def epi2(v, oc, j, psv):
                return v.tensor_scalar(Yseg[:, oc * SEG:(oc + 1) * SEG], psv,
                                       sbc[:, 2:3], None, ALU.mult)
            emit_gemm(p, lambda oc, h: k2_ap[oc, :, h * 1024:(h + 1) * 1024], FCH, 32,
                      lambda kc, j: A2[:, kc * SEG:(kc + 1) * SEG], epi2,
                      wdtype=U8, epi_eng="vector", cast_eng="gpsimd", wcast=affcast)

            emit_ln(p, Yseg, acc_into=p.X, ntok=SEG, src_stride=SEG,
                    dst_stride=TOK, dst_tok0=g * SEG)


def prep_l2ffn(ins):
    k0 = np.asarray(ins["L2_ffn_k0"]).astype(np.uint8)
    k1 = np.asarray(ins["L2_ffn_k1"]).astype(np.uint8)
    k2 = np.asarray(ins["L2_ffn_k2"]).astype(np.uint8)
    k0b = np.concatenate([tile_lhsT(k0[:1024]), tile_lhsT(k0[1024:])], axis=2)
    s3 = np.array([[ins["L2_ffn_s0"], ins["L2_ffn_s1"], ins["L2_ffn_s2"]]], dtype=np.float32)
    return {
        "L2f_k0": np.ascontiguousarray(k0b),
        "L2f_k1": tile_lhsT(k1),
        "L2f_k2": tile_lhsT(k2),
        "L2f_s3": s3,
    }


def decl_l2ffn(nc):
    return dict(
        k0_ap=nc.dram_tensor("L2f_k0", [32, 128, 2048], U8, kind="ExternalInput").ap(),
        k1_ap=nc.dram_tensor("L2f_k1", [32, 128, 4096], U8, kind="ExternalInput").ap(),
        k2_ap=nc.dram_tensor("L2f_k2", [8, 128, 4096], U8, kind="ExternalInput").ap(),
        s3_ap=nc.dram_tensor("L2f_s3", [1, 3], F32, kind="ExternalInput").ap(),
    )


from concourse.ap import AP as BassAP


def emit_l3attn(p: Prog, text_ap, wqkv_ap, gb_ap):
    nc = p.nc
    temp = nc.dram_tensor("l3bias", [16, 16, 16, 128, 128], BF16).ap()

    with ExitStack() as es:
        qraw = [p.sbuf(es, [128, 2048], F32) for _ in range(2)]
        qbf = [p.sbuf(es, [128, 2048], BF16) for _ in range(2)]
        pst = [p.psum(es, [128, 128], BF16) for _ in range(2)]
        stage = [p.sbuf(es, [128, 2048], BF16) for _ in range(2)]
        lq = p.take_dq(2)
        sq = p.take_dq(2, sw=True)
        base = dict(p.C)
        NIT = 256

        def qm_src(h, it, phi):
            baseoff = -2048 * h + 32752 - 128 * it - 16 * phi
            return BassAP(text_ap.tensor, baseoff, [[2048, 16], [1, 2048]])

        with p.block() as blk:
            @blk.sync
            def _(s):
                for i in range(NIT):
                    h, it = i // 16, i % 16
                    if i >= 2:
                        s.wait_ge(p.S["dve"], base["dve"] + (i - 1) * 17 + 1)
                    for phi in range(8):
                        p.inc(s.dma_start(out=qraw[i % 2][phi * 16:(phi + 1) * 16, :],
                                          in_=qm_src(h, it, phi)), lq[i % 2], 16)
            @blk.vector
            def _(v):
                for i in range(NIT):
                    v.wait_ge(p.S[lq[i % 2]], base[lq[i % 2]] + 128 * (i // 2 + 1))
                    if i >= 2:
                        v.wait_ge(p.S["pe"], base["pe"] + 16 * (i - 1))
                    p.inc(v.tensor_copy(qbf[i % 2][:], qraw[i % 2][:]), "dve")
                    for j in range(16):
                        v.wait_ge(p.S["pe"], base["pe"] + 16 * i + j + 1)
                        if i >= 2 and j == 0:
                            v.wait_ge(p.S[sq[i % 2]], base[sq[i % 2]] + 16 * (i // 2))
                        p.inc(v.tensor_copy(stage[i % 2][:, j * 128:(j + 1) * 128], pst[j % 2][:]), "dve")
            @blk.tensor
            def _(t):
                for i in range(NIT):
                    t.wait_ge(p.S["dve"], base["dve"] + i * 17 + 1)
                    for j in range(16):
                        if j >= 2:
                            t.wait_ge(p.S["dve"], base["dve"] + i * 17 + 1 + j - 1)
                        elif i >= 1:
                            t.wait_ge(p.S["dve"], base["dve"] + i * 17)
                        p.inc(t.transpose(pst[j % 2][:], qbf[i % 2][:, j * 128:(j + 1) * 128], p.eye[:]), "pe")
            @blk.gpsimd
            def _(g):
                for i in range(NIT):
                    h, it = i // 16, i % 16
                    g.wait_ge(p.S["dve"], base["dve"] + (i + 1) * 17)
                    p.inc(g.dma_start(out=temp[h, it].rearrange("jt p i -> p jt i"), in_=stage[i % 2][:].rearrange("p (jt i) -> p jt i", jt=16)),
                          sq[i % 2], 16)
        p.barrier()

    ao_dram = nc.dram_tensor("l3ao", [8, 128, 2048], BF16).ap()
    with ExitStack() as es:
        gb = p.sbuf(es, [128, 16], F32)
        Hseq = p.sbuf(es, [128, FCH * 2048], BF16)
        wraw = p.sbuf(es, [128, 512], F32)
        wbf = p.sbuf(es, [128, 1536], BF16)
        QK = p.sbuf(es, [64, 2 * 2048], BF16)
        Vst = p.sbuf(es, [64, 512], BF16)
        Vkm = p.sbuf(es, [128, 16 * 65], BF16)
        Ptile = [p.sbuf(es, [128, 512], BF16) for _ in range(1)]
        btile = [p.sbuf(es, [128, 512], BF16) for _ in range(1)]
        rz = p.sbuf(es, [1, 512], F32)
        oT = p.sbuf(es, [64, 512], BF16)
        oT2 = p.sbuf(es, [64, 512], BF16)
        AOst = p.sbuf(es, [64, 512], BF16)
        SEG2 = 256
        AOseg = p.sbuf(es, [128, FCH * SEG2], BF16)
        Yseg = p.sbuf(es, [128, FCH * SEG2], BF16)
        wq_s = p.take_dq(2)
        bq_s = p.take_dq(2)
        aoq = p.take_dq(1, sw=True)[0]
        with p.block() as blk:
            @blk.sync
            def _(s):
                p.inc(s.dma_start(out=gb[:], in_=gb_ap), "dma", 16)
        p.barrier()

        for seq in range(SEQ):
            emit_ln(p, p.X, dst=Hseq, ntok=2048, src_tok0=seq * 2048, dst_stride=2048, dst_tok0=0)
            for h in range(16):
                with p.block() as blk:
                    basew = dict(p.C)
                    @blk.sync
                    def _(s):
                        for t3 in range(3):
                            if t3 >= 1:
                                s.wait_ge(p.S["dve"], basew["dve"] + t3)
                            p.inc(s.dma_start(out=wraw[:], in_=wqkv_ap[h, t3]), wq_s[0], 16)
                    @blk.vector
                    def _(v):
                        for t3 in range(3):
                            v.wait_ge(p.S[wq_s[0]], basew[wq_s[0]] + 16 * (t3 + 1))
                            p.inc(v.tensor_copy(wbf[:, t3 * 512:(t3 + 1) * 512], wraw[:]), "dve")
                with ExitStack() as es2:
                    ps_kv = [p.psum(es2, [64, 512]) for _ in range(2)]
                    ps_vt = [p.psum(es2, [128, 64], BF16) for _ in range(2)]
                    base2 = dict(p.C)
                    with p.block() as blk:
                        @blk.tensor
                        def _(t):
                            t.wait_ge(p.S["dve"], base2["dve"])
                            for i in range(12):
                                t3, sg = i // 4, i % 4
                                if i >= 2:
                                    t.wait_ge(p.S["act"], base2["act"] + i - 1)
                                for kc in range(FCH):
                                    ins = t.matmul(ps_kv[i % 2][:],
                                                   wbf[:, t3 * 512 + kc * 64:t3 * 512 + (kc + 1) * 64],
                                                   Hseq[:, kc * 2048 + sg * 512:kc * 2048 + (sg + 1) * 512],
                                                   start=(kc == 0), stop=(kc == FCH - 1))
                                p.inc(ins, "pe")
                                if t3 == 2:
                                    for j in range(4):
                                        kt = sg * 4 + j
                                        t.wait_ge(p.S["act"], base2["act"] + i + 1)
                                        if kt >= 2:
                                            t.wait_ge(p.S["dve"], base2["dve"] + kt)
                                        p.inc(t.transpose(ps_vt[kt % 2][:], Vst[:, j * 128:(j + 1) * 128], p.eye[0:64, 0:64]), "pe")
                        @blk.scalar
                        def _(a):
                            for i in range(12):
                                t3, sg = i // 4, i % 4
                                npe = i + 1 + max(0, (i - 8)) * 4
                                a.wait_ge(p.S["pe"], base2["pe"] + npe)
                                dst = (Vst[:, 0:512] if t3 == 2 else
                                       QK[0:64, t3 * 2048 + sg * 512:t3 * 2048 + (sg + 1) * 512])
                                p.inc(a.activation(dst, ps_kv[i % 2][:], AF.Identity, scale=1.0), "act")
                        @blk.vector
                        def _(v):
                            p.fence(v, "dve")
                            p.inc(v.memset(Vkm[:].rearrange("p (kt c) -> p kt c", c=65)[:, :, 64:65], 1.0), "dve")
                            for kt in range(16):
                                sg = kt // 4
                                i = 8 + sg
                                npe = i + 1 + sg * 4 + (kt % 4) + 1
                                v.wait_ge(p.S["pe"], base2["pe"] + npe)
                                p.inc(v.tensor_copy(Vkm[:, kt * 65:kt * 65 + 64], ps_vt[kt % 2][:]), "dve")
                    p.barrier(("tensor", "vector", "scalar"))
                with ExitStack() as es2:
                    ps_s = [p.psum(es2, [128, 512]) for _ in range(2)]
                    ps_o = p.psum(es2, [128, 512])
                    ps_o2 = p.psum(es2, [128, 512])
                    ps_z = p.psum(es2, [64, 512])
                    for qs in range(4):
                        base3 = dict(p.C)
                        with p.block() as blk:
                            @blk.sync
                            def _(s):
                                for kt in range(16):
                                    if kt >= 1:
                                        s.wait_ge(p.S["pe"], base3["pe"] + 3 * kt)
                                    p.inc(s.dma_start(out=btile[0][:].rearrange("p (itt i) -> p itt i", itt=4),
                                                      in_=BassAP(temp.tensor,
                                                                 ((h * 16 + 4 * qs) * 16 + kt) * 16384,
                                                                 [[128, 128], [16384 * 16, 4], [1, 128]])), bq_s[0], 16)
                            @blk.tensor
                            def _(t):
                                for kt in range(16):
                                    if kt >= 2:
                                        t.wait_ge(p.S["act"], base3["act"] + kt - 1)
                                    ins = t.matmul(ps_s[kt % 2][:],
                                                   QK[0:64, 1 * 2048 + kt * 128:1 * 2048 + (kt + 1) * 128],
                                                   QK[0:64, 0 * 2048 + qs * 512:0 * 2048 + (qs + 1) * 512],
                                                   start=True, stop=True)
                                    p.inc(ins, "pe")
                                    t.wait_ge(p.S["act"], base3["act"] + kt + 1)
                                    p.inc(t.matmul(ps_o[0:65, :], Vkm[:, kt * 65:(kt + 1) * 65], Ptile[0][:],
                                                   start=(kt == 0), stop=(kt == 15)), "pe")
                                    t.wait_ge(p.S[bq_s[0]], base3[bq_s[0]] + 16 * (kt + 1))
                                    p.inc(t.matmul(ps_o2[0:64, :], Vkm[:, kt * 65:kt * 65 + 64], btile[0][:],
                                                   start=(kt == 0), stop=(kt == 15)), "pe")
                            @blk.scalar
                            def _(a):
                                for kt in range(16):
                                    a.wait_ge(p.S["pe"], base3["pe"] + 3 * kt + 1)
                                    p.inc(a.activation(Ptile[0][:], ps_s[kt % 2][:], AF.Exp, scale=0.03125), "act")
                        base4 = dict(p.C)
                        with p.block() as blk:
                            @blk.vector
                            def _(v):
                                v.wait_ge(p.S["pe"], base4["pe"])
                                p.inc(v.reciprocal(rz[:], ps_o[64:65, :]), "dve")
                            @blk.tensor
                            def _(t):
                                t.wait_ge(p.S["dve"], p.C["dve"])
                                p.inc(t.matmul(ps_z[:], p.rowsel[0:1, 0:64], rz[:], start=True, stop=True), "pe")
                            @blk.scalar
                            def _(a):
                                a.wait_ge(p.S["pe"], base4["pe"])
                                p.inc(a.activation(oT[:], ps_o[0:64, :], AF.Identity, scale=1.0), "act")
                        with p.block() as blk:
                            @blk.vector
                            def _(v):
                                v.wait_ge(p.S["pe"], p.C["pe"])
                                v.wait_ge(p.S["act"], p.C["act"])
                                p.inc(v.tensor_tensor(oT2[:], oT[:], ps_z[:], ALU.mult), "dve")
                                p.fence(v, "dve")
                                p.inc(v.tensor_tensor(AOst[:], oT2[:], ps_o2[0:64, :], ALU.add), "dve")
                            @blk.gpsimd
                            def _(g):
                                g.wait_ge(p.S["dve"], p.C["dve"])
                                dst = BassAP(ao_dram.tensor,
                                             (h // 2) * 128 * 2048 + (h % 2) * 64 * 2048 + qs * 512,
                                             [[2048, 64], [1, 512]])
                                p.inc(g.dma_start(out=dst, in_=AOst[:]), aoq, 16)
                        p.barrier(("tensor", "vector", "scalar", "sync", "gpsimd"))
            for sg in range(2048 // SEG2):
                base5 = dict(p.C)
                with p.block() as blk:
                    @blk.sync
                    def _(s):
                        p.inc(s.dma_start(out=AOseg[:].rearrange("p (c t) -> p c t", c=FCH),
                                          in_=ao_dram[:, :, sg * SEG2:(sg + 1) * SEG2].rearrange("c p t -> p c t")),
                              "dma", 16)
                    @blk.vector
                    def _(v):
                        v.wait_ge(p.S["dma"], p.C["dma"])
                p.barrier()
                emit_ln(p, AOseg, dst=Yseg, ntok=SEG2, src_stride=SEG2,
                        dst_stride=SEG2, dst_tok0=0, eps=1e-3, gamma=gb, beta=gb,
                        gb_col0=0, beta_col0=8)
                emit_ln(p, Yseg, acc_into=p.X, ntok=SEG2, src_stride=SEG2,
                        dst_stride=TOK, dst_tok0=seq * 2048 + sg * SEG2)


def prep_l3attn(ins):
    T = np.asarray(ins["L3_attn_rel_table"]).astype(np.float32)
    T_ext = np.concatenate([T, np.repeat(T[2046:2047], 4096 - 2047, axis=0)], axis=0)
    def headtiles(W):
        return np.ascontiguousarray(
            np.asarray(W).astype(np.float32).reshape(8, 128, 16, 64).transpose(2, 1, 0, 3).reshape(16, 128, 512))
    wq = headtiles(ins["L3_attn_Wq"])
    wk = headtiles(ins["L3_attn_Wk"])
    wv = headtiles(ins["L3_attn_Wv"])
    wqkv = np.ascontiguousarray(np.stack([wq, wk, wv], axis=1))
    gb = np.concatenate([bias_tiles(ins["L3_attn_gamma"]), bias_tiles(ins["L3_attn_beta"])], axis=1)
    return {
        "L3a_text": np.ascontiguousarray(T_ext.reshape(-1)),
        "L3a_wqkv": wqkv,
        "L3a_gb": np.ascontiguousarray(gb),
    }


def decl_l3attn(nc):
    return dict(
        text_ap=nc.dram_tensor("L3a_text", [4096 * 16], F32, kind="ExternalInput").ap(),
        wqkv_ap=nc.dram_tensor("L3a_wqkv", [16, 3, 128, 512], F32, kind="ExternalInput").ap(),
        gb_ap=nc.dram_tensor("L3a_gb", [128, 16], F32, kind="ExternalInput").ap(),
    )


def emit_l0attn(p: Prog, wqkv_ap, wo_ap):
    nc = p.nc
    ao_dram = nc.dram_tensor("l0ao", [8, 128, 2048], BF16).ap()
    with ExitStack() as es:
        Hseq = p.sbuf(es, [128, FCH * 2048], BF16)
        wraw = p.sbuf(es, [128, 512], F32)
        wbf = p.sbuf(es, [128, 1536], BF16)
        Qaug = p.sbuf(es, [66, 2048], BF16)
        Kaug = p.sbuf(es, [66, 2048], BF16)
        qnrow = p.sbuf(es, [1, 2048], BF16)
        arow = p.sbuf(es, [1, 2048], BF16)
        brow = p.sbuf(es, [1, 512], BF16)
        Vkm = p.sbuf(es, [128, 16 * 65], BF16)
        cbuf = p.sbuf(es, [128, 4096], F32)
        Vst = cbuf[0:64, 0:256].bitcast(BF16)
        sq2 = [cbuf[0:64, 256:512].bitcast(BF16)]
        c0 = cbuf[:, 0:1024]
        c1 = cbuf[:, 1024:2048]
        c2 = cbuf[:, 2048:3072]
        Ptile = cbuf[:, 3072:3584].bitcast(BF16)
        oT = Ptile[0:64, 512:1024]
        AOst = Ptile[64:128, 512:1024]
        SEG2 = 256
        AOseg = cbuf[:, 2048:3072].bitcast(BF16)
        Yseg = cbuf[:, 0:2048]
        wq_s = p.take_dq(2)
        qnq = p.take_dq(1)[0]
        aoq = p.take_dq(1, sw=True)[0]

        for seq in range(SEQ):
            emit_ln(p, p.X, dst=Hseq, ntok=2048, src_tok0=seq * 2048, dst_stride=2048, dst_tok0=0, scratch=cbuf[:, 0:3072])
            for h in range(16):
                with p.block() as blk:
                    basew = dict(p.C)
                    @blk.sync
                    def _(s):
                        for t3 in range(3):
                            if t3 >= 1:
                                s.wait_ge(p.S["dve"], basew["dve"] + t3)
                            p.inc(s.dma_start(out=wraw[:], in_=wqkv_ap[h, t3]), wq_s[0], 16)
                    @blk.vector
                    def _(v):
                        for t3 in range(3):
                            v.wait_ge(p.S[wq_s[0]], basew[wq_s[0]] + 16 * (t3 + 1))
                            p.inc(v.tensor_copy(wbf[:, t3 * 512:(t3 + 1) * 512], wraw[:]), "dve")
                with ExitStack() as es2:
                    ps_kv = [p.psum(es2, [64, 512]) for _ in range(2)]
                    ps_vt = [p.psum(es2, [128, 64], BF16) for _ in range(2)]
                    ps_n = p.psum(es2, [1, 2048])
                    base2 = dict(p.C)

                    def pe_T(b):
                        return [6, 11, 16, 20][b]

                    def pe_proj(sg):
                        return [1, 2, 7, 12][sg]

                    with p.block() as blk:
                        @blk.tensor
                        def _(t):
                            t.wait_ge(p.S["dve"], base2["dve"])
                            for sg in range(4):
                                if sg >= 2:
                                    t.wait_ge(p.S["act"], base2["act"] + sg - 1)
                                for kc in range(FCH):
                                    ins = t.matmul(ps_kv[sg % 2][:],
                                                   wbf[:, 2 * 512 + kc * 64:2 * 512 + (kc + 1) * 64],
                                                   Hseq[:, kc * 2048 + sg * 512:kc * 2048 + (sg + 1) * 512],
                                                   start=(kc == 0), stop=(kc == FCH - 1))
                                p.inc(ins, "pe")
                                if sg >= 1:
                                    b = sg - 1
                                    t.wait_ge(p.S["act"], base2["act"] + b + 1)
                                    for j in range(4):
                                        kt = b * 4 + j
                                        if kt >= 2:
                                            t.wait_ge(p.S["dve"], base2["dve"] + kt)
                                        p.inc(t.transpose(ps_vt[kt % 2][:], Vst[:, j * 128:(j + 1) * 128], p.eye[0:64, 0:64]), "pe")
                            t.wait_ge(p.S["act"], base2["act"] + 4)
                            for j in range(4):
                                kt = 12 + j
                                t.wait_ge(p.S["dve"], base2["dve"] + kt)
                                p.inc(t.transpose(ps_vt[kt % 2][:], Vst[:, j * 128:(j + 1) * 128], p.eye[0:64, 0:64]), "pe")
                        @blk.scalar
                        def _(a):
                            for sg in range(4):
                                a.wait_ge(p.S["pe"], base2["pe"] + pe_proj(sg))
                                if sg >= 1:
                                    a.wait_ge(p.S["pe"], base2["pe"] + pe_T(sg - 1))
                                p.inc(a.activation(Vst[:], ps_kv[sg % 2][:], AF.Identity, scale=1.0), "act")
                        @blk.vector
                        def _(v):
                            p.fence(v, "dve")
                            p.inc(v.memset(Vkm[:].rearrange("p (kt c) -> p kt c", c=65)[:, :, 64:65], 1.0), "dve")
                            for kt in range(16):
                                b, j = kt // 4, kt % 4
                                pos = (3 + b * 5 + j) if b < 3 else (17 + j)
                                v.wait_ge(p.S["pe"], base2["pe"] + pos)
                                p.inc(v.tensor_copy(Vkm[:, kt * 65:kt * 65 + 64], ps_vt[kt % 2][:]), "dve")
                    p.barrier(("tensor", "vector", "scalar"))
                    base2 = dict(p.C)
                    with p.block() as blk:
                        @blk.tensor
                        def _(t):
                            for i in range(8):
                                t3, sg = i // 4, i % 4
                                if i >= 2:
                                    t.wait_ge(p.S["act"], base2["act"] + i - 1)
                                for kc in range(FCH):
                                    ins = t.matmul(ps_kv[i % 2][:],
                                                   wbf[:, t3 * 512 + kc * 64:t3 * 512 + (kc + 1) * 64],
                                                   Hseq[:, kc * 2048 + sg * 512:kc * 2048 + (sg + 1) * 512],
                                                   start=(kc == 0), stop=(kc == FCH - 1))
                                p.inc(ins, "pe")
                        @blk.scalar
                        def _(a):
                            for i in range(8):
                                t3, sg = i // 4, i % 4
                                a.wait_ge(p.S["pe"], base2["pe"] + i + 1)
                                dst = (Qaug if t3 == 0 else Kaug)[0:64, sg * 512:(sg + 1) * 512]
                                p.inc(a.activation(dst, ps_kv[i % 2][:], AF.Identity,
                                                   scale=(-2.0 if t3 == 0 else 1.0)), "act")
                    p.barrier(("tensor", "scalar"))
                    base2 = dict(p.C)
                    with p.block() as blk:
                        @blk.vector
                        def _(v):
                            p.fence(v, "dve")
                            p.inc(v.memset(Qaug[64:66, :], 1.0), "dve")
                            p.inc(v.memset(Kaug[64:66, :], 1.0), "dve")
                            for i in range(8):
                                t3, sg = i // 4, i % 4
                                if i >= 1:
                                    v.wait_ge(p.S["pe"], base2["pe"] + i)
                                src = (Qaug if t3 == 0 else Kaug)[0:64, sg * 512:(sg + 1) * 512]
                                p.inc(v.tensor_tensor(sq2[0][:], src, src, ALU.mult), "dve")
                        @blk.tensor
                        def _(t):
                            for i in range(8):
                                t3, sg = i // 4, i % 4
                                t.wait_ge(p.S["dve"], base2["dve"] + 2 + i + 1)
                                if t3 == 1:
                                    t.wait_ge(p.S["act"], base2["act"] + sg + 1)
                                p.inc(t.matmul(ps_n[0:1, sg * 512:(sg + 1) * 512],
                                               p.sel[0:64, 0:1], sq2[0][:],
                                               start=True, stop=True), "pe")
                        @blk.scalar
                        def _(a):
                            for i in range(8):
                                t3, sg = i // 4, i % 4
                                a.wait_ge(p.S["pe"], base2["pe"] + i + 1)
                                if t3 == 0:
                                    p.inc(a.activation(qnrow[0:1, sg * 512:(sg + 1) * 512],
                                                       ps_n[0:1, sg * 512:(sg + 1) * 512],
                                                       AF.Identity, scale=0.25), "act")
                                else:
                                    a.wait_ge(p.S["dve"], base2["dve"] + 1)
                                    p.inc(a.activation(Kaug[64:65, sg * 512:(sg + 1) * 512],
                                                       ps_n[0:1, sg * 512:(sg + 1) * 512],
                                                       AF.Identity, scale=1.0), "act")
                    with p.block() as blk:
                        @blk.sync
                        def _(s):
                            s.wait_ge(p.S["act"], p.C["act"])
                            s.wait_ge(p.S["dve"], p.C["dve"])
                            p.inc(s.dma_start(out=Qaug[65:66, :], in_=qnrow[:]), qnq, 16)
                        @blk.vector
                        def _(v):
                            p.fence(v, "dve")
                            v.wait_ge(p.S["act"], p.C["act"])
                            p.inc(v.tensor_scalar(arow[:], Kaug[64:65, :], -1.0, 1.0, ALU.mult, ALU.add), "dve")
                    p.barrier()
                with ExitStack() as es2:
                    ps_s = [p.psum(es2, [128, 1024]) for _ in range(1)]
                    ps_d = [p.psum(es2, [128, 1024]) for _ in range(1)]
                    ps_o = p.psum(es2, [128, 512])
                    ps_z = p.psum(es2, [64, 512])
                    for qs in range(4):
                        with p.block() as blk:
                            @blk.vector
                            def _(v):
                                p.fence(v, "dve")
                                p.inc(v.tensor_scalar(brow[:], qnrow[0:1, qs * 512:(qs + 1) * 512], -1.0, 1.0, ALU.mult, ALU.add), "dve")
                        base3 = dict(p.C)
                        with p.block() as blk:
                            @blk.tensor
                            def _(t):
                                for kp in range(8):
                                    if kp >= 1:
                                        t.wait_ge(p.S["act"], base3["act"] + 2 * kp - 1)
                                        t.wait_ge(p.S["dve"], base3["dve"] + (kp - 1) * 6 + 1)
                                    for half in range(2):
                                        kt = kp * 2 + half
                                        t.matmul(ps_s[0][:, half * 512:(half + 1) * 512],
                                                 Kaug[:, kt * 128:(kt + 1) * 128],
                                                 Qaug[:, qs * 512:(qs + 1) * 512],
                                                 start=True, stop=True)
                                        p.inc(t.matmul(ps_d[0][:, half * 512:(half + 1) * 512],
                                                       arow[:, kt * 128:(kt + 1) * 128],
                                                       brow[:, 0:512],
                                                       start=True, stop=True), "pe")
                                    t.wait_ge(p.S["dve"], base3["dve"] + (kp + 1) * 6)
                                    for half in range(2):
                                        kt = kp * 2 + half
                                        p.inc(t.matmul(ps_o[0:65, :], Vkm[:, kt * 65:(kt + 1) * 65],
                                                       Ptile[:, half * 512:(half + 1) * 512],
                                                       start=(kt == 0), stop=(kt == 15)), "pe")
                            @blk.scalar
                            def _(a):
                                for kp in range(8):
                                    a.wait_ge(p.S["pe"], base3["pe"] + kp * 4 + 2)
                                    if kp >= 1:
                                        a.wait_ge(p.S["dve"], base3["dve"] + (kp - 1) * 6 + 5)
                                    p.inc(a.activation(c0[:], ps_s[0][:], AF.Relu, scale=1.0), "act")
                                    a.wait_ge(p.S["dve"], base3["dve"] + kp * 6 + 4)
                                    p.inc(a.activation(c0[:], c2[:], AF.Sqrt, scale=4.0), "act")
                            @blk.vector
                            def _(v):
                                for kp in range(8):
                                    v.wait_ge(p.S["act"], base3["act"] + kp * 2 + 1)
                                    p.inc(v.tensor_scalar(c1[:], ps_d[0][:], 1e-6, None, ALU.max), "dve")
                                    p.fence(v, "dve")
                                    p.inc(v.reciprocal(c2[:], c1[:]), "dve")
                                    p.fence(v, "dve")
                                    p.inc(v.tensor_tensor(c1[:], c0[:], c2[:], ALU.mult), "dve")
                                    p.fence(v, "dve")
                                    p.inc(v.scalar_tensor_tensor(c2[:], c1[:], 1.0, c1[:], ALU.add, ALU.mult), "dve")
                                    v.wait_ge(p.S["act"], base3["act"] + kp * 2 + 2)
                                    p.inc(v.scalar_tensor_tensor(c2[:], c1[:], 2.0, c0[:], ALU.mult, ALU.subtract), "dve")
                                    p.fence(v, "dve")
                                    if kp >= 1:
                                        v.wait_ge(p.S["pe"], base3["pe"] + kp * 4)
                                    p.inc(v.tensor_scalar(Ptile[:], c2[:], 1.0, None, ALU.add), "dve")
                        base4 = dict(p.C)
                        with p.block() as blk:
                            @blk.vector
                            def _(v):
                                v.wait_ge(p.S["pe"], base4["pe"])
                                p.inc(v.reciprocal(cbuf[0:1, 2048:2560], ps_o[64:65, :]), "dve")
                            @blk.tensor
                            def _(t):
                                t.wait_ge(p.S["dve"], p.C["dve"])
                                p.inc(t.matmul(ps_z[:], p.rowsel[0:1, 0:64], cbuf[0:1, 2048:2560], start=True, stop=True), "pe")
                            @blk.scalar
                            def _(a):
                                a.wait_ge(p.S["pe"], base4["pe"])
                                p.inc(a.activation(oT[:], ps_o[0:64, :], AF.Identity, scale=1.0), "act")
                        with p.block() as blk:
                            @blk.vector
                            def _(v):
                                v.wait_ge(p.S["pe"], p.C["pe"])
                                v.wait_ge(p.S["act"], p.C["act"])
                                p.inc(v.tensor_tensor(AOst[:], oT[:], ps_z[:], ALU.mult), "dve")
                            @blk.gpsimd
                            def _(g):
                                g.wait_ge(p.S["dve"], p.C["dve"])
                                dst = BassAP(ao_dram.tensor,
                                             (h // 2) * 128 * 2048 + (h % 2) * 64 * 2048 + qs * 512,
                                             [[2048, 64], [1, 512]])
                                p.inc(g.dma_start(out=dst, in_=AOst[:]), aoq, 16)
                        p.barrier(("tensor", "vector", "scalar", "sync", "gpsimd"))
            for sg in range(2048 // SEG2):
                with p.block() as blk:
                    @blk.sync
                    def _(s):
                        p.inc(s.dma_start(out=AOseg[:].rearrange("p (c t) -> p c t", c=FCH),
                                          in_=ao_dram[:, :, sg * SEG2:(sg + 1) * SEG2].rearrange("c p t -> p c t")),
                              "dma", 16)
                    @blk.vector
                    def _(v):
                        v.wait_ge(p.S["dma"], p.C["dma"])
                p.barrier()

                def epi(a, oc, j, psv):
                    return a.activation(Yseg[:, oc * SEG2:(oc + 1) * SEG2], psv,
                                        AF.Identity, scale=1.0)
                emit_gemm(p, lambda oc, hh: wo_ap[oc, :, hh * 512:(hh + 1) * 512], FCH, FCH,
                          lambda kc, j: AOseg[:, kc * SEG2:(kc + 1) * SEG2], epi, kcw=4, nbuf=1)
                emit_ln(p, Yseg, acc_into=p.X, ntok=SEG2, src_stride=SEG2,
                        dst_stride=TOK, dst_tok0=seq * 2048 + sg * SEG2,
                        scratch=cbuf[:, 2048:4096])


def prep_l0attn(ins):
    def headtiles_T(W):
        Wt = np.ascontiguousarray(np.asarray(W).astype(np.float32).T)
        return np.ascontiguousarray(Wt.reshape(8, 128, 16, 64).transpose(2, 1, 0, 3).reshape(16, 128, 512))
    wq = headtiles_T(ins["L0_attn_wq"])
    wk = headtiles_T(ins["L0_attn_wk"])
    wv = headtiles_T(ins["L0_attn_wv"])
    wqkv = np.ascontiguousarray(np.stack([wq, wk, wv], axis=1))
    wo_t = tile_lhsT(np.ascontiguousarray(np.asarray(ins["L0_attn_wo"]).astype(np.float32).T))
    return {"L0a_wqkv": wqkv, "L0a_wo": np.ascontiguousarray(wo_t)}


def decl_l0attn(nc):
    return dict(
        wqkv_ap=nc.dram_tensor("L0a_wqkv", [16, 3, 128, 512], F32, kind="ExternalInput").ap(),
        wo_ap=nc.dram_tensor("L0a_wo", [8, 128, 1024], F32, kind="ExternalInput").ap(),
    )


def emit_seq(p: Prog, ops):
    nc = p.nc
    plan = []
    cnt = {"dve": p.C["dve"], "act": p.C["act"], "pool": p.C["pool"]}
    base = dict(cnt)
    for eng, fn in ops:
        plan.append((eng, fn, dict(cnt)))
        cnt[ENG2SEM[eng]] += 1
    secs = {"vector": [], "scalar": [], "gpsimd": []}
    for eng, fn, pre in plan:
        secs[eng].append((eng, fn, pre))
    with p.block() as blk:
        for engname in ("vector", "scalar", "gpsimd"):
            if not secs[engname]:
                continue
            def mk(engname):
                def _(q):
                    own = ENG2SEM[engname]
                    seen = {"dve": -1, "act": -1, "pool": -1}
                    for eng, fn, pre in secs[engname]:
                        for s2 in ("dve", "act", "pool"):
                            want = pre[s2]
                            if s2 == own:
                                if want > base[own] and want > seen[s2]:
                                    q.wait_ge(p.S[s2], want)
                                    seen[s2] = want
                            else:
                                if want > 0 and want > seen[s2]:
                                    q.wait_ge(p.S[s2], want)
                                    seen[s2] = want
                        p.inc(fn(q), own)
                        if ENG2SEM[engname] == own:
                            pass
                return _
            getattr(blk, engname)(mk(engname))



PI = float(np.pi)

def emit_l2attn(p: Prog, wmag_ap, wq_ap, bmq_ap, ramp_ap, phases_ap, lng_ap,
                w1_ap, b1_ap, w2_ap, b2_ap):
    nc = p.nc
    SEG = 256
    NSG = 2048 // SEG
    with ExitStack() as es:
        ramp = p.sbuf(es, [1, 256], F32)
        phs = p.sbuf(es, [128, 8], F32)
        bmq = p.sbuf(es, [128, 16], F32)
        lng = p.sbuf(es, [128, 64], F32)
        b1s = p.sbuf(es, [128, 16], F32)
        b2s = p.sbuf(es, [128, 8], F32)
        Hseg = p.sbuf(es, [128, FCH * SEG], BF16)
        mag0 = p.sbuf(es, [128, FCH * SEG], BF16)
        qoff = p.sbuf(es, [128, FCH * SEG], F32)
        CTX = p.sbuf(es, [128, 32 * SEG], BF16)
        G = p.sbuf(es, [128, 16 * SEG], BF16)
        Yseg = p.sbuf(es, [128, FCH * SEG], F32)
        carry = p.sbuf(es, [128, 24], F32)
        cbuf = p.sbuf(es, [128, 4096], F32)

        def cv(i):
            return cbuf[:, i * SEG:(i + 1) * SEG]
        t_phi, t_n, t_m, t_sin, t_cos, t_w0, t_a, t_b, t_mr, t_mi, t_sr, t_si = [cv(i) for i in range(12)]
        with p.block() as blk:
            @blk.sync
            def _(s):
                for ap_, buf in ((phases_ap, phs), (bmq_ap, bmq),
                                 (lng_ap, lng), (b1_ap, b1s), (b2_ap, b2s)):
                    p.inc(s.dma_start(out=buf[:], in_=ap_), "dma", 16)
            @blk.vector
            def _(v):
                v.wait_ge(p.S["dma"], p.C["dma"])
        p.barrier()
        for seq in range(SEQ):
            with p.block() as blk:
                @blk.vector
                def _(v):
                    p.fence(v, "dve")
                    p.inc(v.memset(carry[:], 0.0), "dve")
            p.barrier(("vector",))
            for sg in range(NSG):
                tok0 = seq * 2048 + sg * SEG
                emit_ln(p, p.X, dst=Hseg, ntok=SEG, src_tok0=tok0,
                        dst_stride=SEG, dst_tok0=0, scratch=cbuf)

                def epi_mag(a, oc, j, psv):
                    return a.activation(mag0[:, oc * SEG:(oc + 1) * SEG], psv,
                                        AF.Sigmoid, bias=bmq[:, oc:oc + 1], scale=1.0)
                emit_gemm(p, lambda oc, hh: wmag_ap[oc, :, :], FCH, FCH,
                          lambda kc, j: Hseg[:, kc * SEG:(kc + 1) * SEG], epi_mag, kcw=8, nbuf=1)

                def epi_q(a, oc, j, psv):
                    return a.activation(qoff[:, oc * SEG:(oc + 1) * SEG], psv,
                                        AF.Identity, bias=bmq[:, 8 + oc:9 + oc], scale=1.0)
                emit_gemm(p, lambda oc, hh: wq_ap[oc, :, :], FCH, FCH,
                          lambda kc, j: Hseg[:, kc * SEG:(kc + 1) * SEG], epi_q, kcw=8, nbuf=1)

                with ExitStack() as es2:
                    ps_r = p.psum(es2, [128, 512])
                    with p.block() as blk:
                        @blk.sync
                        def _(s):
                            s.wait_ge(p.S["pe"], p.C["pe"])
                            p.inc(s.dma_start(out=ramp[:], in_=ramp_ap[0:1, sg * SEG:(sg + 1) * SEG]), "dma", 16)
                        @blk.tensor
                        def _(t):
                            t.wait_ge(p.S["dve"], p.C["dve"])
                            t.wait_ge(p.S["act"], p.C["act"])
                            t.wait_ge(p.S["dma"], p.C["dma"])
                            p.inc(t.matmul(ps_r[:, 0:SEG], p.rowsel[0:1, 0:128],
                                           ramp[0:1, 0:SEG],
                                           start=True, stop=True), "pe")
                    p.barrier(("tensor", "vector", "scalar", "sync"))
                    allops = []
                    for c in range(FCH):
                        H_ = Hseg[:, c * SEG:(c + 1) * SEG]
                        M_ = mag0[:, c * SEG:(c + 1) * SEG]
                        Q_ = qoff[:, c * SEG:(c + 1) * SEG]
                        ops = []
                        ops.append(("vector", lambda v, c=c: v.tensor_scalar(t_phi, ps_r[:, 0:SEG], phs[:, c:c + 1], 0.01, ALU.mult, ALU.mult)))
                        ops.append(("vector", lambda v: v.tensor_scalar(t_n, t_phi, 1.0 / (2 * PI), MAGIC, ALU.mult, ALU.add)))
                        ops.append(("vector", lambda v: v.tensor_scalar(t_n, t_n, MAGIC, None, ALU.subtract)))
                        ops.append(("vector", lambda v: v.scalar_tensor_tensor(t_m, t_n, -2 * PI, t_phi, ALU.mult, ALU.add)))
                        ops.append(("vector", lambda v: v.tensor_scalar(t_m, t_m, -PI, PI, ALU.max, ALU.min)))
                        ops.append(("scalar", lambda a: a.activation(t_sin, t_m, AF.Sin, scale=1.0)))
                        ops.append(("scalar", lambda a: a.activation(t_n, t_m, AF.Abs, scale=1.0)))
                        ops.append(("scalar", lambda a: a.activation(t_cos, t_n, AF.Sin, bias=p.aux[:, AUX_HALFPI:AUX_HALFPI + 1], scale=-1.0)))
                        ops.append(("vector", lambda v, H_=H_, M_=M_: v.scalar_tensor_tensor(t_w0, M_, 5.0, H_, ALU.mult, ALU.mult)))
                        ops.append(("vector", lambda v: v.tensor_tensor(t_a, t_w0, t_cos, ALU.mult)))
                        ops.append(("vector", lambda v: v.tensor_tensor(t_b, t_w0, t_sin, ALU.mult)))
                        ops.append(("vector", lambda v, c=c: v.tensor_tensor_scan(t_sr, t_a, t_a, carry[:, c:c + 1], ALU.add, ALU.bypass)))
                        ops.append(("vector", lambda v, c=c: v.tensor_tensor_scan(t_si, t_b, t_b, carry[:, 8 + c:9 + c], ALU.add, ALU.bypass)))
                        ops.append(("vector", lambda v, c=c, M_=M_: v.tensor_tensor_scan(t_a, M_, M_, carry[:, 16 + c:17 + c], ALU.add, ALU.bypass)))
                        ops.append(("vector", lambda v, c=c: v.tensor_copy(carry[:, c:c + 1], t_sr[:, SEG - 1:SEG])))
                        ops.append(("vector", lambda v, c=c: v.tensor_copy(carry[:, 8 + c:9 + c], t_si[:, SEG - 1:SEG])))
                        ops.append(("vector", lambda v, c=c: v.tensor_copy(carry[:, 16 + c:17 + c], t_a[:, SEG - 1:SEG])))
                        ops.append(("scalar", lambda a: a.activation(t_b, t_a, AF.Sqrt, bias=p.aux[:, AUX_WKVEPS:AUX_WKVEPS + 1], scale=5.0)))
                        ops.append(("vector", lambda v: v.reciprocal(t_b, t_b)))
                        ops.append(("vector", lambda v: v.tensor_tensor(t_mr, t_sr, t_b, ALU.mult)))
                        ops.append(("vector", lambda v: v.tensor_tensor(t_mi, t_si, t_b, ALU.mult)))
                        ops.append(("vector", lambda v, c=c, H_=H_: v.tensor_tensor(CTX[:, c * SEG:(c + 1) * SEG], H_, t_cos, ALU.mult)))
                        ops.append(("vector", lambda v, c=c, H_=H_: v.tensor_tensor(CTX[:, (8 + c) * SEG:(9 + c) * SEG], H_, t_sin, ALU.mult)))
                        ops.append(("vector", lambda v, Q_=Q_: v.tensor_tensor(t_phi, t_phi, Q_, ALU.add)))
                        ops.append(("vector", lambda v: v.tensor_scalar(t_n, t_phi, 1.0 / (2 * PI), MAGIC, ALU.mult, ALU.add)))
                        ops.append(("vector", lambda v: v.tensor_scalar(t_n, t_n, MAGIC, None, ALU.subtract)))
                        ops.append(("vector", lambda v: v.scalar_tensor_tensor(t_m, t_n, -2 * PI, t_phi, ALU.mult, ALU.add)))
                        ops.append(("vector", lambda v: v.tensor_scalar(t_m, t_m, -PI, PI, ALU.max, ALU.min)))
                        ops.append(("scalar", lambda a: a.activation(t_sin, t_m, AF.Sin, scale=1.0)))
                        ops.append(("scalar", lambda a: a.activation(t_n, t_m, AF.Abs, scale=1.0)))
                        ops.append(("scalar", lambda a: a.activation(t_cos, t_n, AF.Sin, bias=p.aux[:, AUX_HALFPI:AUX_HALFPI + 1], scale=-1.0)))
                        ops.append(("vector", lambda v: v.tensor_tensor(t_a, t_mr, t_cos, ALU.mult)))
                        ops.append(("vector", lambda v: v.tensor_tensor(t_b, t_mi, t_sin, ALU.mult)))
                        ops.append(("vector", lambda v, c=c: v.tensor_tensor(CTX[:, (16 + c) * SEG:(17 + c) * SEG], t_a, t_b, ALU.add)))
                        ops.append(("vector", lambda v: v.tensor_tensor(t_a, t_mi, t_cos, ALU.mult)))
                        ops.append(("vector", lambda v: v.tensor_tensor(t_b, t_mr, t_sin, ALU.mult)))
                        ops.append(("vector", lambda v, c=c: v.tensor_tensor(CTX[:, (24 + c) * SEG:(25 + c) * SEG], t_a, t_b, ALU.subtract)))
                        allops.extend(ops)
                    emit_seq(p, allops)
                    p.barrier(("vector", "scalar", "tensor"))
                emit_ln(p, CTX, dst=CTX, ntok=SEG, nch=32, src_stride=SEG,
                        dst_stride=SEG, eps=1e-5, gamma=lng, beta=lng,
                        gb_col0=0, beta_col0=32, scratch=cbuf)

                def epi_w1(a, oc, j, psv):
                    return a.activation(G[:, oc * SEG:(oc + 1) * SEG], psv,
                                        AF.Gelu, bias=b1s[:, oc:oc + 1], scale=1.0)
                emit_gemm(p, lambda oc, hh: w1_ap[oc, :, hh * 1024:(hh + 1) * 1024], 16, 32,
                          lambda kc, j: CTX[:, kc * SEG:(kc + 1) * SEG], epi_w1, kcw=8, nbuf=1)

                def epi_w2(v, oc, j, psv):
                    return v.scalar_tensor_tensor(Yseg[:, oc * SEG:(oc + 1) * SEG], psv,
                                                  b2s[:, oc:oc + 1],
                                                  Hseg[:, oc * SEG:(oc + 1) * SEG],
                                                  ALU.add, ALU.add)
                emit_gemm(p, lambda oc, hh: w2_ap[oc, :, hh * 1024:(hh + 1) * 1024], FCH, 16,
                          lambda kc, j: G[:, kc * SEG:(kc + 1) * SEG], epi_w2,
                          epi_eng="vector", cast_eng="scalar", kcw=8, nbuf=1)

                emit_ln(p, Yseg, acc_into=p.X, ntok=SEG, src_stride=SEG,
                        dst_stride=TOK, dst_tok0=tok0, scratch=cbuf)


def prep_l2attn(ins):
    wm = tile_lhsT(np.ascontiguousarray(np.asarray(ins["L2_attn_W_mag"]).astype(np.float32).T))
    wq = tile_lhsT(np.ascontiguousarray(np.asarray(ins["L2_attn_W_q"]).astype(np.float32).T))
    bmq = np.concatenate([bias_tiles(ins["L2_attn_b_mag"]), bias_tiles(ins["L2_attn_b_q"])], axis=1)
    ramp = np.arange(2048, dtype=np.float32).reshape(1, 2048)
    phs = bias_tiles(ins["L2_attn_random_phases"])
    lng = np.concatenate([bias_tiles(ins["L2_attn_ln_g"]), bias_tiles(ins["L2_attn_ln_b"])], axis=1)
    w1 = tile_lhsT(np.ascontiguousarray(np.asarray(ins["L2_attn_W1"]).astype(np.float32).T))
    w2 = tile_lhsT(np.ascontiguousarray(np.asarray(ins["L2_attn_W2"]).astype(np.float32).T))
    return {"L2a_wmag": np.ascontiguousarray(wm), "L2a_wq": np.ascontiguousarray(wq),
            "L2a_bmq": np.ascontiguousarray(bmq), "L2a_ramp": ramp,
            "L2a_phs": np.ascontiguousarray(phs), "L2a_lng": np.ascontiguousarray(lng),
            "L2a_w1": np.ascontiguousarray(w1), "L2a_b1": bias_tiles(ins["L2_attn_b1"]),
            "L2a_w2": np.ascontiguousarray(w2), "L2a_b2": bias_tiles(ins["L2_attn_b2"])}


def decl_l2attn(nc):
    return dict(
        wmag_ap=nc.dram_tensor("L2a_wmag", [8, 128, 1024], F32, kind="ExternalInput").ap(),
        wq_ap=nc.dram_tensor("L2a_wq", [8, 128, 1024], F32, kind="ExternalInput").ap(),
        bmq_ap=nc.dram_tensor("L2a_bmq", [128, 16], F32, kind="ExternalInput").ap(),
        ramp_ap=nc.dram_tensor("L2a_ramp", [1, 2048], F32, kind="ExternalInput").ap(),
        phases_ap=nc.dram_tensor("L2a_phs", [128, 8], F32, kind="ExternalInput").ap(),
        lng_ap=nc.dram_tensor("L2a_lng", [128, 64], F32, kind="ExternalInput").ap(),
        w1_ap=nc.dram_tensor("L2a_w1", [16, 128, 4096], F32, kind="ExternalInput").ap(),
        b1_ap=nc.dram_tensor("L2a_b1", [128, 16], F32, kind="ExternalInput").ap(),
        w2_ap=nc.dram_tensor("L2a_w2", [8, 128, 2048], F32, kind="ExternalInput").ap(),
        b2_ap=nc.dram_tensor("L2a_b2", [128, 8], F32, kind="ExternalInput").ap(),
    )


def emit_l1attn(p: Prog, wkt_ap, wvt_ap, wrt_ap, wkc_ap, wvc_ap, wrc_ap,
                lnab_ap, mix_ap, u_ap):
    nc = p.nc
    SEG = 256
    NSG = 2048 // SEG
    EPSW = 1e-8
    with ExitStack() as es:
        lnab = p.sbuf(es, [128, 32], F32)
        mixs = p.sbuf(es, [128, 40], F32)
        imix = p.sbuf(es, [128, 40], F32)
        ucol = p.sbuf(es, [128, 8], F32)
        XA = p.sbuf(es, [128, FCH * SEG], F32)
        Hseg = p.sbuf(es, [128, FCH * SEG], BF16)
        HS = p.sbuf(es, [128, FCH * SEG], BF16)
        TSB = p.sbuf(es, [128, FCH * SEG], BF16)
        Kf = p.sbuf(es, [128, FCH * SEG], F32)
        Vf = p.sbuf(es, [128, FCH * SEG], BF16)
        KC = p.sbuf(es, [128, 32 * SEG], BF16)
        RC = p.sbuf(es, [128, FCH * SEG], BF16)
        SSc = p.sbuf(es, [128, SEG + 8], F32)
        SSs = p.sbuf(es, [128, SEG + 8], F32)
        SSd = p.sbuf(es, [128, SEG + 8], F32)
        hprev = p.sbuf(es, [128, 32], F32)
        carry = p.sbuf(es, [128, 32], F32)
        cbuf = p.sbuf(es, [128, 2048], F32)
        Rsig = cbuf[:, 1024:2048].bitcast(BF16)
        VV = TSB

        def cv(i):
            return cbuf[:, i * SEG:(i + 1) * SEG]
        t0, t1, t2, t3 = [cv(i) for i in range(4)]

        with p.block() as blk:
            @blk.sync
            def _(s):
                for ap_, buf in ((lnab_ap, lnab), (mix_ap, mixs), (u_ap, ucol)):
                    p.inc(s.dma_start(out=buf[:], in_=ap_), "dma", 16)
            @blk.vector
            def _(v):
                v.wait_ge(p.S["dma"], p.C["dma"])
                p.inc(v.tensor_scalar(imix[:], mixs[:], -1.0, 1.0, ALU.mult, ALU.add), "dve")
        p.barrier()

        for seq in range(SEQ):
            with p.block() as blk:
                @blk.vector
                def _(v):
                    p.fence(v, "dve")
                    p.inc(v.memset(carry[:], 0.0), "dve")
                    p.inc(v.memset(hprev[:], 0.0), "dve")
            p.barrier(("vector",))
            for sg in range(NSG):
                tok0 = seq * 2048 + sg * SEG
                par, npar = (sg % 2) * 16, ((sg + 1) % 2) * 16
                emit_ln(p, p.X, dst=XA, ntok=SEG, src_tok0=tok0,
                        dst_stride=SEG, dst_tok0=0, scratch=cbuf)
                emit_ln(p, XA, dst=Hseg, ntok=SEG, src_stride=SEG, dst_stride=SEG,
                        gamma=lnab, beta=lnab, gb_col0=0, beta_col0=8, scratch=cbuf)
                ops = []
                for c in range(FCH):
                    ops.append(("vector", lambda v, c=c: v.tensor_copy(HS[:, c * SEG + 1:(c + 1) * SEG], Hseg[:, c * SEG:(c + 1) * SEG - 1])))
                    ops.append(("vector", lambda v, c=c: v.tensor_copy(HS[:, c * SEG:c * SEG + 1], hprev[:, par + c:par + c + 1])))
                    ops.append(("vector", lambda v, c=c: v.tensor_copy(hprev[:, npar + c:npar + c + 1], Hseg[:, (c + 1) * SEG - 1:(c + 1) * SEG])))
                emit_seq(p, ops)
                for i3, (w_ap, mcol) in enumerate(((wkt_ap, 0), (wvt_ap, 8), (wrt_ap, 16))):
                    ops = []
                    for c in range(FCH):
                        ops.append(("vector", lambda v, c=c, mcol=mcol: v.tensor_scalar(t0, Hseg[:, c * SEG:(c + 1) * SEG], mixs[:, mcol + c:mcol + c + 1], None, ALU.mult)))
                        ops.append(("vector", lambda v, c=c, mcol=mcol: v.scalar_tensor_tensor(TSB[:, c * SEG:(c + 1) * SEG], HS[:, c * SEG:(c + 1) * SEG], imix[:, mcol + c:mcol + c + 1], t0, ALU.mult, ALU.add)))
                    emit_seq(p, ops)
                    p.barrier(("vector", "tensor", "scalar"))
                    if i3 == 0:
                        def epi(a, oc, j, psv):
                            return a.activation(Kf[:, oc * SEG:(oc + 1) * SEG], psv, AF.Identity, scale=1.0)
                    elif i3 == 1:
                        def epi(a, oc, j, psv):
                            return a.activation(Vf[:, oc * SEG:(oc + 1) * SEG], psv, AF.Identity, scale=1.0)
                    else:
                        def epi(a, oc, j, psv):
                            return a.activation(Rsig[:, oc * SEG:(oc + 1) * SEG], psv, AF.Sigmoid, scale=1.0)
                    emit_gemm(p, lambda oc, hh, w_ap=w_ap: w_ap[oc, :, :], FCH, FCH,
                              lambda kc, j: TSB[:, kc * SEG:(kc + 1) * SEG], epi, kcw=8, nbuf=1)
                allops = []
                for c in range(FCH):
                    K_ = Kf[:, c * SEG:(c + 1) * SEG]
                    V_ = Vf[:, c * SEG:(c + 1) * SEG]
                    Cex = SSc[:, 0:SEG]
                    Cin = SSc[:, 1:SEG + 1]
                    Dex = SSd[:, 0:SEG]
                    Din = SSd[:, 1:SEG + 1]
                    Rex = SSs[:, 0:SEG]
                    ops = []
                    ops.append(("scalar", lambda a, K_=K_: a.activation(t0, K_, AF.Exp, scale=1.0)))
                    ops.append(("vector", lambda v, V_=V_: v.tensor_tensor(t1, t0, V_, ALU.mult)))
                    ops.append(("vector", lambda v, c=c: v.tensor_copy(SSc[:, 0:1], carry[:, c:c + 1])))
                    ops.append(("vector", lambda v, c=c, Cin=Cin: v.tensor_tensor_scan(Cin, t0, t0, carry[:, c:c + 1], ALU.add, ALU.bypass)))
                    ops.append(("vector", lambda v, c=c: v.tensor_copy(carry[:, c:c + 1], SSc[:, SEG:SEG + 1])))
                    ops.append(("vector", lambda v, Cex=Cex: v.tensor_tensor(t2, Cex, t0, ALU.max)))
                    ops.append(("vector", lambda v, c=c, Din=Din: v.tensor_tensor_scan(Din, t2, t2, carry[:, 24 + c:25 + c], ALU.add, ALU.bypass)))
                    ops.append(("vector", lambda v, c=c: v.tensor_copy(carry[:, 24 + c:25 + c], SSd[:, SEG:SEG + 1])))
                    ops.append(("vector", lambda v, c=c: v.tensor_copy(SSd[:, 0:1], carry[:, 16 + c:17 + c])))
                    ops.append(("vector", lambda v, Dex=Dex, Cex=Cex: v.scalar_tensor_tensor(t3, Dex, EPSW, Cex, ALU.mult, ALU.add)))
                    ops.append(("vector", lambda v: v.tensor_tensor(t2, t3, t0, ALU.max)))
                    ops.append(("vector", lambda v, c=c, Din=Din: v.tensor_tensor_scan(Din, t2, t2, carry[:, 16 + c:17 + c], ALU.add, ALU.bypass)))
                    ops.append(("vector", lambda v, c=c: v.tensor_copy(carry[:, 16 + c:17 + c], SSd[:, SEG:SEG + 1])))
                    ops.append(("vector", lambda v, Din=Din, Cin=Cin: v.scalar_tensor_tensor(t3, Din, EPSW, Cin, ALU.mult, ALU.add)))
                    ops.append(("vector", lambda v: v.reciprocal(t2, t2)))
                    ops.append(("vector", lambda v: v.tensor_tensor(t3, t3, t2, ALU.mult)))
                    ops.append(("vector", lambda v, c=c: v.tensor_copy(SSs[:, 0:1], carry[:, 8 + c:9 + c])))
                    ops.append(("vector", lambda v, c=c: v.tensor_tensor_scan(SSs[:, 1:SEG + 1], t1, t3, carry[:, 8 + c:9 + c], ALU.add, ALU.mult)))
                    ops.append(("vector", lambda v, c=c: v.tensor_copy(carry[:, 8 + c:9 + c], SSs[:, SEG:SEG + 1])))
                    ops.append(("scalar", lambda a, K_=K_, c=c: a.activation(t0, K_, AF.Exp, bias=ucol[:, c:c + 1], scale=1.0)))
                    ops.append(("vector", lambda v, Dex=Dex, Cex=Cex: v.scalar_tensor_tensor(t2, Dex, EPSW, Cex, ALU.mult, ALU.add)))
                    ops.append(("vector", lambda v: v.tensor_tensor(t1, t2, t0, ALU.max)))
                    ops.append(("vector", lambda v: v.scalar_tensor_tensor(t1, t1, EPSW, t0, ALU.mult, ALU.add)))
                    ops.append(("vector", lambda v: v.tensor_tensor(t1, t1, t2, ALU.add)))
                    ops.append(("vector", lambda v, V_=V_: v.tensor_tensor(t0, t0, V_, ALU.mult)))
                    ops.append(("vector", lambda v, Rex=Rex: v.tensor_tensor(t0, t0, Rex, ALU.add)))
                    ops.append(("vector", lambda v: v.reciprocal(t1, t1)))
                    ops.append(("vector", lambda v: v.tensor_tensor(t0, t0, t1, ALU.mult)))
                    ops.append(("vector", lambda v, c=c: v.tensor_tensor(t0, t0, Rsig[:, c * SEG:(c + 1) * SEG], ALU.mult)))
                    ops.append(("vector", lambda v, c=c: v.tensor_tensor(XA[:, c * SEG:(c + 1) * SEG], XA[:, c * SEG:(c + 1) * SEG], t0, ALU.add)))
                    allops.extend(ops)
                emit_seq(p, allops)
                emit_ln(p, XA, dst=Hseg, ntok=SEG, src_stride=SEG, dst_stride=SEG,
                        gamma=lnab, beta=lnab, gb_col0=16, beta_col0=24, scratch=cbuf)
                ops = []
                for c in range(FCH):
                    ops.append(("vector", lambda v, c=c: v.tensor_copy(HS[:, c * SEG + 1:(c + 1) * SEG], Hseg[:, c * SEG:(c + 1) * SEG - 1])))
                    ops.append(("vector", lambda v, c=c: v.tensor_copy(HS[:, c * SEG:c * SEG + 1], hprev[:, par + 8 + c:par + 9 + c])))
                    ops.append(("vector", lambda v, c=c: v.tensor_copy(hprev[:, npar + 8 + c:npar + 9 + c], Hseg[:, (c + 1) * SEG - 1:(c + 1) * SEG])))
                emit_seq(p, ops)
                for i3, (w_ap, mcol, och) in enumerate(((wkc_ap, 24, 32), (wrc_ap, 32, 8))):
                    ops = []
                    for c in range(FCH):
                        ops.append(("vector", lambda v, c=c, mcol=mcol: v.tensor_scalar(t0, Hseg[:, c * SEG:(c + 1) * SEG], mixs[:, mcol + c:mcol + c + 1], None, ALU.mult)))
                        ops.append(("vector", lambda v, c=c, mcol=mcol: v.scalar_tensor_tensor(TSB[:, c * SEG:(c + 1) * SEG], HS[:, c * SEG:(c + 1) * SEG], imix[:, mcol + c:mcol + c + 1], t0, ALU.mult, ALU.add)))
                    emit_seq(p, ops)
                    p.barrier(("vector", "tensor", "scalar"))
                    if i3 == 0:
                        def epi(v, oc, j, psv):
                            return v.tensor_scalar(KC[:, oc * SEG:(oc + 1) * SEG], psv, 0.0, None, ALU.max)
                        emit_gemm(p, lambda oc, hh, w_ap=w_ap: w_ap[oc, :, :], och, FCH,
                                  lambda kc, j: TSB[:, kc * SEG:(kc + 1) * SEG], epi,
                                  epi_eng="vector", cast_eng="scalar", kcw=8, nbuf=1)
                    else:
                        def epi(a, oc, j, psv):
                            return a.activation(RC[:, oc * SEG:(oc + 1) * SEG], psv, AF.Sigmoid, scale=1.0)
                        emit_gemm(p, lambda oc, hh, w_ap=w_ap: w_ap[oc, :, :], och, FCH,
                                  lambda kc, j: TSB[:, kc * SEG:(kc + 1) * SEG], epi, kcw=8, nbuf=1)
                ops = []
                for c in range(32):
                    ops.append(("scalar", lambda a, c=c: a.activation(KC[:, c * SEG:(c + 1) * SEG], KC[:, c * SEG:(c + 1) * SEG], AF.Square, scale=1.0)))
                emit_seq(p, ops)
                p.barrier(("scalar", "tensor", "vector"))

                def epi_vv(a, oc, j, psv):
                    return a.activation(VV[:, oc * SEG:(oc + 1) * SEG], psv, AF.Identity, scale=1.0)
                emit_gemm(p, lambda oc, hh: wvc_ap[oc, :, hh * 1024:(hh + 1) * 1024], FCH, 32,
                          lambda kc, j: KC[:, kc * SEG:(kc + 1) * SEG], epi_vv, kcw=8, nbuf=1)
                ops = []
                for c in range(FCH):
                    ops.append(("vector", lambda v, c=c: v.tensor_tensor(t0, VV[:, c * SEG:(c + 1) * SEG], RC[:, c * SEG:(c + 1) * SEG], ALU.mult)))
                    ops.append(("vector", lambda v, c=c: v.tensor_tensor(XA[:, c * SEG:(c + 1) * SEG], XA[:, c * SEG:(c + 1) * SEG], t0, ALU.add)))
                emit_seq(p, ops)
                emit_ln(p, XA, acc_into=p.X, ntok=SEG, src_stride=SEG,
                        dst_stride=TOK, dst_tok0=tok0, scratch=cbuf)


def prep_l1attn(ins):
    def tl(W):
        return np.ascontiguousarray(tile_lhsT(np.ascontiguousarray(np.asarray(W).astype(np.float32).T)))
    lnab = np.concatenate([bias_tiles(ins["L1_attn_ln1_g"]), bias_tiles(ins["L1_attn_ln1_b"]),
                           bias_tiles(ins["L1_attn_ln2_g"]), bias_tiles(ins["L1_attn_ln2_b"])], axis=1)
    mixs = np.concatenate([bias_tiles(ins["L1_attn_mix_k"]), bias_tiles(ins["L1_attn_mix_v"]),
                           bias_tiles(ins["L1_attn_mix_r"]), bias_tiles(ins["L1_attn_cmix_k"]),
                           bias_tiles(ins["L1_attn_cmix_r"])], axis=1)
    return {"L1a_wkt": tl(ins["L1_attn_Wk_t"]), "L1a_wvt": tl(ins["L1_attn_Wv_t"]),
            "L1a_wrt": tl(ins["L1_attn_Wr_t"]), "L1a_wkc": tl(ins["L1_attn_Wk_c"]),
            "L1a_wvc": tl(ins["L1_attn_Wv_c"]), "L1a_wrc": tl(ins["L1_attn_Wr_c"]),
            "L1a_lnab": np.ascontiguousarray(lnab), "L1a_mix": np.ascontiguousarray(mixs),
            "L1a_u": bias_tiles(ins["L1_attn_time_first"])}


def decl_l1attn(nc):
    return dict(
        wkt_ap=nc.dram_tensor("L1a_wkt", [8, 128, 1024], F32, kind="ExternalInput").ap(),
        wvt_ap=nc.dram_tensor("L1a_wvt", [8, 128, 1024], F32, kind="ExternalInput").ap(),
        wrt_ap=nc.dram_tensor("L1a_wrt", [8, 128, 1024], F32, kind="ExternalInput").ap(),
        wkc_ap=nc.dram_tensor("L1a_wkc", [32, 128, 1024], F32, kind="ExternalInput").ap(),
        wvc_ap=nc.dram_tensor("L1a_wvc", [8, 128, 4096], F32, kind="ExternalInput").ap(),
        wrc_ap=nc.dram_tensor("L1a_wrc", [8, 128, 1024], F32, kind="ExternalInput").ap(),
        lnab_ap=nc.dram_tensor("L1a_lnab", [128, 32], F32, kind="ExternalInput").ap(),
        mix_ap=nc.dram_tensor("L1a_mix", [128, 40], F32, kind="ExternalInput").ap(),
        u_ap=nc.dram_tensor("L1a_u", [128, 8], F32, kind="ExternalInput").ap(),
    )


def emit_l0ffn(p: Prog, router_ap, w_ap):
    nc = p.nc
    SEG = 512
    NSG2 = TOK // SEG
    FF = 22
    with ExitStack() as es:
        rt = p.sbuf(es, [128, 64], F32)
        rtb = p.sbuf(es, [128, 64], BF16)
        Hseg = p.sbuf(es, [128, FCH * SEG], BF16)
        M1 = p.sbuf(es, [128, FF * SEG], BF16)
        OUT = p.sbuf(es, [128, FCH * SEG], F32)
        sc_tm = p.sbuf(es, [128, 32], F32)
        mx_tm = p.sbuf(es, [128, 4], F32)
        gekm = p.sbuf(es, [8, SEG], F32)
        cbuf = p.sbuf(es, [128, 3584], F32)
        GX = cbuf[:, 1536:3584].bitcast(BF16)
        ge = cbuf[0:1, 0:SEG]
        mxrow = cbuf[0:1, SEG:2 * SEG]
        grow = cbuf[0:1, 2 * SEG:3 * SEG]
        sc_km = cbuf[0:8, 3 * SEG:4 * SEG]
        geq = p.take_dq(1, sw=True)[0]
        with p.block() as blk:
            @blk.sync
            def _(s):
                p.inc(s.dma_start(out=rt[:].rearrange("p (kc e) -> p kc e", kc=8),
                                  in_=router_ap.rearrange("kc p e -> p kc e")), "dma", 16)
            @blk.vector
            def _(v):
                v.wait_ge(p.S["dma"], p.C["dma"])
                p.inc(v.tensor_copy(rtb[:], rt[:]), "dve")
        p.barrier()
        for sg in range(NSG2):
            tok0 = sg * SEG
            emit_ln(p, p.X, dst=Hseg, ntok=SEG, src_tok0=tok0,
                    dst_stride=SEG, dst_tok0=0, scratch=cbuf)
            with ExitStack() as es2:
                ps_s = p.psum(es2, [8, 512])
                ps_t = [p.psum(es2, [128, 8], F32) for _ in range(2)]
                ps_r = [p.psum(es2, [1, 128], F32) for _ in range(2)]
                ps_b = p.psum(es2, [8, 512])
                with p.block() as blk:
                    @blk.tensor
                    def _(t):
                        base = dict(p.C)
                        t.wait_ge(p.S["dve"], base["dve"])
                        for kc in range(FCH):
                            ins = t.matmul(ps_s[:], rtb[:, kc * 8:(kc + 1) * 8],
                                           Hseg[:, kc * SEG:(kc + 1) * SEG],
                                           start=(kc == 0), stop=(kc == FCH - 1))
                        p.inc(ins, "pe")
                    @blk.scalar
                    def _(a):
                        a.wait_ge(p.S["pe"], p.C["pe"])
                        p.inc(a.activation(sc_km[:], ps_s[:, 0:SEG], AF.Identity, scale=1.0), "act")
                with p.block() as blk:
                    base = dict(p.C)
                    @blk.tensor
                    def _(t):
                        for j in range(4):
                            if j >= 2:
                                t.wait_ge(p.S["dve"], base["dve"] + 2 * (j - 1))
                            p.inc(t.transpose(ps_t[j % 2][:], sc_km[:, j * 128:(j + 1) * 128], p.eyef[0:8, 0:8]), "pe")
                    @blk.vector
                    def _(v):
                        for j in range(4):
                            v.wait_ge(p.S["pe"], base["pe"] + j + 1)
                            p.fence(v, "dve")
                            p.inc(v.tensor_copy(sc_tm[:, j * 8:(j + 1) * 8], ps_t[j % 2][:]), "dve")
                            p.fence(v, "dve")
                            p.inc(v.tensor_reduce(mx_tm[:, j:j + 1], sc_tm[:, j * 8:(j + 1) * 8], mybir.AxisListType.X, ALU.max), "dve")
                with p.block() as blk:
                    base = dict(p.C)
                    @blk.tensor
                    def _(t):
                        for j in range(4):
                            if j >= 2:
                                t.wait_ge(p.S["act"], base["act"] + j - 1)
                            p.inc(t.transpose(ps_r[j % 2][:], mx_tm[:, j:j + 1], p.eyef[:]), "pe")
                    @blk.scalar
                    def _(a):
                        for j in range(4):
                            a.wait_ge(p.S["pe"], base["pe"] + j + 1)
                            p.inc(a.activation(mxrow[0:1, j * 128:(j + 1) * 128], ps_r[j % 2][:], AF.Identity, scale=1.0), "act")
                with p.block() as blk:
                    @blk.scalar
                    def _(a):
                        p.fence(a, "act")
                        p.inc(a.activation(grow[:], mxrow[:], AF.Sigmoid, scale=1.0), "act")
                    @blk.tensor
                    def _(t):
                        p.inc(t.matmul(ps_b[:, 0:SEG], p.rowsel[0:1, 0:8], mxrow[:], start=True, stop=True), "pe")
                    @blk.vector
                    def _(v):
                        v.wait_ge(p.S["pe"], p.C["pe"])
                        p.fence(v, "dve")
                        p.inc(v.tensor_tensor(gekm[:], sc_km[:], ps_b[:, 0:SEG], ALU.is_ge), "dve")
                with p.block() as blk:
                    @blk.tensor
                    def _(t):
                        p.inc(t.matmul(ps_b[:, 0:SEG], p.rowsel[0:1, 0:8], grow[:], start=True, stop=True), "pe")
                    @blk.vector
                    def _(v):
                        v.wait_ge(p.S["pe"], p.C["pe"])
                        p.fence(v, "dve")
                        p.inc(v.tensor_tensor(gekm[:], gekm[:], ps_b[:, 0:SEG], ALU.mult), "dve")
                p.barrier(("tensor", "vector", "scalar"))
            with ExitStack() as es2:
                ps_bc = p.psum(es2, [128, 512])
                for e in range(9):
                    if e < 8:
                        with p.block() as blk:
                            @blk.gpsimd
                            def _(g):
                                g.wait_ge(p.S["dve"], p.C["dve"])
                                g.wait_ge(p.S["pe"], p.C["pe"])
                                p.inc(g.dma_start(out=ge[:], in_=gekm[e:e + 1, :]), geq, 16)
                        with p.block() as blk:
                            @blk.tensor
                            def _(t):
                                t.wait_ge(p.S[geq], p.C[geq])
                                p.inc(t.matmul(ps_bc[:], p.rowsel[0:1, 0:128], ge[:], start=True, stop=True), "pe")
                            @blk.vector
                            def _(v):
                                v.wait_ge(p.S["pe"], p.C["pe"])
                                for c in range(FCH):
                                    p.fence(v, "dve")
                                    p.inc(v.tensor_tensor(GX[:, c * SEG:(c + 1) * SEG],
                                                          Hseg[:, c * SEG:(c + 1) * SEG], ps_bc[:], ALU.mult), "dve")
                        p.barrier(("tensor", "vector", "gpsimd", "sync"))
                        rhs_x = GX
                    else:
                        rhs_x = Hseg

                    def epi1(a, oc, j, psv):
                        return a.activation(M1[:, oc * SEG:(oc + 1) * SEG], psv, AF.Silu, scale=1.0)
                    emit_gemm(p, lambda oc, hh, e=e: w_ap[0][e, oc, :, :], FF, FCH,
                              lambda kc, j, rhs_x=rhs_x: rhs_x[:, kc * SEG:(kc + 1) * SEG], epi1, kcw=8, nbuf=1)

                    def epi3(v, oc, j, psv):
                        return v.tensor_tensor(M1[:, oc * SEG:(oc + 1) * SEG], psv,
                                               M1[:, oc * SEG:(oc + 1) * SEG], ALU.mult)
                    emit_gemm(p, lambda oc, hh, e=e: w_ap[1][e, oc, :, :], FF, FCH,
                              lambda kc, j, rhs_x=rhs_x: rhs_x[:, kc * SEG:(kc + 1) * SEG], epi3,
                              epi_eng="vector", cast_eng="scalar", kcw=8, nbuf=1)

                    if e == 0:
                        def epi2(v, oc, j, psv):
                            return v.tensor_copy(OUT[:, oc * SEG:(oc + 1) * SEG], psv)
                    else:
                        def epi2(v, oc, j, psv):
                            return v.tensor_tensor(OUT[:, oc * SEG:(oc + 1) * SEG], OUT[:, oc * SEG:(oc + 1) * SEG], psv, ALU.add)
                    emit_gemm(p, lambda oc, hh, e=e: w_ap[2][e, oc, :, hh * 1408:(hh + 1) * 1408], FCH, FF,
                              lambda kc, j: M1[:, kc * SEG:(kc + 1) * SEG], epi2,
                              epi_eng="vector", cast_eng="scalar", kcw=11, nbuf=1)
            emit_ln(p, OUT, acc_into=p.X, ntok=SEG, src_stride=SEG,
                    dst_stride=TOK, dst_tok0=tok0, scratch=cbuf)


def prep_l0ffn(ins):
    r = np.asarray(ins["L0_ffn_router_DE"]).astype(np.float32).reshape(8, 128, 8)
    w1 = np.asarray(ins["L0_ffn_w1_eDF"]).astype(np.float32)
    w3 = np.asarray(ins["L0_ffn_w3_eDF"]).astype(np.float32)
    w2 = np.asarray(ins["L0_ffn_w2_eFD"]).astype(np.float32)
    ws1 = np.asarray(ins["L0_ffn_ws1_DF"]).astype(np.float32)
    ws3 = np.asarray(ins["L0_ffn_ws3_DF"]).astype(np.float32)
    ws2 = np.asarray(ins["L0_ffn_ws2_FD"]).astype(np.float32)
    def t9(mats):
        return np.stack([tile_lhsT(m) for m in mats], axis=0)
    w1t = t9([w1[e] for e in range(8)] + [ws1])
    w3t = t9([w3[e] for e in range(8)] + [ws3])
    w2t = t9([w2[e] for e in range(8)] + [ws2])
    return {"L0f_rt": np.ascontiguousarray(r), "L0f_w1": np.ascontiguousarray(w1t),
            "L0f_w3": np.ascontiguousarray(w3t), "L0f_w2": np.ascontiguousarray(w2t)}


def decl_l0ffn(nc):
    w1 = nc.dram_tensor("L0f_w1", [9, 22, 128, 1024], F32, kind="ExternalInput").ap()
    w3 = nc.dram_tensor("L0f_w3", [9, 22, 128, 1024], F32, kind="ExternalInput").ap()
    w2 = nc.dram_tensor("L0f_w2", [9, 8, 128, 2816], F32, kind="ExternalInput").ap()
    return dict(router_ap=nc.dram_tensor("L0f_rt", [8, 128, 8], F32, kind="ExternalInput").ap(),
                w_ap=(w1, w3, w2))


def emit_l1ffn(p: Prog, wexp_ap, wcon_ap, upc_ap, dnc_ap, bexp_ap, biases_ap):
    nc = p.nc
    w1p = nc.dram_tensor("l1w1p", [32, 128, 1024], BF16).ap()
    w2tmp = nc.dram_tensor("l1w2t", [8, 128, 4096], BF16).ap()
    w2p = nc.dram_tensor("l1w2p", [8, 128, 4096], BF16).ap()
    es0 = ExitStack()
    b1t = p.sbuf(es0, [128, 32], F32)
    b2t = p.sbuf(es0, [128, 8], F32)
    with ExitStack() as es:
        WF = p.sbuf(es, [128, 4096], F32)
        WFb = p.sbuf(es, [128, 4096], BF16)
        brow = p.sbuf(es, [1, 4096], F32)
        cf = [p.sbuf(es, [128, 2048], BF16) for _ in range(4)]
        t1 = p.sbuf(es, [128, 2048], F32)
        t2 = p.sbuf(es, [128, 2048], F32)
        t3 = WFb[:].bitcast(F32)[:, 0:2048]
        crow = [t1[0:1, 0:2048], t2[0:1, 0:2048]]
        bs = p.sbuf(es, [128, 48], F32)
        wq = p.take_dq(2)
        cq = p.take_dq(1)[0]
        sq = p.take_dq(1, sw=True)[0]
        with p.block() as blk:
            @blk.sync
            def _(s):
                p.inc(s.dma_start(out=bs[:], in_=biases_ap), "dma", 16)
        for side in range(2):
            w_in = wexp_ap if side == 0 else wcon_ap
            c_in = upc_ap if side == 0 else dnc_ap
            strides = [2 ** t for t in range(12)] if side == 0 else [2 ** (11 - t) for t in range(12)]
            for pp in range(8):
                with p.block() as blk:
                    @blk.sync
                    def _(s):
                        s.wait_ge(p.S["dve"], p.C["dve"])
                        s.wait_ge(p.S[sq], p.C[sq])
                        p.inc(s.dma_start(out=WF[:], in_=w_in[pp]), wq[0], 16)
                        if side == 0 and pp == 0:
                            p.inc(s.dma_start(out=brow[:], in_=bexp_ap), wq[1], 16)
                    @blk.vector
                    def _(v):
                        v.wait_ge(p.S[wq[0]], p.C[wq[0]])
                        v.wait_ge(p.S[wq[1]], p.C[wq[1]])
                for sti, sz in enumerate(strides):
                    with ExitStack() as es2:
                        ps_c = p.psum(es2, [128, 512])
                        base = dict(p.C)
                        with p.block() as blk:
                            @blk.sync
                            def _(s):
                                for i4 in range(4):
                                    if i4 >= 2:
                                        s.wait_ge(p.S["act"], base["act"] + 4 * (i4 - 1))
                                    p.inc(s.dma_start(out=crow[i4 % 2][:],
                                                      in_=c_in[sti, i4].rearrange("(one n) -> one n", one=1)), wq[i4 % 2], 16)
                            @blk.tensor
                            def _(t):
                                t.wait_ge(p.S["dve"], base["dve"])
                                for i4 in range(4):
                                    t.wait_ge(p.S[wq[i4 % 2]], base[wq[i4 % 2]] + 16 * (i4 // 2 + 1))
                                    for j in range(4):
                                        if i4 * 4 + j >= 1:
                                            t.wait_ge(p.S["act"], base["act"] + i4 * 4 + j)
                                        p.inc(t.matmul(ps_c[:], p.rowsel[0:1, 0:128],
                                                       crow[i4 % 2][0:1, j * 512:(j + 1) * 512],
                                                       start=True, stop=True), "pe")
                            @blk.scalar
                            def _(a):
                                for i4 in range(4):
                                    for j in range(4):
                                        a.wait_ge(p.S["pe"], base["pe"] + i4 * 4 + j + 1)
                                        p.inc(a.activation(cf[i4][:, j * 512:(j + 1) * 512], ps_c[:], AF.Identity, scale=1.0), "act")
                        for dc in range(1):
                            W4 = WF[:, 0:4096]
                            def rs(x):
                                return x.rearrange("q (nb i) -> q nb i", i=sz)
                            av = W4.rearrange("q (nb two i) -> q nb two i", two=2, i=sz)[:, :, 0, :]
                            bv = W4.rearrange("q (nb two i) -> q nb two i", two=2, i=sz)[:, :, 1, :]
                            ops = []
                            ops.append(("vector", lambda v, av=av: v.tensor_tensor(rs(t1[:]), av, rs(cf[0][:]), ALU.mult)))
                            ops.append(("vector", lambda v, av=av: v.tensor_tensor(rs(t2[:]), av, rs(cf[2][:]), ALU.mult)))
                            ops.append(("vector", lambda v, bv=bv: v.tensor_tensor(rs(t3[:]), bv, rs(cf[1][:]), ALU.mult)))
                            ops.append(("vector", lambda v, av=av: v.tensor_tensor(av, rs(t1[:]), rs(t3[:]), ALU.add)))
                            ops.append(("vector", lambda v, bv=bv: v.tensor_tensor(rs(t3[:]), bv, rs(cf[3][:]), ALU.mult)))
                            ops.append(("vector", lambda v, bv=bv: v.tensor_tensor(bv, rs(t2[:]), rs(t3[:]), ALU.add)))
                            if side == 0 and pp == 0 and dc == 0:
                                bav = brow[:].rearrange("q (nb two i) -> q nb two i", two=2, i=sz)[:, :, 0, :]
                                bbv = brow[:].rearrange("q (nb two i) -> q nb two i", two=2, i=sz)[:, :, 1, :]
                                ops.append(("vector", lambda v, bav=bav: v.tensor_tensor(rs(t1[0:1, :]), bav, rs(cf[0][0:1, :]), ALU.mult)))
                                ops.append(("vector", lambda v, bav=bav: v.tensor_tensor(rs(t2[0:1, :]), bav, rs(cf[2][0:1, :]), ALU.mult)))
                                ops.append(("vector", lambda v, bbv=bbv: v.tensor_tensor(rs(t3[0:1, :]), bbv, rs(cf[1][0:1, :]), ALU.mult)))
                                ops.append(("vector", lambda v, bav=bav: v.tensor_tensor(bav, rs(t1[0:1, :]), rs(t3[0:1, :]), ALU.add)))
                                ops.append(("vector", lambda v, bbv=bbv: v.tensor_tensor(rs(t3[0:1, :]), bbv, rs(cf[3][0:1, :]), ALU.mult)))
                                ops.append(("vector", lambda v, bbv=bbv: v.tensor_tensor(bbv, rs(t2[0:1, :]), rs(t3[0:1, :]), ALU.add)))
                            emit_seq(p, ops)
                with p.block() as blk:
                    @blk.vector
                    def _(v):
                        p.fence(v, "dve")
                        p.inc(v.tensor_copy(WFb[:], WF[:]), "dve")
                    @blk.gpsimd
                    def _(g):
                        g.wait_ge(p.S["dve"], p.C["dve"])
                        if side == 0:
                            dst = BassAP(w1p.tensor, pp * 128,
                                         [[1024, 128], [131072, 32], [1, 128]])
                            p.inc(g.dma_start(out=dst, in_=WFb[:].rearrange("q (oc m) -> q oc m", oc=32)), sq, 16)
                        else:
                            p.inc(g.dma_start(out=w2tmp[pp], in_=WFb[:]), sq, 16)
        p.barrier()
        with ExitStack() as es2:
            ps_tb = [p.psum(es2, [128, 1]) for _ in range(2)]
            base = dict(p.C)
            with p.block() as blk:
                @blk.tensor
                def _(t):
                    for oc in range(32):
                        if oc >= 2:
                            t.wait_ge(p.S["act"], base["act"] + oc - 1)
                        p.inc(t.transpose(ps_tb[oc % 2][:], brow[0:1, oc * 128:(oc + 1) * 128], p.eyef[0:1, 0:1]), "pe")
                @blk.scalar
                def _(a):
                    for oc in range(32):
                        a.wait_ge(p.S["pe"], base["pe"] + oc + 1)
                        p.inc(a.activation(b1t[:, oc:oc + 1], ps_tb[oc % 2][:], AF.Identity, scale=1.0), "act")
            with p.block() as blk:
                @blk.vector
                def _(v):
                    v.wait_ge(p.S["act"], p.C["act"])
                    v.wait_ge(p.S["dma"], p.C["dma"])
                    p.fence(v, "dve")
                    p.inc(v.tensor_tensor(b1t[:], b1t[:], bs[:, 0:32], ALU.add), "dve")
                    p.inc(v.tensor_tensor(b2t[:], bs[:, 32:40], bs[:, 40:48], ALU.add), "dve")
        p.barrier()
        sqs = p.take_dq(2, sw=True)
        with ExitStack() as es2:
            ps_t2 = [p.psum(es2, [128, 128], BF16) for _ in range(2)]
            tst = [p.sbuf(es2, [128, 128], BF16) for _ in range(2)]
            for ooc in range(8):
                with p.block() as blk:
                    @blk.sync
                    def _(s):
                        s.wait_ge(p.S["pe"], p.C["pe"])
                        s.wait_ge(p.S[sq], p.C[sq])
                        p.inc(s.dma_start(out=WFb[:, 0:4096], in_=w2tmp[ooc]), wq[0], 16)
                    @blk.vector
                    def _(v):
                        v.wait_ge(p.S[wq[0]], p.C[wq[0]])
                base2 = dict(p.C)
                with p.block() as blk:
                    @blk.tensor
                    def _(t):
                        for hc in range(32):
                            if hc >= 2:
                                t.wait_ge(p.S["dve"], base2["dve"] + hc - 1)
                            p.inc(t.transpose(ps_t2[hc % 2][:], WFb[:, hc * 128:(hc + 1) * 128], p.eye[:]), "pe")
                    @blk.vector
                    def _(v):
                        for hc in range(32):
                            v.wait_ge(p.S["pe"], base2["pe"] + hc + 1)
                            if hc >= 2:
                                v.wait_ge(p.S[sqs[hc % 2]], base2[sqs[hc % 2]] + 16 * (hc // 2))
                            elif base2[sqs[hc % 2]] > 0:
                                v.wait_ge(p.S[sqs[hc % 2]], base2[sqs[hc % 2]])
                            p.fence(v, "dve")
                            p.inc(v.tensor_copy(tst[hc % 2][:], ps_t2[hc % 2][:]), "dve")
                    @blk.gpsimd
                    def _(g):
                        for hc in range(32):
                            g.wait_ge(p.S["dve"], base2["dve"] + hc + 1)
                            p.inc(g.dma_start(out=w2p[ooc, :, hc * 128:(hc + 1) * 128], in_=tst[hc % 2][:]), sqs[hc % 2], 16)
        p.barrier()
    SEG = 512
    with ExitStack() as es:
        Hseg = p.sbuf(es, [128, FCH * SEG], BF16)
        G = p.sbuf(es, [128, 32 * SEG], BF16)
        Yseg = p.sbuf(es, [128, FCH * SEG], F32)
        cbuf = p.sbuf(es, [128, 3584], F32)
        for sg in range(TOK // SEG):
            tok0 = sg * SEG
            emit_ln(p, p.X, dst=Hseg, ntok=SEG, src_tok0=tok0,
                    dst_stride=SEG, dst_tok0=0, scratch=cbuf)

            def epi1(a, oc, j, psv):
                return a.activation(G[:, oc * SEG:(oc + 1) * SEG], psv,
                                    AF.Gelu, bias=b1t[:, oc:oc + 1], scale=1.0)
            emit_gemm(p, lambda oc, hh: w1p[oc, :, :], 32, FCH,
                      lambda kc, j: Hseg[:, kc * SEG:(kc + 1) * SEG], epi1,
                      wdtype=BF16, kcw=8, nbuf=1)

            def epi2(a, oc, j, psv):
                return a.activation(Yseg[:, oc * SEG:(oc + 1) * SEG], psv,
                                    AF.Identity, bias=b2t[:, oc:oc + 1], scale=1.0)
            emit_gemm(p, lambda oc, hh: w2p[oc, :, hh * 1024:(hh + 1) * 1024], FCH, 32,
                      lambda kc, j: G[:, kc * SEG:(kc + 1) * SEG], epi2,
                      wdtype=BF16, kcw=8, nbuf=1)

            emit_ln(p, Yseg, acc_into=p.X, ntok=SEG, src_stride=SEG,
                    dst_stride=TOK, dst_tok0=tok0, scratch=cbuf)
    es0.close()


def prep_l1ffn(ins):
    wexp = np.ascontiguousarray(np.asarray(ins["L1_ffn_W_exp"]).astype(np.float32).reshape(8, 128, 4096))
    wcon = np.ascontiguousarray(np.asarray(ins["L1_ffn_W_con"]).astype(np.float32).T.reshape(8, 128, 4096))
    up = np.asarray(ins["L1_ffn_up_weights"]).astype(np.float32)
    dn = np.asarray(ins["L1_ffn_down_weights"]).astype(np.float32)
    upc = np.ascontiguousarray(up.transpose(0, 2, 3, 1).reshape(12, 4, 2048))
    dnt = dn.transpose(0, 3, 2, 1).reshape(12, 4, 2048)
    dnc = np.ascontiguousarray(dnt[::-1])
    bexp = np.ascontiguousarray(np.asarray(ins["L1_ffn_b_exp"]).astype(np.float32).reshape(1, 4096))
    biases = np.concatenate([bias_tiles(ins["L1_ffn_up_bias"]),
                             bias_tiles(ins["L1_ffn_b_con"]),
                             bias_tiles(ins["L1_ffn_down_bias"])], axis=1)
    return {"L1f_wexp": wexp, "L1f_wcon": wcon, "L1f_upc": upc, "L1f_dnc": dnc,
            "L1f_bexp": bexp, "L1f_bias": np.ascontiguousarray(biases)}


def decl_l1ffn(nc):
    return dict(
        wexp_ap=nc.dram_tensor("L1f_wexp", [8, 128, 4096], F32, kind="ExternalInput").ap(),
        wcon_ap=nc.dram_tensor("L1f_wcon", [8, 128, 4096], F32, kind="ExternalInput").ap(),
        upc_ap=nc.dram_tensor("L1f_upc", [12, 4, 2048], F32, kind="ExternalInput").ap(),
        dnc_ap=nc.dram_tensor("L1f_dnc", [12, 4, 2048], F32, kind="ExternalInput").ap(),
        bexp_ap=nc.dram_tensor("L1f_bexp", [1, 4096], F32, kind="ExternalInput").ap(),
        biases_ap=nc.dram_tensor("L1f_bias", [128, 48], F32, kind="ExternalInput").ap(),
    )


BLOCKS = [
    ("l0attn", decl_l0attn, emit_l0attn, prep_l0attn, True, False),
    ("l0ffn", decl_l0ffn, emit_l0ffn, prep_l0ffn, True, True),
    ("l1attn", decl_l1attn, emit_l1attn, prep_l1attn, False, False),
    ("l1ffn", decl_l1ffn, emit_l1ffn, prep_l1ffn, True, True),
    ("l2attn", decl_l2attn, emit_l2attn, prep_l2attn, False, False),
    ("l2ffn", decl_l2ffn, emit_l2ffn, prep_l2ffn, False, False),
    ("l3attn", decl_l3attn, emit_l3attn, prep_l3attn, True, False),
    ("l3ffn", decl_l3ffn, emit_l3ffn, prep_l3ffn, False, False),
]

_prog_cache = {}


def build_block_program(idx):
    if idx in _prog_cache:
        return _prog_cache[idx]
    name, decl, emit, _, use_eye, use_eyef = BLOCKS[idx]
    nc = bass.Bass("TRN2", target_bir_lowering=False)
    x_in = nc.dram_tensor("x", [D, TOK], F32, kind="ExternalInput").ap()
    aux_in = nc.dram_tensor("aux", [128, AUXC], F32, kind="ExternalInput").ap()
    sel_in = nc.dram_tensor("sel", [128, 64], BF16, kind="ExternalInput").ap()
    rowsel_in = nc.dram_tensor("rowsel", [8, 512], F32, kind="ExternalInput").ap()
    eye_in = nc.dram_tensor("eye", [128, 128], BF16, kind="ExternalInput").ap() if use_eye else None
    eyef_in = nc.dram_tensor("eyef", [128, 128], F32, kind="ExternalInput").ap() if use_eyef else None
    x_out = nc.dram_tensor("xo", [D, TOK], F32, kind="ExternalOutput").ap()
    daps = decl(nc)
    with ExitStack() as es:
        p = Prog(nc, es)
        with nc.Block() as blk:
            p.blk = blk
            emit_setup(p, x_in, aux_in, sel_in, rowsel_in, eye_in=eye_in, eyef_in=eyef_in)
            emit(p, **daps)
            emit_store_x(p, x_out)
        p.blk = None
    _prog_cache[idx] = nc
    return nc


def kernel(**inputs):
    consts = {"aux": host_aux(), "sel": host_sel(),
              "rowsel": host_rowsel().astype(np.float32),
              "eye": host_eye(), "eyef": np.eye(128, dtype=np.float32)}
    xs = [feat_major(np.asarray(inputs["L0_attn_x"])[2 * c:2 * c + 2])
          for c in range(NCORES)]
    for idx, (name, decl, emit, prep, use_eye, use_eyef) in enumerate(BLOCKS):
        nc = build_block_program(idx)
        wk = prep(inputs)
        in_maps = []
        for c in range(NCORES):
            m = dict(consts)
            if not use_eye:
                m.pop("eye", None)
            if not use_eyef:
                m.pop("eyef", None)
            m = {k: v for k, v in m.items()}
            m.update(wk)
            m["x"] = xs[c]
            in_maps.append(m)
        res = run_bass_kernel_spmd(nc, in_maps, core_ids=list(range(NCORES)))
        xs = [np.asarray(res.results[c]["xo"]) for c in range(NCORES)]
    out = np.empty((B, T, D), dtype=np.float32)
    for c in range(NCORES):
        out[2 * c:2 * c + 2] = xs[c].T.reshape(2, T, D)
    return out


_fused_cache = []


def build_fused_program():
    if _fused_cache:
        return _fused_cache[0]
    nc = bass.Bass("TRN2", target_bir_lowering=False)
    x_in = nc.dram_tensor("x", [D, TOK], F32, kind="ExternalInput").ap()
    aux_in = nc.dram_tensor("aux", [128, AUXC], F32, kind="ExternalInput").ap()
    sel_in = nc.dram_tensor("sel", [128, 64], BF16, kind="ExternalInput").ap()
    rowsel_in = nc.dram_tensor("rowsel", [8, 512], F32, kind="ExternalInput").ap()
    eye_in = nc.dram_tensor("eye", [128, 128], BF16, kind="ExternalInput").ap()
    eyef_in = nc.dram_tensor("eyef", [128, 128], F32, kind="ExternalInput").ap()
    x_out = nc.dram_tensor("xo", [D, TOK], F32, kind="ExternalOutput").ap()
    alldaps = []
    for name, decl, emit, _, _, _ in BLOCKS:
        alldaps.append(decl(nc))
    with ExitStack() as es:
        p = Prog(nc, es)
        with nc.Block() as blk:
            p.blk = blk
            with nc.named_scope("setup"):
                emit_setup(p, x_in, aux_in, sel_in, rowsel_in, eye_in=eye_in, eyef_in=eyef_in)
            for (name, decl, emit, _, _, _), daps in zip(BLOCKS, alldaps):
                with nc.named_scope(name):
                    emit(p, **daps)
            with nc.named_scope("store"):
                emit_store_x(p, x_out)
        p.blk = None
    _fused_cache.append(nc)
    return nc


def kernel_fused(**inputs):
    consts = {"aux": host_aux(), "sel": host_sel(),
              "rowsel": host_rowsel().astype(np.float32),
              "eye": host_eye(), "eyef": np.eye(128, dtype=np.float32)}
    wk = {}
    for name, decl, emit, prep, _, _ in BLOCKS:
        wk.update(prep(inputs))
    nc = build_fused_program()
    xs = [feat_major(np.asarray(inputs["L0_attn_x"])[2 * c:2 * c + 2])
          for c in range(NCORES)]
    in_maps = []
    for c in range(NCORES):
        m = dict(consts)
        m.update(wk)
        m["x"] = xs[c]
        in_maps.append(m)
    res = run_bass_kernel_spmd(nc, in_maps, core_ids=list(range(NCORES)))
    out = np.empty((B, T, D), dtype=np.float32)
    for c in range(NCORES):
        out[2 * c:2 * c + 2] = np.asarray(res.results[c]["xo"]).T.reshape(2, T, D)
    return out


_orig_kernel_unfused = kernel

def kernel(**inputs):
    import os as _os
    if _os.environ.get("MK_UNFUSED", "0") == "1":
        return _orig_kernel_unfused(**inputs)
    return kernel_fused(**inputs)


# version 06
# speedup vs baseline: 1.9075x; 1.9075x over previous
import numpy as np
import ml_dtypes
from contextlib import ExitStack
import concourse.bass as bass
import concourse.mybir as mybir
from concourse.bass_utils import run_bass_kernel_spmd

F32 = mybir.dt.float32
BF16 = mybir.dt.bfloat16
U8 = mybir.dt.uint8
AF = mybir.ActivationFunctionType
ALU = mybir.AluOpType

NCORES = 8
WRN = 4
WRB = 2816
B, T, D = 16, 2048, 1024
SEQ = 2
TOK = SEQ * T
FCH = D // 128
TSEG = 512
NSEG = TOK // TSEG
MAGIC = 12582912.0

ENGINES = ("sync", "tensor", "vector", "scalar", "gpsimd")
CENG = ("tensor", "vector", "scalar", "gpsimd")



def tile_lhsT(W):
    K, M = W.shape
    assert K % 128 == 0 and M % 128 == 0, (K, M)
    t = W.reshape(K // 128, 128, M // 128, 128).transpose(2, 1, 0, 3)
    return np.ascontiguousarray(t.reshape(M // 128, 128, K))


def bias_tiles(b):
    M = b.shape[0]
    return np.ascontiguousarray(np.asarray(b).reshape(M // 128, 128).T.astype(np.float32))


def feat_major(x):
    x = np.asarray(x).reshape(-1, x.shape[-1])
    return np.ascontiguousarray(x.T.astype(np.float32))


def host_sel():
    a = np.zeros((128, 8, 8), dtype=np.float32)
    for j in range(8):
        a[:, j, j] = 1.0
    return a.reshape(128, 64).astype(ml_dtypes.bfloat16)


def host_rowsel():
    a = np.zeros((8, 4, 128), dtype=np.float32)
    for j in range(4):
        a[j, j, :] = 1.0
    return a.reshape(8, 512)


def host_eye():
    return np.eye(128, dtype=np.float32).astype(ml_dtypes.bfloat16)


AUX_ZERO, AUX_ONE, AUX_EPS5, AUX_EPS3, AUX_WKVEPS, AUX_HALFPI = 0, 1, 2, 3, 4, 5
AUXC = 8

def host_aux():
    a = np.zeros((128, AUXC), dtype=np.float32)
    a[:, AUX_ONE] = 1.0
    a[:, AUX_EPS5] = 1e-5
    a[:, AUX_EPS3] = 1e-3
    a[:, AUX_WKVEPS] = 1e-8
    a[:, AUX_HALFPI] = np.pi / 2
    return a



class Prog:
    def __init__(self, nc, es):
        self.nc, self.es = nc, es
        self.uid = 0
        self.S = {}
        self.C = {}
        for s in ("dma", "dmo", "pe", "dve", "act", "pool", "bar", "sp",
                  "dq0", "dq1", "dq2", "dq3", "dq4", "dq5", "dq6", "dq7",
                  "wr0", "wr1", "wr2", "wr3", "wr4", "wr5"):
            self.S[s] = es.enter_context(nc.semaphore(s))
            self.C[s] = 0
        self.X = None
        self.aux = None
        self.sel = None
        self.rowsel = None
        self.dq_rr = 0
        self.dq_sw_rr = 0
        self.blk = None
        self.lastjoin = {e: {} for e in ENGINES}
        self.wring = None
        self.wr_pos = 0
        self.wr_guard = [None] * WRN

    def wr_init(p):
        if p.wring is None:
            p.wring = [p.sbuf(p.es, [128, WRB], U8) for _ in range(WRN)]

    def wr_plan(p, consume):
        i = p.wr_pos % WRN
        p.wr_pos += 1
        sem = f"wr{i}"
        p.C[sem] += 16
        prev = p.wr_guard[i]
        p.wr_guard[i] = consume
        return i, sem, p.C[sem], prev

    def block(self):
        if self.blk is not None:
            return _SharedBlk(self)
        return self.nc.Block()

    def take_dq(self, k, sw=False):
        if sw:
            out = [f"dq{6 + (self.dq_sw_rr + i) % 2}" for i in range(k)]
            self.dq_sw_rr = (self.dq_sw_rr + k) % 2
            return out
        out = [f"dq{(self.dq_rr + i) % 6}" for i in range(k)]
        self.dq_rr = (self.dq_rr + k) % 6
        return out

    def name(self, pfx):
        self.uid += 1
        return f"{pfx}{self.uid}"

    def sbuf(self, es, shape, dtype):
        return es.enter_context(self.nc.sbuf_tensor(self.name("t"), shape, dtype))

    def psum(self, es, shape, dtype=F32):
        return es.enter_context(self.nc.psum_tensor(self.name("p"), shape, dtype))

    def inc(self, instr, s, n=1):
        instr.then_inc(self.S[s], n)
        self.C[s] += n

    def fence(self, q, s):
        q.wait_ge(self.S[s], self.C[s])

    def barrier(p, engines=ENGINES):
        nc = p.nc
        ALLSEMS = ("sp", "pe", "dve", "act", "pool", "dma", "dmo",
                   "dq0", "dq1", "dq2", "dq3", "dq4", "dq5", "dq6", "dq7")
        snap = {nm: p.C[nm] for nm in ALLSEMS}

        def _emit(blk):
            for eng in engines:
                lj = p.lastjoin[eng]
                todo = [(nm, snap[nm]) for nm in ALLSEMS
                        if snap[nm] > lj.get(nm, 0)]
                if not todo:
                    continue

                def mk(items):
                    def _(q):
                        for nm, v in items:
                            q.wait_ge(p.S[nm], v)
                    return _
                getattr(blk, eng)(mk(todo))
                for nm, v in todo:
                    lj[nm] = v

        if p.blk is not None:
            _emit(p.blk)
        else:
            with nc.Block() as blk:
                _emit(blk)


class _SharedBlk:
    def __init__(self, p):
        self.p = p

    def __enter__(self):
        return self.p.blk

    def __exit__(self, exc_type, exc_val, exc_tb):
        if exc_type is None:
            self.p.barrier(CENG)
        return False


def emit_setup(p: Prog, x_in, aux_in, sel_in, rowsel_in, eye_in=None, eyef_in=None, load_x=True):
    nc = p.nc
    es = p.es
    p.X = p.sbuf(es, [128, FCH * TOK], F32)
    p.aux = p.sbuf(es, [128, AUXC], F32)
    p.sel = p.sbuf(es, [128, 64], BF16)
    p.rowsel = p.sbuf(es, [8, 512], F32)
    p.eye = p.sbuf(es, [128, 128], BF16) if eye_in is not None else None
    p.eyef = p.sbuf(es, [128, 128], F32) if eyef_in is not None else None
    p.wr_init()
    with p.block() as blk:
        @blk.sync
        def _(s):
            if load_x:
                p.inc(s.dma_start(out=p.X[:].rearrange("p (c t) -> p c t", c=FCH), in_=x_in.rearrange("(c p) t -> p c t", p=128)), "dma", 16)
            p.inc(s.dma_start(out=p.aux[:], in_=aux_in), "dma", 16)
            p.inc(s.dma_start(out=p.sel[:], in_=sel_in), "dma", 16)
            p.inc(s.dma_start(out=p.rowsel[:], in_=rowsel_in), "dma", 16)
            if eye_in is not None:
                p.inc(s.dma_start(out=p.eye[:], in_=eye_in), "dma", 16)
            if eyef_in is not None:
                p.inc(s.dma_start(out=p.eyef[:], in_=eyef_in), "dma", 16)
        @blk.vector
        def _(v):
            v.wait_ge(p.S["dma"], p.C["dma"])
    p.barrier()


def emit_store_x(p: Prog, x_out):
    nc = p.nc
    p.barrier()
    with p.block() as blk:
        @blk.sync
        def _(s):
            p.inc(s.dma_start(out=x_out.rearrange("(c p) t -> p c t", p=128), in_=p.X[:].rearrange("p (c t) -> p c t", c=FCH)), "dmo", 16)
            s.wait_ge(p.S["dmo"], p.C["dmo"])



EPS_COL = {1e-5: AUX_EPS5, 1e-3: AUX_EPS3}

def emit_ln(p: Prog, src, dst=None, dst2=None, acc_into=None, eps=1e-5,
            nch=FCH, src_stride=None, src_tok0=0, ntok=TOK,
            dst_stride=None, dst_tok0=0, gamma=None, beta=None, gb_col0=0, beta_col0=None,
            scratch=None):
    nc = p.nc
    sstride = src_stride if src_stride is not None else TOK
    dstride = dst_stride if dst_stride is not None else sstride
    seglen = min(TSEG, ntok)
    nseg = ntok // seglen
    assert ntok % seglen == 0 and nseg <= 4
    invn = 1.0 / (nch * 128)
    epsap = p.aux[0:8, EPS_COL[eps]:EPS_COL[eps] + 1]
    if beta_col0 is None:
        beta_col0 = gb_col0

    plain = (gamma is None and dst2 is None and acc_into is None and dst is not None)
    gsimple = (gamma is not None and dst is not None and dst2 is None and acc_into is None)
    nwr = (0 if (plain or gsimple) else
           (1 if dst is not None else 0) + (1 if dst2 is not None else 0)
           + (1 if acc_into is not None else 0))
    chain_len = 2 + (1 if gamma is not None else 0) + nwr
    napply = nch * chain_len

    with ExitStack() as es:
        s1 = p.psum(es, [128, 512])
        s2 = p.psum(es, [128, 512])
        bc = [p.psum(es, [128, 1024]) for _ in range(2)]
        nscr = 1 if plain else (3 if gamma is not None else 2)
        if scratch is not None:
            o = 0
            scr = [scratch[:, o + i * seglen:o + (i + 1) * seglen] for i in range(nscr)]
            o += nscr * seglen
            rows = scratch[0:8, o:o + 3 * seglen]
            o += 3 * seglen
            h = seglen // 2
            xb = [scratch[:, o + i * h:o + (i + 1) * h].bitcast(BF16) for i in range(2)]
            o += 2 * h
            xq = [scratch[:, o + i * h:o + (i + 1) * h].bitcast(BF16) for i in range(2)]
        else:
            xb = [p.sbuf(es, [128, seglen], BF16) for _ in range(2)]
            xq = [p.sbuf(es, [128, seglen], BF16) for _ in range(2)]
            rows = p.sbuf(es, [8, 1536], F32)
            scr = [p.sbuf(es, [128, seglen], F32) for _ in range(nscr)]
        X1 = rows[0:8, 0:seglen]
        X2 = rows[0:8, seglen:2 * seglen]
        X3 = rows[0:8, 2 * seglen:3 * seglen]
        R_A, R_B = X3, X2
        base = dict(p.C)
        N = nseg * nch

        def src_ap(c, g):
            t0 = src_tok0 + g * seglen
            return src[:, c * sstride + t0: c * sstride + t0 + seglen]

        def dst_ap(buf, c, g):
            t0 = dst_tok0 + g * seglen
            return buf[:, c * dstride + t0: c * dstride + t0 + seglen]

        with p.block() as blk:
            @blk.scalar
            def _(a):
                for i in range(N):
                    if i >= 2:
                        a.wait_ge(p.S["pe"], base["pe"] + i - 1)
                    p.inc(a.activation(xq[i % 2][:], src_ap(i % nch, i // nch), AF.Square), "act")
                a.wait_ge(p.S["dve"], base["dve"] + N + 3)
                p.inc(a.activation(X2, X3, AF.Sqrt, bias=epsap, scale=1.0), "act")

            @blk.tensor
            def _(t):
                for i in range(N):
                    g = i // nch
                    t.wait_ge(p.S["dve"], base["dve"] + i + 1)
                    t.wait_ge(p.S["act"], base["act"] + i + 1)
                    first, last = (i == 0), (i == N - 1)
                    selj = p.sel[:, g * 8:(g + 1) * 8]
                    t.matmul(s1[0:8, 0:seglen], selj, xb[i % 2][:], start=first, stop=last)
                    p.inc(t.matmul(s2[0:8, 0:seglen], selj, xq[i % 2][:], start=first, stop=last), "pe")
                for g in range(nseg):
                    t.wait_ge(p.S["dve"], base["dve"] + N + 5 + (0 if g < 2 else (g - 1) * napply))
                    rj = p.rowsel[0:8, g * 128:(g + 1) * 128]
                    p.inc(t.matmul(bc[g % 2][:, 0:seglen], rj, X3, start=True, stop=True), "pe")
                    p.inc(t.matmul(bc[g % 2][:, 512:512 + seglen], rj, X2, start=True, stop=True), "pe")

            @blk.vector
            def _(v):
                for i in range(N):
                    if i >= 2:
                        v.wait_ge(p.S["pe"], base["pe"] + i - 1)
                    p.inc(v.tensor_copy(xb[i % 2][:], src_ap(i % nch, i // nch)), "dve")
                v.wait_ge(p.S["pe"], base["pe"] + N)
                p.inc(v.tensor_scalar(X1, s1[0:8, 0:seglen], invn, None, ALU.mult), "dve")
                p.fence(v, "dve")
                p.inc(v.tensor_tensor(X2, X1, X1, ALU.mult), "dve")
                p.fence(v, "dve")
                p.inc(v.scalar_tensor_tensor(X3, s2[0:8, 0:seglen], invn, X2, ALU.mult, ALU.subtract), "dve")
                v.wait_ge(p.S["act"], base["act"] + N + 1)
                p.inc(v.reciprocal(X3, X2), "dve")
                p.fence(v, "dve")
                p.inc(v.scalar_tensor_tensor(X2, X1, -1.0, X3, ALU.mult, ALU.mult), "dve")
                for g in range(nseg):
                    v.wait_ge(p.S["pe"], base["pe"] + N + 2 * (g + 1))
                    A = bc[g % 2][:, 0:seglen]
                    Bv = bc[g % 2][:, 512:512 + seglen]
                    for c in range(nch):
                        p.fence(v, "dve")
                        p.inc(v.tensor_tensor(scr[0][:], src_ap(c, g), A, ALU.mult), "dve")
                        p.fence(v, "dve")
                        if plain:
                            p.inc(v.tensor_tensor(dst_ap(dst, c, g), scr[0][:], Bv, ALU.add), "dve")
                            continue
                        if gamma is not None:
                            p.inc(v.tensor_tensor(scr[1][:], scr[0][:], Bv, ALU.add), "dve")
                            p.fence(v, "dve")
                            tgt = dst_ap(dst, c, g) if gsimple else scr[2][:]
                            p.inc(v.tensor_scalar(tgt, scr[1][:],
                                                  gamma[:, gb_col0 + c:gb_col0 + c + 1],
                                                  beta[:, beta_col0 + c:beta_col0 + c + 1], ALU.mult, ALU.add), "dve")
                            if gsimple:
                                continue
                            r = scr[2]
                        else:
                            p.inc(v.tensor_tensor(scr[1][:], scr[0][:], Bv, ALU.add), "dve")
                            r = scr[1]
                        p.fence(v, "dve")
                        writes = []
                        if dst is not None:
                            writes.append(("copy", dst))
                        if dst2 is not None:
                            writes.append(("copy", dst2))
                        if acc_into is not None:
                            writes.append(("acc", acc_into))
                        assert writes
                        for wi, (kind, buf) in enumerate(writes):
                            if kind == "copy":
                                ins = v.tensor_copy(dst_ap(buf, c, g), r[:])
                            else:
                                ins = v.tensor_tensor(dst_ap(buf, c, g), dst_ap(buf, c, g), r[:], ALU.add)
                            if wi == len(writes) - 1:
                                p.inc(ins, "dve")



ENG2SEM = {"vector": "dve", "scalar": "act", "gpsimd": "pool", "tensor": "pe"}

def emit_gemm(p: Prog, w_oc_ap, och, kch, rhs_fn, epi, njobs=1,
              wdtype=F32, epi_eng="scalar", cast_eng=None, wcast=None, kcw=8,
              wraw_mult=1, nbuf=2):
    nc = p.nc
    kcw = min(kcw, kch)
    assert kch % kcw == 0
    nw = kch // kcw
    L = rhs_fn(0, 0).shape[-1]
    assert L <= 512 and njobs <= 2
    nocast = (wcast is None and wdtype == BF16)
    tile_elems = wraw_mult * kcw * 128
    tile_bytes = tile_elems * {F32: 4, BF16: 2, U8: 1}[wdtype]
    assert tile_bytes <= WRB, (tile_bytes, WRB)
    if cast_eng is None:
        cast_eng = "vector" if epi_eng != "vector" else "gpsimd"
    assert cast_eng != epi_eng
    episem = ENG2SEM[epi_eng]
    castsem = ENG2SEM[cast_eng]
    ntiles = och * nw
    p.wr_init()
    with ExitStack() as es:
        wbf = None if nocast else [p.sbuf(es, [128, kcw * 128], BF16) for _ in range(nbuf)]
        ps = [p.psum(es, [128, 512]) for _ in range(2)]
        base = dict(p.C)
        plan = []
        for l in range(ntiles):
            consume = (("pe", base["pe"] + l + 1) if nocast
                       else (castsem, base[castsem] + l + 1))
            plan.append(p.wr_plan(consume))

        def wview(l):
            i = plan[l][0]
            return p.wring[i][:].bitcast(wdtype)[:, 0:tile_elems]

        with p.block() as blk:
            @blk.sync
            def _(s):
                for l in range(ntiles):
                    i, sem, cnt, prev = plan[l]
                    if prev is not None:
                        s.wait_ge(p.S[prev[0]], prev[1])
                    s.dma_start(out=wview(l), in_=w_oc_ap(l // nw, l % nw)
                                ).then_inc(p.S[sem], 16)

            def cast_sec(q):
                copyf = q.copy if cast_eng == "scalar" else q.tensor_copy
                for l in range(ntiles):
                    i, sem, cnt, prev = plan[l]
                    q.wait_ge(p.S[sem], cnt)
                    if l >= nbuf:
                        q.wait_ge(p.S["pe"], base["pe"] + l - nbuf + 1)
                    if wcast is not None:
                        p.inc(wcast(q, wbf[l % nbuf][:], wview(l)), castsem)
                    else:
                        p.inc(copyf(wbf[l % nbuf][:], wview(l)), castsem)

            def epi_sec(q):
                for oc in range(och):
                    for j in range(njobs):
                        idx = oc * njobs + j
                        q.wait_ge(p.S["pe"], base["pe"] + (oc + 1) * nw)
                        p.inc(epi(q, oc, j, ps[idx % 2][:, 0:L]), episem)

            if not nocast:
                getattr(blk, cast_eng)(cast_sec)
            getattr(blk, epi_eng)(epi_sec)

            @blk.tensor
            def _(t):
                for oc in range(och):
                    for h in range(nw):
                        l = oc * nw + h
                        if nocast:
                            i, sem, cnt, prev = plan[l]
                            t.wait_ge(p.S[sem], cnt)
                            wt = wview(l)
                        else:
                            t.wait_ge(p.S[castsem], base[castsem] + l + 1)
                            wt = wbf[l % nbuf][:]
                        for j in range(njobs):
                            idx = oc * njobs + j
                            if h == 0 and idx >= 2:
                                t.wait_ge(p.S[episem], base[episem] + idx - 1)
                            for k in range(kcw):
                                kc = h * kcw + k
                                p_ins = t.matmul(ps[idx % 2][:, 0:L],
                                                 wt[:, k * 128:(k + 1) * 128],
                                                 rhs_fn(kc, j),
                                                 start=(kc == 0), stop=(kc == kch - 1))
                        p.inc(p_ins, "pe")


def emit_precast(p: Prog, pairs, obuf=None):
    nc = p.nc
    p.wr_init()
    split = []
    for sap, dap in pairs:
        C = sap.shape[-1]
        for o in range(0, C, 512):
            w = min(512, C - o)
            split.append((sap[:, o:o + w], dap[:, o:o + w]))
    pairs = split
    NOB = 3
    with ExitStack() as es:
        if obuf is not None:
            obf = obuf[:, 0:NOB * 256].bitcast(BF16)
            ob = [obf[:, i * 512:(i + 1) * 512] for i in range(NOB)]
        else:
            ob = [p.sbuf(es, [128, 512], BF16)[:] for _ in range(NOB)]
        oq = p.take_dq(NOB)
        base = dict(p.C)
        n = len(pairs)
        plan = [p.wr_plan(("dve", base["dve"] + l + 1)) for l in range(n)]
        ocnt = []
        for l in range(n):
            p.C[oq[l % NOB]] += 16
            ocnt.append(p.C[oq[l % NOB]])

        with p.block() as blk:
            @blk.sync
            def _(s):
                for l in range(n + NOB):
                    if l < n:
                        i, sem, cnt, prev = plan[l]
                        if prev is not None:
                            s.wait_ge(p.S[prev[0]], prev[1])
                        C = pairs[l][0].shape[-1]
                        s.dma_start(out=p.wring[i][:].bitcast(F32)[:, 0:C],
                                    in_=pairs[l][0]).then_inc(p.S[sem], 16)
                    if l >= NOB:
                        w = l - NOB
                        s.wait_ge(p.S["dve"], base["dve"] + w + 1)
                        C = pairs[w][0].shape[-1]
                        s.dma_start(out=pairs[w][1], in_=ob[w % NOB][:, 0:C]
                                    ).then_inc(p.S[oq[w % NOB]], 16)

            @blk.vector
            def _(v):
                for l in range(n):
                    i, sem, cnt, prev = plan[l]
                    v.wait_ge(p.S[sem], cnt)
                    if l >= NOB:
                        v.wait_ge(p.S[oq[l % NOB]], ocnt[l - NOB])
                    C = pairs[l][0].shape[-1]
                    p.inc(v.tensor_copy(ob[l % NOB][:, 0:C],
                                        p.wring[i][:].bitcast(F32)[:, 0:C]), "dve")



def emit_l3ffn(p: Prog, w1_ap, b1_ap, w2_ap, b2_ap):
    nc = p.nc
    w1b = nc.dram_tensor("l3fw1b", [32, 128, 1024], BF16).ap()
    w2b = nc.dram_tensor("l3fw2b", [8, 128, 4096], BF16).ap()
    with ExitStack() as es:
        b1s = p.sbuf(es, [128, 32], F32)
        b2s = p.sbuf(es, [128, 8], F32)
        Hseg = p.sbuf(es, [128, FCH * TSEG], BF16)
        Mseg = p.sbuf(es, [128, 32 * TSEG], BF16)
        Yseg = p.sbuf(es, [128, FCH * TSEG], F32)
        p.barrier(("sync",))
        with p.block() as blk:
            @blk.sync
            def _(s):
                p.inc(s.dma_start(out=b1s[:], in_=b1_ap), "dma", 16)
                p.inc(s.dma_start(out=b2s[:], in_=b2_ap), "dma", 16)
        pairs = [(w1_ap[oc, :, 0:1024], w1b[oc, :, 0:1024]) for oc in range(32)]
        pairs += [(w2_ap[oc, :, j * 1024:(j + 1) * 1024],
                   w2b[oc, :, j * 1024:(j + 1) * 1024])
                  for oc in range(FCH) for j in range(4)]
        emit_precast(p, pairs)
        p.barrier(("sync",))
        for g in range(NSEG):
            emit_ln(p, p.X, dst=Hseg, ntok=TSEG, src_tok0=g * TSEG,
                    dst_stride=TSEG, dst_tok0=0, scratch=Yseg)

            def epi1(a, oc, j, psv):
                return a.activation(Mseg[:, oc * TSEG:(oc + 1) * TSEG], psv,
                                    AF.Relu, bias=b1s[:, oc:oc + 1], scale=1.0)
            emit_gemm(p, lambda oc, h: w1b[oc, :, h * 1024:(h + 1) * 1024], 32, FCH,
                      lambda kc, j: Hseg[:, kc * TSEG:(kc + 1) * TSEG], epi1, wdtype=BF16)

            def epi2(a, oc, j, psv):
                return a.activation(Yseg[:, oc * TSEG:(oc + 1) * TSEG], psv,
                                    AF.Identity, bias=b2s[:, oc:oc + 1], scale=1.0)
            emit_gemm(p, lambda oc, h: w2b[oc, :, h * 1024:(h + 1) * 1024], FCH, 32,
                      lambda kc, j: Mseg[:, kc * TSEG:(kc + 1) * TSEG], epi2, wdtype=BF16)

            emit_ln(p, Yseg, acc_into=p.X, ntok=TSEG, src_stride=TSEG,
                    dst_stride=TOK, dst_tok0=g * TSEG,
                    scratch=Mseg[:].bitcast(F32))


def prep_l3ffn(ins):
    return {
        "L3f_w1": tile_lhsT(ins["L3_ffn_W1"].astype(np.float32)),
        "L3f_b1": bias_tiles(ins["L3_ffn_b1"]),
        "L3f_w2": tile_lhsT(ins["L3_ffn_W2"].astype(np.float32)),
        "L3f_b2": bias_tiles(ins["L3_ffn_b2"]),
    }


def decl_l3ffn(nc):
    return dict(
        w1_ap=nc.dram_tensor("L3f_w1", [32, 128, 1024], F32, kind="ExternalInput").ap(),
        b1_ap=nc.dram_tensor("L3f_b1", [128, 32], F32, kind="ExternalInput").ap(),
        w2_ap=nc.dram_tensor("L3f_w2", [8, 128, 4096], F32, kind="ExternalInput").ap(),
        b2_ap=nc.dram_tensor("L3f_b2", [128, 8], F32, kind="ExternalInput").ap(),
    )


def emit_l2ffn(p: Prog, k0_ap, k1_ap, k2_ap, s3_ap):
    nc = p.nc
    SEG = 256
    NSG = TOK // SEG
    with ExitStack() as es:
        sbc = p.sbuf(es, [128, 3], F32)
        s3t = p.sbuf(es, [1, 3], F32)
        psb = p.psum(es, [128, 512])
        Hseg = p.sbuf(es, [128, FCH * SEG], BF16)
        A1 = p.sbuf(es, [128, 32 * SEG], BF16)
        A2 = p.sbuf(es, [128, 32 * SEG], BF16)
        Yseg = p.sbuf(es, [128, FCH * SEG], F32)
        base = dict(p.C)
        p.barrier(("sync",))
        with p.block() as blk:
            @blk.sync
            def _(s):
                p.inc(s.dma_start(out=s3t[:], in_=s3_ap), "dma", 16)
            @blk.tensor
            def _(t):
                t.wait_ge(p.S["dma"], p.C["dma"])
                p.inc(t.matmul(psb[:, 0:3], p.rowsel[0:1, 0:128], s3t[:], start=True, stop=True), "pe")
            @blk.vector
            def _(v):
                v.wait_ge(p.S["pe"], p.C["pe"])
                p.inc(v.tensor_scalar(sbc[:, 0:1], psb[:, 0:1], 4.0, None, ALU.mult), "dve")
                p.inc(v.tensor_scalar(sbc[:, 1:2], psb[:, 1:2], 2.0, None, ALU.mult), "dve")
                p.inc(v.tensor_scalar(sbc[:, 2:3], psb[:, 2:3], 1.0, None, ALU.mult), "dve")
        p.barrier()

        def subcast(q, dst, src):
            n = dst.shape[-1]
            return q.tensor_tensor(dst, src[:, 0:n], src[:, n:2 * n], ALU.subtract)

        def affcast(q, dst, src):
            return q.tensor_scalar(dst, src, 2.0, -1.0, ALU.mult, ALU.add)

        for g in range(NSG):
            emit_ln(p, p.X, dst=Hseg, ntok=SEG, src_tok0=g * SEG,
                    dst_stride=SEG, dst_tok0=0)

            def epi0(v, oc, j, psv):
                return v.tensor_scalar(A1[:, oc * SEG:(oc + 1) * SEG], psv,
                                       sbc[:, 0:1], 0.0, ALU.mult, ALU.max)
            emit_gemm(p, lambda oc, h: k0_ap[oc, :, :], 32, FCH,
                      lambda kc, j: Hseg[:, kc * SEG:(kc + 1) * SEG], epi0,
                      wdtype=U8, epi_eng="vector", cast_eng="gpsimd",
                      wcast=subcast, kcw=FCH, wraw_mult=2)

            def epi1(v, oc, j, psv):
                return v.tensor_scalar(A2[:, oc * SEG:(oc + 1) * SEG], psv,
                                       sbc[:, 1:2], 0.0, ALU.mult, ALU.max)
            emit_gemm(p, lambda oc, h: k1_ap[oc, :, h * 1024:(h + 1) * 1024], 32, 32,
                      lambda kc, j: A1[:, kc * SEG:(kc + 1) * SEG], epi1,
                      wdtype=U8, epi_eng="vector", cast_eng="gpsimd", wcast=affcast)

            def epi2(v, oc, j, psv):
                return v.tensor_scalar(Yseg[:, oc * SEG:(oc + 1) * SEG], psv,
                                       sbc[:, 2:3], None, ALU.mult)
            emit_gemm(p, lambda oc, h: k2_ap[oc, :, h * 1024:(h + 1) * 1024], FCH, 32,
                      lambda kc, j: A2[:, kc * SEG:(kc + 1) * SEG], epi2,
                      wdtype=U8, epi_eng="vector", cast_eng="gpsimd", wcast=affcast)

            emit_ln(p, Yseg, acc_into=p.X, ntok=SEG, src_stride=SEG,
                    dst_stride=TOK, dst_tok0=g * SEG)


def prep_l2ffn(ins):
    k0 = np.asarray(ins["L2_ffn_k0"]).astype(np.uint8)
    k1 = np.asarray(ins["L2_ffn_k1"]).astype(np.uint8)
    k2 = np.asarray(ins["L2_ffn_k2"]).astype(np.uint8)
    k0b = np.concatenate([tile_lhsT(k0[:1024]), tile_lhsT(k0[1024:])], axis=2)
    s3 = np.array([[ins["L2_ffn_s0"], ins["L2_ffn_s1"], ins["L2_ffn_s2"]]], dtype=np.float32)
    return {
        "L2f_k0": np.ascontiguousarray(k0b),
        "L2f_k1": tile_lhsT(k1),
        "L2f_k2": tile_lhsT(k2),
        "L2f_s3": s3,
    }


def decl_l2ffn(nc):
    return dict(
        k0_ap=nc.dram_tensor("L2f_k0", [32, 128, 2048], U8, kind="ExternalInput").ap(),
        k1_ap=nc.dram_tensor("L2f_k1", [32, 128, 4096], U8, kind="ExternalInput").ap(),
        k2_ap=nc.dram_tensor("L2f_k2", [8, 128, 4096], U8, kind="ExternalInput").ap(),
        s3_ap=nc.dram_tensor("L2f_s3", [1, 3], F32, kind="ExternalInput").ap(),
    )


from concourse.ap import AP as BassAP


def emit_l3attn(p: Prog, text_ap, wqkv_ap, gb_ap):
    nc = p.nc
    temp = nc.dram_tensor("l3bias", [16, 16, 16, 128, 128], BF16).ap()

    with ExitStack() as es:
        qraw = [p.sbuf(es, [128, 2048], F32) for _ in range(2)]
        qbf = [p.sbuf(es, [128, 2048], BF16) for _ in range(2)]
        pst = [p.psum(es, [128, 128], BF16) for _ in range(2)]
        stage = [p.sbuf(es, [128, 2048], BF16) for _ in range(2)]
        lq = p.take_dq(2)
        sq = p.take_dq(2, sw=True)
        base = dict(p.C)
        NIT = 256

        def qm_src(h, it, phi):
            baseoff = -2048 * h + 32752 - 128 * it - 16 * phi
            return BassAP(text_ap.tensor, baseoff, [[2048, 16], [1, 2048]])

        p.barrier(("sync",))
        with p.block() as blk:
            @blk.sync
            def _(s):
                for i in range(NIT):
                    h, it = i // 16, i % 16
                    if i >= 2:
                        s.wait_ge(p.S["dve"], base["dve"] + (i - 1) * 17 + 1)
                    for phi in range(8):
                        p.inc(s.dma_start(out=qraw[i % 2][phi * 16:(phi + 1) * 16, :],
                                          in_=qm_src(h, it, phi)), lq[i % 2], 16)
            @blk.vector
            def _(v):
                for i in range(NIT):
                    v.wait_ge(p.S[lq[i % 2]], base[lq[i % 2]] + 128 * (i // 2 + 1))
                    if i >= 2:
                        v.wait_ge(p.S["pe"], base["pe"] + 16 * (i - 1))
                    p.inc(v.tensor_copy(qbf[i % 2][:], qraw[i % 2][:]), "dve")
                    for j in range(16):
                        v.wait_ge(p.S["pe"], base["pe"] + 16 * i + j + 1)
                        if i >= 2 and j == 0:
                            v.wait_ge(p.S[sq[i % 2]], base[sq[i % 2]] + 16 * (i // 2))
                        p.inc(v.tensor_copy(stage[i % 2][:, j * 128:(j + 1) * 128], pst[j % 2][:]), "dve")
            @blk.tensor
            def _(t):
                for i in range(NIT):
                    t.wait_ge(p.S["dve"], base["dve"] + i * 17 + 1)
                    for j in range(16):
                        if j >= 2:
                            t.wait_ge(p.S["dve"], base["dve"] + i * 17 + 1 + j - 1)
                        elif i >= 1:
                            t.wait_ge(p.S["dve"], base["dve"] + i * 17)
                        p.inc(t.transpose(pst[j % 2][:], qbf[i % 2][:, j * 128:(j + 1) * 128], p.eye[:]), "pe")
            @blk.gpsimd
            def _(g):
                for i in range(NIT):
                    h, it = i // 16, i % 16
                    g.wait_ge(p.S["dve"], base["dve"] + (i + 1) * 17)
                    p.inc(g.dma_start(out=temp[h, it].rearrange("jt p i -> p jt i"), in_=stage[i % 2][:].rearrange("p (jt i) -> p jt i", jt=16)),
                          sq[i % 2], 16)
        p.barrier()

    ao_dram = nc.dram_tensor("l3ao", [8, 128, 2048], BF16).ap()
    with ExitStack() as es:
        gb = p.sbuf(es, [128, 16], F32)
        Hseq = p.sbuf(es, [128, FCH * 2048], BF16)
        QK = p.sbuf(es, [64, 2 * 2048], BF16)
        Vst = p.sbuf(es, [64, 512], BF16)
        Vkm = p.sbuf(es, [128, 16 * 65], BF16)
        Ptile = [p.sbuf(es, [128, 512], BF16) for _ in range(1)]
        btile = [p.sbuf(es, [128, 512], BF16) for _ in range(1)]
        rz = p.sbuf(es, [1, 512], F32)
        oT = p.sbuf(es, [64, 512], BF16)
        AOst = p.sbuf(es, [64, 512], BF16)
        SEG2 = 128
        AOseg = p.sbuf(es, [128, FCH * SEG2], BF16)
        Yseg = p.sbuf(es, [128, FCH * SEG2], BF16)
        wq_s = p.take_dq(2)
        bq_s = p.take_dq(2)
        aoq = p.take_dq(1, sw=True)[0]
        wqkvb = nc.dram_tensor("l3wqkvb", [16, 3, 128, 512], BF16).ap()
        p.barrier(("sync",))
        with p.block() as blk:
            @blk.sync
            def _(s):
                p.inc(s.dma_start(out=gb[:], in_=gb_ap), "dma", 16)
        emit_precast(p, [(wqkv_ap[h_, t3_], wqkvb[h_, t3_])
                         for h_ in range(16) for t3_ in range(3)],
                     obuf=Hseq[:].bitcast(F32))
        p.barrier()

        for seq in range(SEQ):
            emit_ln(p, p.X, dst=Hseq, ntok=2048, src_tok0=seq * 2048, dst_stride=2048, dst_tok0=0)
            for h in range(16):
                with ExitStack() as es2:
                    ps_kv = [p.psum(es2, [64, 512]) for _ in range(2)]
                    ps_vt = [p.psum(es2, [128, 64], BF16) for _ in range(2)]
                    base2 = dict(p.C)
                    wcons = [("pe", base2["pe"] + 4), ("pe", base2["pe"] + 8),
                             ("pe", base2["pe"] + 28)]
                    wplan = [p.wr_plan(wcons[t3]) for t3 in range(3)]

                    def wv(t3):
                        return p.wring[wplan[t3][0]][:].bitcast(BF16)[:, 0:512]

                    with p.block() as blk:
                        @blk.sync
                        def _(s):
                            for t3 in range(3):
                                i, sem, cnt, prev = wplan[t3]
                                if prev is not None:
                                    s.wait_ge(p.S[prev[0]], prev[1])
                                s.dma_start(out=wv(t3), in_=wqkvb[h, t3]
                                            ).then_inc(p.S[sem], 16)
                        @blk.tensor
                        def _(t):
                            t.wait_ge(p.S["dve"], base2["dve"])
                            for i in range(12):
                                t3, sg = i // 4, i % 4
                                if sg == 0:
                                    t.wait_ge(p.S[wplan[t3][1]], wplan[t3][2])
                                if i >= 2:
                                    t.wait_ge(p.S["act"], base2["act"] + i - 1)
                                for kc in range(FCH):
                                    ins = t.matmul(ps_kv[i % 2][:],
                                                   wv(t3)[:, kc * 64:(kc + 1) * 64],
                                                   Hseq[:, kc * 2048 + sg * 512:kc * 2048 + (sg + 1) * 512],
                                                   start=(kc == 0), stop=(kc == FCH - 1))
                                p.inc(ins, "pe")
                                if t3 == 2:
                                    for j in range(4):
                                        kt = sg * 4 + j
                                        t.wait_ge(p.S["act"], base2["act"] + i + 1)
                                        if kt >= 2:
                                            t.wait_ge(p.S["dve"], base2["dve"] + kt)
                                        p.inc(t.transpose(ps_vt[kt % 2][:], Vst[:, j * 128:(j + 1) * 128], p.eye[0:64, 0:64]), "pe")
                        @blk.scalar
                        def _(a):
                            for i in range(12):
                                t3, sg = i // 4, i % 4
                                npe = i + 1 + max(0, (i - 8)) * 4
                                a.wait_ge(p.S["pe"], base2["pe"] + npe)
                                dst = (Vst[:, 0:512] if t3 == 2 else
                                       QK[0:64, t3 * 2048 + sg * 512:t3 * 2048 + (sg + 1) * 512])
                                p.inc(a.activation(dst, ps_kv[i % 2][:], AF.Identity, scale=1.0), "act")
                        @blk.vector
                        def _(v):
                            p.fence(v, "dve")
                            p.inc(v.memset(Vkm[:].rearrange("p (kt c) -> p kt c", c=65)[:, :, 64:65], 1.0), "dve")
                            for kt in range(16):
                                sg = kt // 4
                                i = 8 + sg
                                npe = i + 1 + sg * 4 + (kt % 4) + 1
                                v.wait_ge(p.S["pe"], base2["pe"] + npe)
                                p.inc(v.tensor_copy(Vkm[:, kt * 65:kt * 65 + 64], ps_vt[kt % 2][:]), "dve")
                    p.barrier(("tensor", "vector", "scalar"))
                with ExitStack() as es2:
                    ps_s = [p.psum(es2, [128, 512]) for _ in range(2)]
                    ps_o = p.psum(es2, [128, 512])
                    ps_o2 = p.psum(es2, [128, 512])
                    ps_z = p.psum(es2, [64, 512])
                    for qs in range(4):
                        base3 = dict(p.C)
                        with p.block() as blk:
                            @blk.sync
                            def _(s):
                                for kt in range(16):
                                    s.wait_ge(p.S["pe"], base3["pe"] + 3 * kt)
                                    p.inc(s.dma_start(out=btile[0][:].rearrange("p (itt i) -> p itt i", itt=4),
                                                      in_=BassAP(temp.tensor,
                                                                 ((h * 16 + 4 * qs) * 16 + kt) * 16384,
                                                                 [[128, 128], [16384 * 16, 4], [1, 128]])), bq_s[0], 16)
                            @blk.tensor
                            def _(t):
                                for kt in range(16):
                                    if kt >= 2:
                                        t.wait_ge(p.S["act"], base3["act"] + kt - 1)
                                    ins = t.matmul(ps_s[kt % 2][:],
                                                   QK[0:64, 1 * 2048 + kt * 128:1 * 2048 + (kt + 1) * 128],
                                                   QK[0:64, 0 * 2048 + qs * 512:0 * 2048 + (qs + 1) * 512],
                                                   start=True, stop=True)
                                    p.inc(ins, "pe")
                                    t.wait_ge(p.S["act"], base3["act"] + kt + 1)
                                    p.inc(t.matmul(ps_o[0:65, :], Vkm[:, kt * 65:(kt + 1) * 65], Ptile[0][:],
                                                   start=(kt == 0), stop=(kt == 15)), "pe")
                                    t.wait_ge(p.S[bq_s[0]], base3[bq_s[0]] + 16 * (kt + 1))
                                    p.inc(t.matmul(ps_o2[0:64, :], Vkm[:, kt * 65:kt * 65 + 64], btile[0][:],
                                                   start=(kt == 0), stop=(kt == 15)), "pe")
                            @blk.scalar
                            def _(a):
                                for kt in range(16):
                                    a.wait_ge(p.S["pe"], base3["pe"] + 3 * kt + 1)
                                    p.inc(a.activation(Ptile[0][:], ps_s[kt % 2][:], AF.Exp, scale=0.03125), "act")
                        base4 = dict(p.C)
                        with p.block() as blk:
                            @blk.vector
                            def _(v):
                                v.wait_ge(p.S["pe"], base4["pe"])
                                p.inc(v.reciprocal(rz[:], ps_o[64:65, :]), "dve")
                            @blk.tensor
                            def _(t):
                                t.wait_ge(p.S["dve"], p.C["dve"])
                                p.inc(t.matmul(ps_z[:], p.rowsel[0:1, 0:64], rz[:], start=True, stop=True), "pe")
                            @blk.scalar
                            def _(a):
                                a.wait_ge(p.S["pe"], base4["pe"])
                                p.inc(a.activation(oT[:], ps_o[0:64, :], AF.Identity, scale=1.0), "act")
                        with p.block() as blk:
                            @blk.vector
                            def _(v):
                                v.wait_ge(p.S["pe"], p.C["pe"])
                                v.wait_ge(p.S["act"], p.C["act"])
                                p.inc(v.tensor_tensor(oT[:], oT[:], ps_z[:], ALU.mult), "dve")
                                p.fence(v, "dve")
                                p.inc(v.tensor_tensor(AOst[:], oT[:], ps_o2[0:64, :], ALU.add), "dve")
                            @blk.gpsimd
                            def _(g):
                                g.wait_ge(p.S["dve"], p.C["dve"])
                                dst = BassAP(ao_dram.tensor,
                                             (h // 2) * 128 * 2048 + (h % 2) * 64 * 2048 + qs * 512,
                                             [[2048, 64], [1, 512]])
                                p.inc(g.dma_start(out=dst, in_=AOst[:]), aoq, 16)
                        p.barrier(("tensor", "vector", "scalar", "sync", "gpsimd"))
            for sg in range(2048 // SEG2):
                base5 = dict(p.C)
                p.barrier(("sync",))
                with p.block() as blk:
                    @blk.sync
                    def _(s):
                        p.inc(s.dma_start(out=AOseg[:].rearrange("p (c t) -> p c t", c=FCH),
                                          in_=ao_dram[:, :, sg * SEG2:(sg + 1) * SEG2].rearrange("c p t -> p c t")),
                              "dma", 16)
                    @blk.vector
                    def _(v):
                        v.wait_ge(p.S["dma"], p.C["dma"])
                emit_ln(p, AOseg, dst=Yseg, ntok=SEG2, src_stride=SEG2,
                        dst_stride=SEG2, dst_tok0=0, eps=1e-3, gamma=gb, beta=gb,
                        gb_col0=0, beta_col0=8)
                emit_ln(p, Yseg, acc_into=p.X, ntok=SEG2, src_stride=SEG2,
                        dst_stride=TOK, dst_tok0=seq * 2048 + sg * SEG2)


def prep_l3attn(ins):
    T = np.asarray(ins["L3_attn_rel_table"]).astype(np.float32)
    T_ext = np.concatenate([T, np.repeat(T[2046:2047], 4096 - 2047, axis=0)], axis=0)
    def headtiles(W):
        return np.ascontiguousarray(
            np.asarray(W).astype(np.float32).reshape(8, 128, 16, 64).transpose(2, 1, 0, 3).reshape(16, 128, 512))
    wq = headtiles(ins["L3_attn_Wq"])
    wk = headtiles(ins["L3_attn_Wk"])
    wv = headtiles(ins["L3_attn_Wv"])
    wqkv = np.ascontiguousarray(np.stack([wq, wk, wv], axis=1))
    gb = np.concatenate([bias_tiles(ins["L3_attn_gamma"]), bias_tiles(ins["L3_attn_beta"])], axis=1)
    return {
        "L3a_text": np.ascontiguousarray(T_ext.reshape(-1)),
        "L3a_wqkv": wqkv,
        "L3a_gb": np.ascontiguousarray(gb),
    }


def decl_l3attn(nc):
    return dict(
        text_ap=nc.dram_tensor("L3a_text", [4096 * 16], F32, kind="ExternalInput").ap(),
        wqkv_ap=nc.dram_tensor("L3a_wqkv", [16, 3, 128, 512], F32, kind="ExternalInput").ap(),
        gb_ap=nc.dram_tensor("L3a_gb", [128, 16], F32, kind="ExternalInput").ap(),
    )


def emit_l0attn(p: Prog, wqkv_ap, wo_ap):
    nc = p.nc
    ao_dram = nc.dram_tensor("l0ao", [8, 128, 2048], BF16).ap()
    wqkvb = nc.dram_tensor("l0wqkvb", [16, 3, 128, 512], BF16).ap()
    wob = nc.dram_tensor("l0wob", [8, 128, 1024], BF16).ap()
    with ExitStack() as es:
        Hseq = p.sbuf(es, [128, FCH * 2048], BF16)
        Qaug = p.sbuf(es, [66, 2048], BF16)
        Kaug = p.sbuf(es, [66, 2048], BF16)
        qnrow = p.sbuf(es, [1, 2048], BF16)
        arow = p.sbuf(es, [1, 2048], BF16)
        Vkm = p.sbuf(es, [128, 16 * 65], BF16)
        cbuf = p.sbuf(es, [128, 3840], F32)
        brow = cbuf[0:1, 3584:3840].bitcast(BF16)
        Vst = cbuf[0:64, 0:256].bitcast(BF16)
        sq2 = [cbuf[0:64, 256:512].bitcast(BF16)]
        c0 = cbuf[:, 0:1024]
        c1 = cbuf[:, 1024:2048]
        c2 = cbuf[:, 2048:3072]
        Ptile = cbuf[:, 3072:3584].bitcast(BF16)
        oT = Ptile[0:64, 512:1024]
        AOst = Ptile[64:128, 512:1024]
        SEG2 = 256
        AOseg = cbuf[:, 2048:3072].bitcast(BF16)
        Yseg = cbuf[:, 0:2048]
        wq_s = p.take_dq(2)
        qnq = p.take_dq(1)[0]
        aoq = p.take_dq(1, sw=True)[0]

        p.barrier(("sync",))
        emit_precast(p, [(wqkv_ap[h_, t3_], wqkvb[h_, t3_])
                         for h_ in range(16) for t3_ in range(3)]
                     + [(wo_ap[oc, :, :], wob[oc, :, :]) for oc in range(FCH)],
                     obuf=Hseq[:].bitcast(F32))
        p.barrier(("sync",))
        for seq in range(SEQ):
            emit_ln(p, p.X, dst=Hseq, ntok=2048, src_tok0=seq * 2048, dst_stride=2048, dst_tok0=0, scratch=cbuf[:, 0:3072])
            for h in range(16):
                basev = dict(p.C)
                wplan = [p.wr_plan(("pe", basev["pe"] + 20 + 4)),
                         p.wr_plan(("pe", basev["pe"] + 20 + 8)),
                         p.wr_plan(("pe", basev["pe"] + 12))]

                def wv(t3):
                    return p.wring[wplan[t3][0]][:].bitcast(BF16)[:, 0:512]

                with p.block() as blk:
                    @blk.sync
                    def _(s):
                        for t3 in range(3):
                            i, sem, cnt, prev = wplan[t3]
                            if prev is not None:
                                s.wait_ge(p.S[prev[0]], prev[1])
                            s.dma_start(out=wv(t3), in_=wqkvb[h, t3]
                                        ).then_inc(p.S[sem], 16)
                with ExitStack() as es2:
                    ps_kv = [p.psum(es2, [64, 512]) for _ in range(2)]
                    ps_vt = [p.psum(es2, [128, 64], BF16) for _ in range(2)]
                    ps_n = p.psum(es2, [1, 2048])
                    base2 = dict(p.C)

                    def pe_T(b):
                        return [6, 11, 16, 20][b]

                    def pe_proj(sg):
                        return [1, 2, 7, 12][sg]

                    with p.block() as blk:
                        @blk.tensor
                        def _(t):
                            t.wait_ge(p.S["dve"], base2["dve"])
                            t.wait_ge(p.S[wplan[2][1]], wplan[2][2])
                            for sg in range(4):
                                if sg >= 2:
                                    t.wait_ge(p.S["act"], base2["act"] + sg - 1)
                                for kc in range(FCH):
                                    ins = t.matmul(ps_kv[sg % 2][:],
                                                   wv(2)[:, kc * 64:(kc + 1) * 64],
                                                   Hseq[:, kc * 2048 + sg * 512:kc * 2048 + (sg + 1) * 512],
                                                   start=(kc == 0), stop=(kc == FCH - 1))
                                p.inc(ins, "pe")
                                if sg >= 1:
                                    b = sg - 1
                                    t.wait_ge(p.S["act"], base2["act"] + b + 1)
                                    for j in range(4):
                                        kt = b * 4 + j
                                        if kt >= 2:
                                            t.wait_ge(p.S["dve"], base2["dve"] + kt)
                                        p.inc(t.transpose(ps_vt[kt % 2][:], Vst[:, j * 128:(j + 1) * 128], p.eye[0:64, 0:64]), "pe")
                            t.wait_ge(p.S["act"], base2["act"] + 4)
                            for j in range(4):
                                kt = 12 + j
                                t.wait_ge(p.S["dve"], base2["dve"] + kt)
                                p.inc(t.transpose(ps_vt[kt % 2][:], Vst[:, j * 128:(j + 1) * 128], p.eye[0:64, 0:64]), "pe")
                        @blk.scalar
                        def _(a):
                            for sg in range(4):
                                a.wait_ge(p.S["pe"], base2["pe"] + pe_proj(sg))
                                if sg >= 1:
                                    a.wait_ge(p.S["pe"], base2["pe"] + pe_T(sg - 1))
                                p.inc(a.activation(Vst[:], ps_kv[sg % 2][:], AF.Identity, scale=1.0), "act")
                        @blk.vector
                        def _(v):
                            p.fence(v, "dve")
                            p.inc(v.memset(Vkm[:].rearrange("p (kt c) -> p kt c", c=65)[:, :, 64:65], 1.0), "dve")
                            for kt in range(16):
                                b, j = kt // 4, kt % 4
                                pos = (3 + b * 5 + j) if b < 3 else (17 + j)
                                v.wait_ge(p.S["pe"], base2["pe"] + pos)
                                p.inc(v.tensor_copy(Vkm[:, kt * 65:kt * 65 + 64], ps_vt[kt % 2][:]), "dve")
                    p.barrier(("tensor", "vector", "scalar"))
                    base2 = dict(p.C)
                    with p.block() as blk:
                        @blk.tensor
                        def _(t):
                            for i in range(8):
                                t3, sg = i // 4, i % 4
                                if sg == 0:
                                    t.wait_ge(p.S[wplan[t3][1]], wplan[t3][2])
                                if i >= 2:
                                    t.wait_ge(p.S["act"], base2["act"] + i - 1)
                                for kc in range(FCH):
                                    ins = t.matmul(ps_kv[i % 2][:],
                                                   wv(t3)[:, kc * 64:(kc + 1) * 64],
                                                   Hseq[:, kc * 2048 + sg * 512:kc * 2048 + (sg + 1) * 512],
                                                   start=(kc == 0), stop=(kc == FCH - 1))
                                p.inc(ins, "pe")
                        @blk.scalar
                        def _(a):
                            for i in range(8):
                                t3, sg = i // 4, i % 4
                                a.wait_ge(p.S["pe"], base2["pe"] + i + 1)
                                dst = (Qaug if t3 == 0 else Kaug)[0:64, sg * 512:(sg + 1) * 512]
                                p.inc(a.activation(dst, ps_kv[i % 2][:], AF.Identity,
                                                   scale=(-2.0 if t3 == 0 else 1.0)), "act")
                    p.barrier(("tensor", "scalar"))
                    base2 = dict(p.C)
                    with p.block() as blk:
                        @blk.vector
                        def _(v):
                            p.fence(v, "dve")
                            p.inc(v.memset(Qaug[64:66, :], 1.0), "dve")
                            p.inc(v.memset(Kaug[64:66, :], 1.0), "dve")
                            for i in range(8):
                                t3, sg = i // 4, i % 4
                                if i >= 1:
                                    v.wait_ge(p.S["pe"], base2["pe"] + i)
                                src = (Qaug if t3 == 0 else Kaug)[0:64, sg * 512:(sg + 1) * 512]
                                p.inc(v.tensor_tensor(sq2[0][:], src, src, ALU.mult), "dve")
                        @blk.tensor
                        def _(t):
                            for i in range(8):
                                t3, sg = i // 4, i % 4
                                t.wait_ge(p.S["dve"], base2["dve"] + 2 + i + 1)
                                if t3 == 1:
                                    t.wait_ge(p.S["act"], base2["act"] + sg + 1)
                                p.inc(t.matmul(ps_n[0:1, sg * 512:(sg + 1) * 512],
                                               p.sel[0:64, 0:1], sq2[0][:],
                                               start=True, stop=True), "pe")
                        @blk.scalar
                        def _(a):
                            for i in range(8):
                                t3, sg = i // 4, i % 4
                                a.wait_ge(p.S["pe"], base2["pe"] + i + 1)
                                if t3 == 0:
                                    p.inc(a.activation(qnrow[0:1, sg * 512:(sg + 1) * 512],
                                                       ps_n[0:1, sg * 512:(sg + 1) * 512],
                                                       AF.Identity, scale=0.25), "act")
                                else:
                                    a.wait_ge(p.S["dve"], base2["dve"] + 1)
                                    p.inc(a.activation(Kaug[64:65, sg * 512:(sg + 1) * 512],
                                                       ps_n[0:1, sg * 512:(sg + 1) * 512],
                                                       AF.Identity, scale=1.0), "act")
                    with p.block() as blk:
                        @blk.sync
                        def _(s):
                            s.wait_ge(p.S["act"], p.C["act"])
                            s.wait_ge(p.S["dve"], p.C["dve"])
                            s.wait_ge(p.S["pe"], p.C["pe"])
                            p.inc(s.dma_start(out=Qaug[65:66, :], in_=qnrow[:]), qnq, 16)
                        @blk.vector
                        def _(v):
                            p.fence(v, "dve")
                            v.wait_ge(p.S["act"], p.C["act"])
                            p.inc(v.tensor_scalar(arow[:], Kaug[64:65, :], -1.0, 1.0, ALU.mult, ALU.add), "dve")
                    p.barrier()
                with ExitStack() as es2:
                    ps_s = [p.psum(es2, [128, 1024]) for _ in range(1)]
                    ps_d = [p.psum(es2, [128, 1024]) for _ in range(1)]
                    ps_o = p.psum(es2, [128, 512])
                    ps_z = p.psum(es2, [64, 512])
                    for qs in range(4):
                        with p.block() as blk:
                            @blk.vector
                            def _(v):
                                p.fence(v, "dve")
                                p.inc(v.tensor_scalar(brow[:], qnrow[0:1, qs * 512:(qs + 1) * 512], -1.0, 1.0, ALU.mult, ALU.add), "dve")
                        base3 = dict(p.C)
                        with p.block() as blk:
                            @blk.tensor
                            def _(t):
                                for kp in range(8):
                                    if kp >= 1:
                                        t.wait_ge(p.S["act"], base3["act"] + 2 * kp - 1)
                                        t.wait_ge(p.S["dve"], base3["dve"] + (kp - 1) * 6 + 1)
                                    for half in range(2):
                                        kt = kp * 2 + half
                                        t.matmul(ps_s[0][:, half * 512:(half + 1) * 512],
                                                 Kaug[:, kt * 128:(kt + 1) * 128],
                                                 Qaug[:, qs * 512:(qs + 1) * 512],
                                                 start=True, stop=True)
                                        p.inc(t.matmul(ps_d[0][:, half * 512:(half + 1) * 512],
                                                       arow[:, kt * 128:(kt + 1) * 128],
                                                       brow[:, 0:512],
                                                       start=True, stop=True), "pe")
                                    t.wait_ge(p.S["dve"], base3["dve"] + (kp + 1) * 6)
                                    for half in range(2):
                                        kt = kp * 2 + half
                                        p.inc(t.matmul(ps_o[0:65, :], Vkm[:, kt * 65:(kt + 1) * 65],
                                                       Ptile[:, half * 512:(half + 1) * 512],
                                                       start=(kt == 0), stop=(kt == 15)), "pe")
                            @blk.scalar
                            def _(a):
                                for kp in range(8):
                                    a.wait_ge(p.S["pe"], base3["pe"] + kp * 4 + 2)
                                    if kp >= 1:
                                        a.wait_ge(p.S["dve"], base3["dve"] + (kp - 1) * 6 + 5)
                                    p.inc(a.activation(c0[:], ps_s[0][:], AF.Relu, scale=1.0), "act")
                                    a.wait_ge(p.S["dve"], base3["dve"] + kp * 6 + 4)
                                    p.inc(a.activation(c0[:], c2[:], AF.Sqrt, scale=4.0), "act")
                            @blk.vector
                            def _(v):
                                for kp in range(8):
                                    v.wait_ge(p.S["act"], base3["act"] + kp * 2 + 1)
                                    p.inc(v.tensor_scalar(c1[:], ps_d[0][:], 1e-6, None, ALU.max), "dve")
                                    p.fence(v, "dve")
                                    p.inc(v.reciprocal(c2[:], c1[:]), "dve")
                                    p.fence(v, "dve")
                                    p.inc(v.tensor_tensor(c1[:], c0[:], c2[:], ALU.mult), "dve")
                                    p.fence(v, "dve")
                                    p.inc(v.scalar_tensor_tensor(c2[:], c1[:], 1.0, c1[:], ALU.add, ALU.mult), "dve")
                                    v.wait_ge(p.S["act"], base3["act"] + kp * 2 + 2)
                                    p.inc(v.scalar_tensor_tensor(c2[:], c1[:], 2.0, c0[:], ALU.mult, ALU.subtract), "dve")
                                    p.fence(v, "dve")
                                    if kp >= 1:
                                        v.wait_ge(p.S["pe"], base3["pe"] + kp * 4)
                                    p.inc(v.tensor_scalar(Ptile[:], c2[:], 1.0, None, ALU.add), "dve")
                        base4 = dict(p.C)
                        with p.block() as blk:
                            @blk.vector
                            def _(v):
                                v.wait_ge(p.S["pe"], base4["pe"])
                                p.inc(v.reciprocal(cbuf[0:1, 2048:2560], ps_o[64:65, :]), "dve")
                            @blk.tensor
                            def _(t):
                                t.wait_ge(p.S["dve"], p.C["dve"])
                                p.inc(t.matmul(ps_z[:], p.rowsel[0:1, 0:64], cbuf[0:1, 2048:2560], start=True, stop=True), "pe")
                            @blk.scalar
                            def _(a):
                                a.wait_ge(p.S["pe"], base4["pe"])
                                p.inc(a.activation(oT[:], ps_o[0:64, :], AF.Identity, scale=1.0), "act")
                        with p.block() as blk:
                            @blk.vector
                            def _(v):
                                v.wait_ge(p.S["pe"], p.C["pe"])
                                v.wait_ge(p.S["act"], p.C["act"])
                                p.inc(v.tensor_tensor(AOst[:], oT[:], ps_z[:], ALU.mult), "dve")
                            @blk.gpsimd
                            def _(g):
                                g.wait_ge(p.S["dve"], p.C["dve"])
                                dst = BassAP(ao_dram.tensor,
                                             (h // 2) * 128 * 2048 + (h % 2) * 64 * 2048 + qs * 512,
                                             [[2048, 64], [1, 512]])
                                p.inc(g.dma_start(out=dst, in_=AOst[:]), aoq, 16)
                        p.barrier(("tensor", "vector", "scalar", "sync", "gpsimd"))
            for sg in range(2048 // SEG2):
                p.barrier(("sync",))
                with p.block() as blk:
                    @blk.sync
                    def _(s):
                        p.inc(s.dma_start(out=AOseg[:].rearrange("p (c t) -> p c t", c=FCH),
                                          in_=ao_dram[:, :, sg * SEG2:(sg + 1) * SEG2].rearrange("c p t -> p c t")),
                              "dma", 16)
                    @blk.vector
                    def _(v):
                        v.wait_ge(p.S["dma"], p.C["dma"])

                def epi(a, oc, j, psv):
                    return a.activation(Yseg[:, oc * SEG2:(oc + 1) * SEG2], psv,
                                        AF.Identity, scale=1.0)
                emit_gemm(p, lambda oc, hh: wob[oc, :, hh * 512:(hh + 1) * 512], FCH, FCH,
                          lambda kc, j: AOseg[:, kc * SEG2:(kc + 1) * SEG2], epi,
                          wdtype=BF16, kcw=4, nbuf=1)
                emit_ln(p, Yseg, acc_into=p.X, ntok=SEG2, src_stride=SEG2,
                        dst_stride=TOK, dst_tok0=seq * 2048 + sg * SEG2,
                        scratch=cbuf[:, 2048:3840])


def prep_l0attn(ins):
    def headtiles_T(W):
        Wt = np.ascontiguousarray(np.asarray(W).astype(np.float32).T)
        return np.ascontiguousarray(Wt.reshape(8, 128, 16, 64).transpose(2, 1, 0, 3).reshape(16, 128, 512))
    wq = headtiles_T(ins["L0_attn_wq"])
    wk = headtiles_T(ins["L0_attn_wk"])
    wv = headtiles_T(ins["L0_attn_wv"])
    wqkv = np.ascontiguousarray(np.stack([wq, wk, wv], axis=1))
    wo_t = tile_lhsT(np.ascontiguousarray(np.asarray(ins["L0_attn_wo"]).astype(np.float32).T))
    return {"L0a_wqkv": wqkv, "L0a_wo": np.ascontiguousarray(wo_t)}


def decl_l0attn(nc):
    return dict(
        wqkv_ap=nc.dram_tensor("L0a_wqkv", [16, 3, 128, 512], F32, kind="ExternalInput").ap(),
        wo_ap=nc.dram_tensor("L0a_wo", [8, 128, 1024], F32, kind="ExternalInput").ap(),
    )


def emit_seq(p: Prog, ops):
    nc = p.nc
    plan = []
    cnt = {"dve": p.C["dve"], "act": p.C["act"], "pool": p.C["pool"]}
    base = dict(cnt)
    for eng, fn in ops:
        plan.append((eng, fn, dict(cnt)))
        cnt[ENG2SEM[eng]] += 1
    secs = {"vector": [], "scalar": [], "gpsimd": []}
    for eng, fn, pre in plan:
        secs[eng].append((eng, fn, pre))
    with p.block() as blk:
        for engname in ("vector", "scalar", "gpsimd"):
            if not secs[engname]:
                continue
            def mk(engname):
                def _(q):
                    own = ENG2SEM[engname]
                    seen = {"dve": -1, "act": -1, "pool": -1}
                    for eng, fn, pre in secs[engname]:
                        for s2 in ("dve", "act", "pool"):
                            want = pre[s2]
                            if s2 == own:
                                if want > base[own] and want > seen[s2]:
                                    q.wait_ge(p.S[s2], want)
                                    seen[s2] = want
                            else:
                                if want > 0 and want > seen[s2]:
                                    q.wait_ge(p.S[s2], want)
                                    seen[s2] = want
                        p.inc(fn(q), own)
                        if ENG2SEM[engname] == own:
                            pass
                return _
            getattr(blk, engname)(mk(engname))



PI = float(np.pi)

def emit_l2attn(p: Prog, wmag_ap, wq_ap, bmq_ap, ramp_ap, phases_ap, lng_ap,
                w1_ap, b1_ap, w2_ap, b2_ap):
    nc = p.nc
    SEG = 256
    NSG = 2048 // SEG
    wmagb = nc.dram_tensor("l2wmagb", [8, 128, 1024], BF16).ap()
    wqb = nc.dram_tensor("l2wqb", [8, 128, 1024], BF16).ap()
    w1b = nc.dram_tensor("l2w1b", [16, 128, 4096], BF16).ap()
    w2b = nc.dram_tensor("l2w2b", [8, 128, 2048], BF16).ap()
    with ExitStack() as es:
        ramp = p.sbuf(es, [1, 256], F32)
        phs = p.sbuf(es, [128, 8], F32)
        bmq = p.sbuf(es, [128, 16], F32)
        lng = p.sbuf(es, [128, 64], F32)
        b1s = p.sbuf(es, [128, 16], F32)
        b2s = p.sbuf(es, [128, 8], F32)
        Hseg = p.sbuf(es, [128, FCH * SEG], BF16)
        mag0 = p.sbuf(es, [128, FCH * SEG], BF16)
        qoff = p.sbuf(es, [128, FCH * SEG], F32)
        CTX = p.sbuf(es, [128, 32 * SEG], BF16)
        G = p.sbuf(es, [128, 16 * SEG], BF16)
        Yseg = p.sbuf(es, [128, FCH * SEG], F32)
        carry = p.sbuf(es, [128, 24], F32)
        cbuf = p.sbuf(es, [128, 4096], F32)

        def cv(i):
            return cbuf[:, i * SEG:(i + 1) * SEG]
        t_phi, t_n, t_m, t_sin, t_cos, t_w0, t_a, t_b, t_mr, t_mi, t_sr, t_si = [cv(i) for i in range(12)]
        p.barrier(("sync",))
        with p.block() as blk:
            @blk.sync
            def _(s):
                for ap_, buf in ((phases_ap, phs), (bmq_ap, bmq),
                                 (lng_ap, lng), (b1_ap, b1s), (b2_ap, b2s)):
                    p.inc(s.dma_start(out=buf[:], in_=ap_), "dma", 16)
            @blk.vector
            def _(v):
                v.wait_ge(p.S["dma"], p.C["dma"])
        pairs = []
        for src, dst, noc in ((wmag_ap, wmagb, 8), (wq_ap, wqb, 8),
                              (w1_ap, w1b, 16), (w2_ap, w2b, 8)):
            for oc in range(noc):
                pairs.append((src[oc, :, :], dst[oc, :, :]))
        emit_precast(p, pairs, obuf=cbuf)
        p.barrier(("sync",))
        for seq in range(SEQ):
            with p.block() as blk:
                @blk.vector
                def _(v):
                    p.fence(v, "dve")
                    p.inc(v.memset(carry[:], 0.0), "dve")
            p.barrier(("vector",))
            for sg in range(NSG):
                tok0 = seq * 2048 + sg * SEG
                emit_ln(p, p.X, dst=Hseg, ntok=SEG, src_tok0=tok0,
                        dst_stride=SEG, dst_tok0=0, scratch=cbuf)

                def epi_mag(a, oc, j, psv):
                    return a.activation(mag0[:, oc * SEG:(oc + 1) * SEG], psv,
                                        AF.Sigmoid, bias=bmq[:, oc:oc + 1], scale=1.0)
                emit_gemm(p, lambda oc, hh: wmagb[oc, :, :], FCH, FCH,
                          lambda kc, j: Hseg[:, kc * SEG:(kc + 1) * SEG], epi_mag,
                          wdtype=BF16, kcw=8, nbuf=1)

                def epi_q(a, oc, j, psv):
                    return a.activation(qoff[:, oc * SEG:(oc + 1) * SEG], psv,
                                        AF.Identity, bias=bmq[:, 8 + oc:9 + oc], scale=1.0)
                emit_gemm(p, lambda oc, hh: wqb[oc, :, :], FCH, FCH,
                          lambda kc, j: Hseg[:, kc * SEG:(kc + 1) * SEG], epi_q,
                          wdtype=BF16, kcw=8, nbuf=1)

                with ExitStack() as es2:
                    ps_r = p.psum(es2, [128, 512])
                    with p.block() as blk:
                        @blk.sync
                        def _(s):
                            s.wait_ge(p.S["pe"], p.C["pe"])
                            p.inc(s.dma_start(out=ramp[:], in_=ramp_ap[0:1, sg * SEG:(sg + 1) * SEG]), "dma", 16)
                        @blk.tensor
                        def _(t):
                            t.wait_ge(p.S["dve"], p.C["dve"])
                            t.wait_ge(p.S["act"], p.C["act"])
                            t.wait_ge(p.S["dma"], p.C["dma"])
                            p.inc(t.matmul(ps_r[:, 0:SEG], p.rowsel[0:1, 0:128],
                                           ramp[0:1, 0:SEG],
                                           start=True, stop=True), "pe")
                    p.barrier(("tensor", "vector", "scalar", "sync"))
                    allops = []
                    for c in range(FCH):
                        H_ = Hseg[:, c * SEG:(c + 1) * SEG]
                        M_ = mag0[:, c * SEG:(c + 1) * SEG]
                        Q_ = qoff[:, c * SEG:(c + 1) * SEG]
                        ops = []
                        ops.append(("vector", lambda v, c=c: v.tensor_scalar(t_phi, ps_r[:, 0:SEG], phs[:, c:c + 1], 0.01, ALU.mult, ALU.mult)))
                        ops.append(("vector", lambda v: v.tensor_scalar(t_n, t_phi, 1.0 / (2 * PI), MAGIC, ALU.mult, ALU.add)))
                        ops.append(("vector", lambda v: v.tensor_scalar(t_n, t_n, MAGIC, None, ALU.subtract)))
                        ops.append(("vector", lambda v: v.scalar_tensor_tensor(t_m, t_n, -2 * PI, t_phi, ALU.mult, ALU.add)))
                        ops.append(("vector", lambda v: v.tensor_scalar(t_m, t_m, -PI, PI, ALU.max, ALU.min)))
                        ops.append(("scalar", lambda a: a.activation(t_sin, t_m, AF.Sin, scale=1.0)))
                        ops.append(("scalar", lambda a: a.activation(t_n, t_m, AF.Abs, scale=1.0)))
                        ops.append(("scalar", lambda a: a.activation(t_cos, t_n, AF.Sin, bias=p.aux[:, AUX_HALFPI:AUX_HALFPI + 1], scale=-1.0)))
                        ops.append(("vector", lambda v, H_=H_, M_=M_: v.scalar_tensor_tensor(t_w0, M_, 5.0, H_, ALU.mult, ALU.mult)))
                        ops.append(("vector", lambda v: v.tensor_tensor(t_a, t_w0, t_cos, ALU.mult)))
                        ops.append(("vector", lambda v: v.tensor_tensor(t_b, t_w0, t_sin, ALU.mult)))
                        ops.append(("vector", lambda v, c=c: v.tensor_tensor_scan(t_sr, t_a, t_a, carry[:, c:c + 1], ALU.add, ALU.bypass)))
                        ops.append(("vector", lambda v, c=c: v.tensor_tensor_scan(t_si, t_b, t_b, carry[:, 8 + c:9 + c], ALU.add, ALU.bypass)))
                        ops.append(("vector", lambda v, c=c, M_=M_: v.tensor_tensor_scan(t_a, M_, M_, carry[:, 16 + c:17 + c], ALU.add, ALU.bypass)))
                        ops.append(("vector", lambda v, c=c: v.tensor_copy(carry[:, c:c + 1], t_sr[:, SEG - 1:SEG])))
                        ops.append(("vector", lambda v, c=c: v.tensor_copy(carry[:, 8 + c:9 + c], t_si[:, SEG - 1:SEG])))
                        ops.append(("vector", lambda v, c=c: v.tensor_copy(carry[:, 16 + c:17 + c], t_a[:, SEG - 1:SEG])))
                        ops.append(("scalar", lambda a: a.activation(t_b, t_a, AF.Sqrt, bias=p.aux[:, AUX_WKVEPS:AUX_WKVEPS + 1], scale=5.0)))
                        ops.append(("vector", lambda v: v.reciprocal(t_b, t_b)))
                        ops.append(("vector", lambda v: v.tensor_tensor(t_mr, t_sr, t_b, ALU.mult)))
                        ops.append(("vector", lambda v: v.tensor_tensor(t_mi, t_si, t_b, ALU.mult)))
                        ops.append(("vector", lambda v, c=c, H_=H_: v.tensor_tensor(CTX[:, c * SEG:(c + 1) * SEG], H_, t_cos, ALU.mult)))
                        ops.append(("vector", lambda v, c=c, H_=H_: v.tensor_tensor(CTX[:, (8 + c) * SEG:(9 + c) * SEG], H_, t_sin, ALU.mult)))
                        ops.append(("vector", lambda v, Q_=Q_: v.tensor_tensor(t_phi, t_phi, Q_, ALU.add)))
                        ops.append(("vector", lambda v: v.tensor_scalar(t_n, t_phi, 1.0 / (2 * PI), MAGIC, ALU.mult, ALU.add)))
                        ops.append(("vector", lambda v: v.tensor_scalar(t_n, t_n, MAGIC, None, ALU.subtract)))
                        ops.append(("vector", lambda v: v.scalar_tensor_tensor(t_m, t_n, -2 * PI, t_phi, ALU.mult, ALU.add)))
                        ops.append(("vector", lambda v: v.tensor_scalar(t_m, t_m, -PI, PI, ALU.max, ALU.min)))
                        ops.append(("scalar", lambda a: a.activation(t_sin, t_m, AF.Sin, scale=1.0)))
                        ops.append(("scalar", lambda a: a.activation(t_n, t_m, AF.Abs, scale=1.0)))
                        ops.append(("scalar", lambda a: a.activation(t_cos, t_n, AF.Sin, bias=p.aux[:, AUX_HALFPI:AUX_HALFPI + 1], scale=-1.0)))
                        ops.append(("vector", lambda v: v.tensor_tensor(t_a, t_mr, t_cos, ALU.mult)))
                        ops.append(("vector", lambda v: v.tensor_tensor(t_b, t_mi, t_sin, ALU.mult)))
                        ops.append(("vector", lambda v, c=c: v.tensor_tensor(CTX[:, (16 + c) * SEG:(17 + c) * SEG], t_a, t_b, ALU.add)))
                        ops.append(("vector", lambda v: v.tensor_tensor(t_a, t_mi, t_cos, ALU.mult)))
                        ops.append(("vector", lambda v: v.tensor_tensor(t_b, t_mr, t_sin, ALU.mult)))
                        ops.append(("vector", lambda v, c=c: v.tensor_tensor(CTX[:, (24 + c) * SEG:(25 + c) * SEG], t_a, t_b, ALU.subtract)))
                        allops.extend(ops)
                    emit_seq(p, allops)
                    p.barrier(("vector", "scalar", "tensor"))
                emit_ln(p, CTX, dst=CTX, ntok=SEG, nch=32, src_stride=SEG,
                        dst_stride=SEG, eps=1e-5, gamma=lng, beta=lng,
                        gb_col0=0, beta_col0=32, scratch=cbuf)

                def epi_w1(a, oc, j, psv):
                    return a.activation(G[:, oc * SEG:(oc + 1) * SEG], psv,
                                        AF.Gelu, bias=b1s[:, oc:oc + 1], scale=1.0)
                emit_gemm(p, lambda oc, hh: w1b[oc, :, hh * 1024:(hh + 1) * 1024], 16, 32,
                          lambda kc, j: CTX[:, kc * SEG:(kc + 1) * SEG], epi_w1,
                          wdtype=BF16, kcw=8, nbuf=1)

                def epi_w2(v, oc, j, psv):
                    return v.scalar_tensor_tensor(Yseg[:, oc * SEG:(oc + 1) * SEG], psv,
                                                  b2s[:, oc:oc + 1],
                                                  Hseg[:, oc * SEG:(oc + 1) * SEG],
                                                  ALU.add, ALU.add)
                emit_gemm(p, lambda oc, hh: w2b[oc, :, hh * 1024:(hh + 1) * 1024], FCH, 16,
                          lambda kc, j: G[:, kc * SEG:(kc + 1) * SEG], epi_w2,
                          wdtype=BF16, epi_eng="vector", cast_eng="scalar", kcw=8, nbuf=1)

                emit_ln(p, Yseg, acc_into=p.X, ntok=SEG, src_stride=SEG,
                        dst_stride=TOK, dst_tok0=tok0, scratch=cbuf)


def prep_l2attn(ins):
    wm = tile_lhsT(np.ascontiguousarray(np.asarray(ins["L2_attn_W_mag"]).astype(np.float32).T))
    wq = tile_lhsT(np.ascontiguousarray(np.asarray(ins["L2_attn_W_q"]).astype(np.float32).T))
    bmq = np.concatenate([bias_tiles(ins["L2_attn_b_mag"]), bias_tiles(ins["L2_attn_b_q"])], axis=1)
    ramp = np.arange(2048, dtype=np.float32).reshape(1, 2048)
    phs = bias_tiles(ins["L2_attn_random_phases"])
    lng = np.concatenate([bias_tiles(ins["L2_attn_ln_g"]), bias_tiles(ins["L2_attn_ln_b"])], axis=1)
    w1 = tile_lhsT(np.ascontiguousarray(np.asarray(ins["L2_attn_W1"]).astype(np.float32).T))
    w2 = tile_lhsT(np.ascontiguousarray(np.asarray(ins["L2_attn_W2"]).astype(np.float32).T))
    return {"L2a_wmag": np.ascontiguousarray(wm), "L2a_wq": np.ascontiguousarray(wq),
            "L2a_bmq": np.ascontiguousarray(bmq), "L2a_ramp": ramp,
            "L2a_phs": np.ascontiguousarray(phs), "L2a_lng": np.ascontiguousarray(lng),
            "L2a_w1": np.ascontiguousarray(w1), "L2a_b1": bias_tiles(ins["L2_attn_b1"]),
            "L2a_w2": np.ascontiguousarray(w2), "L2a_b2": bias_tiles(ins["L2_attn_b2"])}


def decl_l2attn(nc):
    return dict(
        wmag_ap=nc.dram_tensor("L2a_wmag", [8, 128, 1024], F32, kind="ExternalInput").ap(),
        wq_ap=nc.dram_tensor("L2a_wq", [8, 128, 1024], F32, kind="ExternalInput").ap(),
        bmq_ap=nc.dram_tensor("L2a_bmq", [128, 16], F32, kind="ExternalInput").ap(),
        ramp_ap=nc.dram_tensor("L2a_ramp", [1, 2048], F32, kind="ExternalInput").ap(),
        phases_ap=nc.dram_tensor("L2a_phs", [128, 8], F32, kind="ExternalInput").ap(),
        lng_ap=nc.dram_tensor("L2a_lng", [128, 64], F32, kind="ExternalInput").ap(),
        w1_ap=nc.dram_tensor("L2a_w1", [16, 128, 4096], F32, kind="ExternalInput").ap(),
        b1_ap=nc.dram_tensor("L2a_b1", [128, 16], F32, kind="ExternalInput").ap(),
        w2_ap=nc.dram_tensor("L2a_w2", [8, 128, 2048], F32, kind="ExternalInput").ap(),
        b2_ap=nc.dram_tensor("L2a_b2", [128, 8], F32, kind="ExternalInput").ap(),
    )


def emit_l1attn(p: Prog, wkt_ap, wvt_ap, wrt_ap, wkc_ap, wvc_ap, wrc_ap,
                lnab_ap, mix_ap, u_ap):
    nc = p.nc
    SEG = 256
    NSG = 2048 // SEG
    EPSW = 1e-8
    wktb = nc.dram_tensor("l1wktb", [8, 128, 1024], BF16).ap()
    wvtb = nc.dram_tensor("l1wvtb", [8, 128, 1024], BF16).ap()
    wrtb = nc.dram_tensor("l1wrtb", [8, 128, 1024], BF16).ap()
    wkcb = nc.dram_tensor("l1wkcb", [32, 128, 1024], BF16).ap()
    wvcb = nc.dram_tensor("l1wvcb", [8, 128, 4096], BF16).ap()
    wrcb = nc.dram_tensor("l1wrcb", [8, 128, 1024], BF16).ap()
    with ExitStack() as es:
        lnab = p.sbuf(es, [128, 32], F32)
        mixs = p.sbuf(es, [128, 40], F32)
        imix = p.sbuf(es, [128, 40], F32)
        ucol = p.sbuf(es, [128, 8], F32)
        XA = p.sbuf(es, [128, FCH * SEG], F32)
        Hseg = p.sbuf(es, [128, FCH * SEG], BF16)
        HS = p.sbuf(es, [128, FCH * SEG], BF16)
        TSB = p.sbuf(es, [128, FCH * SEG], BF16)
        Kf = p.sbuf(es, [128, FCH * SEG], F32)
        Vf = p.sbuf(es, [128, FCH * SEG], BF16)
        KC = p.sbuf(es, [128, 32 * SEG], BF16)
        RC = p.sbuf(es, [128, FCH * SEG], BF16)
        SSc = p.sbuf(es, [128, SEG + 8], F32)
        SSs = p.sbuf(es, [128, SEG + 8], F32)
        SSd = p.sbuf(es, [128, SEG + 8], F32)
        hprev = p.sbuf(es, [128, 32], F32)
        carry = p.sbuf(es, [128, 32], F32)
        cbuf = p.sbuf(es, [128, 2048], F32)
        Rsig = cbuf[:, 1024:2048].bitcast(BF16)
        VV = TSB

        def cv(i):
            return cbuf[:, i * SEG:(i + 1) * SEG]
        t0, t1, t2, t3 = [cv(i) for i in range(4)]

        p.barrier(("sync",))
        with p.block() as blk:
            @blk.sync
            def _(s):
                for ap_, buf in ((lnab_ap, lnab), (mix_ap, mixs), (u_ap, ucol)):
                    p.inc(s.dma_start(out=buf[:], in_=ap_), "dma", 16)
            @blk.vector
            def _(v):
                v.wait_ge(p.S["dma"], p.C["dma"])
                p.inc(v.tensor_scalar(imix[:], mixs[:], -1.0, 1.0, ALU.mult, ALU.add), "dve")
        pairs = []
        for src, dst, noc in ((wkt_ap, wktb, 8), (wvt_ap, wvtb, 8), (wrt_ap, wrtb, 8),
                              (wkc_ap, wkcb, 32), (wvc_ap, wvcb, 8), (wrc_ap, wrcb, 8)):
            for oc in range(noc):
                pairs.append((src[oc, :, :], dst[oc, :, :]))
        emit_precast(p, pairs, obuf=cbuf)
        p.barrier(("sync",))

        for seq in range(SEQ):
            with p.block() as blk:
                @blk.vector
                def _(v):
                    p.fence(v, "dve")
                    p.inc(v.memset(carry[:], 0.0), "dve")
                    p.inc(v.memset(hprev[:], 0.0), "dve")
            p.barrier(("vector",))
            for sg in range(NSG):
                tok0 = seq * 2048 + sg * SEG
                par, npar = (sg % 2) * 16, ((sg + 1) % 2) * 16
                emit_ln(p, p.X, dst=XA, ntok=SEG, src_tok0=tok0,
                        dst_stride=SEG, dst_tok0=0, scratch=cbuf)
                emit_ln(p, XA, dst=Hseg, ntok=SEG, src_stride=SEG, dst_stride=SEG,
                        gamma=lnab, beta=lnab, gb_col0=0, beta_col0=8, scratch=cbuf)
                ops = []
                for c in range(FCH):
                    ops.append(("vector", lambda v, c=c: v.tensor_copy(HS[:, c * SEG + 1:(c + 1) * SEG], Hseg[:, c * SEG:(c + 1) * SEG - 1])))
                    ops.append(("vector", lambda v, c=c: v.tensor_copy(HS[:, c * SEG:c * SEG + 1], hprev[:, par + c:par + c + 1])))
                    ops.append(("vector", lambda v, c=c: v.tensor_copy(hprev[:, npar + c:npar + c + 1], Hseg[:, (c + 1) * SEG - 1:(c + 1) * SEG])))
                emit_seq(p, ops)
                for i3, (w_ap, mcol) in enumerate(((wktb, 0), (wvtb, 8), (wrtb, 16))):
                    ops = []
                    for c in range(FCH):
                        ops.append(("vector", lambda v, c=c, mcol=mcol: v.tensor_scalar(t0, Hseg[:, c * SEG:(c + 1) * SEG], mixs[:, mcol + c:mcol + c + 1], None, ALU.mult)))
                        ops.append(("vector", lambda v, c=c, mcol=mcol: v.scalar_tensor_tensor(TSB[:, c * SEG:(c + 1) * SEG], HS[:, c * SEG:(c + 1) * SEG], imix[:, mcol + c:mcol + c + 1], t0, ALU.mult, ALU.add)))
                    emit_seq(p, ops)
                    p.barrier(("vector", "tensor", "scalar"))
                    if i3 == 0:
                        def epi(a, oc, j, psv):
                            return a.activation(Kf[:, oc * SEG:(oc + 1) * SEG], psv, AF.Identity, scale=1.0)
                    elif i3 == 1:
                        def epi(a, oc, j, psv):
                            return a.activation(Vf[:, oc * SEG:(oc + 1) * SEG], psv, AF.Identity, scale=1.0)
                    else:
                        def epi(a, oc, j, psv):
                            return a.activation(Rsig[:, oc * SEG:(oc + 1) * SEG], psv, AF.Sigmoid, scale=1.0)
                    emit_gemm(p, lambda oc, hh, w_ap=w_ap: w_ap[oc, :, :], FCH, FCH,
                              lambda kc, j: TSB[:, kc * SEG:(kc + 1) * SEG], epi,
                              wdtype=BF16, kcw=8, nbuf=1)
                allops = []
                for c in range(FCH):
                    K_ = Kf[:, c * SEG:(c + 1) * SEG]
                    V_ = Vf[:, c * SEG:(c + 1) * SEG]
                    Cex = SSc[:, 0:SEG]
                    Cin = SSc[:, 1:SEG + 1]
                    Dex = SSd[:, 0:SEG]
                    Din = SSd[:, 1:SEG + 1]
                    Rex = SSs[:, 0:SEG]
                    ops = []
                    ops.append(("scalar", lambda a, K_=K_: a.activation(t0, K_, AF.Exp, scale=1.0)))
                    ops.append(("vector", lambda v, V_=V_: v.tensor_tensor(t1, t0, V_, ALU.mult)))
                    ops.append(("vector", lambda v, c=c: v.tensor_copy(SSc[:, 0:1], carry[:, c:c + 1])))
                    ops.append(("vector", lambda v, c=c, Cin=Cin: v.tensor_tensor_scan(Cin, t0, t0, carry[:, c:c + 1], ALU.add, ALU.bypass)))
                    ops.append(("vector", lambda v, c=c: v.tensor_copy(carry[:, c:c + 1], SSc[:, SEG:SEG + 1])))
                    ops.append(("vector", lambda v, Cex=Cex: v.tensor_tensor(t2, Cex, t0, ALU.max)))
                    ops.append(("vector", lambda v, c=c, Din=Din: v.tensor_tensor_scan(Din, t2, t2, carry[:, 24 + c:25 + c], ALU.add, ALU.bypass)))
                    ops.append(("vector", lambda v, c=c: v.tensor_copy(carry[:, 24 + c:25 + c], SSd[:, SEG:SEG + 1])))
                    ops.append(("vector", lambda v, c=c: v.tensor_copy(SSd[:, 0:1], carry[:, 16 + c:17 + c])))
                    ops.append(("vector", lambda v, Dex=Dex, Cex=Cex: v.scalar_tensor_tensor(t3, Dex, EPSW, Cex, ALU.mult, ALU.add)))
                    ops.append(("vector", lambda v: v.tensor_tensor(t2, t3, t0, ALU.max)))
                    ops.append(("vector", lambda v, c=c, Din=Din: v.tensor_tensor_scan(Din, t2, t2, carry[:, 16 + c:17 + c], ALU.add, ALU.bypass)))
                    ops.append(("vector", lambda v, c=c: v.tensor_copy(carry[:, 16 + c:17 + c], SSd[:, SEG:SEG + 1])))
                    ops.append(("vector", lambda v, Din=Din, Cin=Cin: v.scalar_tensor_tensor(t3, Din, EPSW, Cin, ALU.mult, ALU.add)))
                    ops.append(("vector", lambda v: v.reciprocal(t2, t2)))
                    ops.append(("vector", lambda v: v.tensor_tensor(t3, t3, t2, ALU.mult)))
                    ops.append(("vector", lambda v, c=c: v.tensor_copy(SSs[:, 0:1], carry[:, 8 + c:9 + c])))
                    ops.append(("vector", lambda v, c=c: v.tensor_tensor_scan(SSs[:, 1:SEG + 1], t1, t3, carry[:, 8 + c:9 + c], ALU.add, ALU.mult)))
                    ops.append(("vector", lambda v, c=c: v.tensor_copy(carry[:, 8 + c:9 + c], SSs[:, SEG:SEG + 1])))
                    ops.append(("scalar", lambda a, K_=K_, c=c: a.activation(t0, K_, AF.Exp, bias=ucol[:, c:c + 1], scale=1.0)))
                    ops.append(("vector", lambda v, Dex=Dex, Cex=Cex: v.scalar_tensor_tensor(t2, Dex, EPSW, Cex, ALU.mult, ALU.add)))
                    ops.append(("vector", lambda v: v.tensor_tensor(t1, t2, t0, ALU.max)))
                    ops.append(("vector", lambda v: v.scalar_tensor_tensor(t1, t1, EPSW, t0, ALU.mult, ALU.add)))
                    ops.append(("vector", lambda v: v.tensor_tensor(t1, t1, t2, ALU.add)))
                    ops.append(("vector", lambda v, V_=V_: v.tensor_tensor(t0, t0, V_, ALU.mult)))
                    ops.append(("vector", lambda v, Rex=Rex: v.tensor_tensor(t0, t0, Rex, ALU.add)))
                    ops.append(("vector", lambda v: v.reciprocal(t1, t1)))
                    ops.append(("vector", lambda v: v.tensor_tensor(t0, t0, t1, ALU.mult)))
                    ops.append(("vector", lambda v, c=c: v.tensor_tensor(t0, t0, Rsig[:, c * SEG:(c + 1) * SEG], ALU.mult)))
                    ops.append(("vector", lambda v, c=c: v.tensor_tensor(XA[:, c * SEG:(c + 1) * SEG], XA[:, c * SEG:(c + 1) * SEG], t0, ALU.add)))
                    allops.extend(ops)
                emit_seq(p, allops)
                emit_ln(p, XA, dst=Hseg, ntok=SEG, src_stride=SEG, dst_stride=SEG,
                        gamma=lnab, beta=lnab, gb_col0=16, beta_col0=24, scratch=cbuf)
                ops = []
                for c in range(FCH):
                    ops.append(("vector", lambda v, c=c: v.tensor_copy(HS[:, c * SEG + 1:(c + 1) * SEG], Hseg[:, c * SEG:(c + 1) * SEG - 1])))
                    ops.append(("vector", lambda v, c=c: v.tensor_copy(HS[:, c * SEG:c * SEG + 1], hprev[:, par + 8 + c:par + 9 + c])))
                    ops.append(("vector", lambda v, c=c: v.tensor_copy(hprev[:, npar + 8 + c:npar + 9 + c], Hseg[:, (c + 1) * SEG - 1:(c + 1) * SEG])))
                emit_seq(p, ops)
                for i3, (w_ap, mcol, och) in enumerate(((wkcb, 24, 32), (wrcb, 32, 8))):
                    ops = []
                    for c in range(FCH):
                        ops.append(("vector", lambda v, c=c, mcol=mcol: v.tensor_scalar(t0, Hseg[:, c * SEG:(c + 1) * SEG], mixs[:, mcol + c:mcol + c + 1], None, ALU.mult)))
                        ops.append(("vector", lambda v, c=c, mcol=mcol: v.scalar_tensor_tensor(TSB[:, c * SEG:(c + 1) * SEG], HS[:, c * SEG:(c + 1) * SEG], imix[:, mcol + c:mcol + c + 1], t0, ALU.mult, ALU.add)))
                    emit_seq(p, ops)
                    p.barrier(("vector", "tensor", "scalar"))
                    if i3 == 0:
                        def epi(v, oc, j, psv):
                            return v.tensor_scalar(KC[:, oc * SEG:(oc + 1) * SEG], psv, 0.0, None, ALU.max)
                        emit_gemm(p, lambda oc, hh, w_ap=w_ap: w_ap[oc, :, :], och, FCH,
                                  lambda kc, j: TSB[:, kc * SEG:(kc + 1) * SEG], epi,
                                  wdtype=BF16, epi_eng="vector", cast_eng="scalar", kcw=8, nbuf=1)
                    else:
                        def epi(a, oc, j, psv):
                            return a.activation(RC[:, oc * SEG:(oc + 1) * SEG], psv, AF.Sigmoid, scale=1.0)
                        emit_gemm(p, lambda oc, hh, w_ap=w_ap: w_ap[oc, :, :], och, FCH,
                                  lambda kc, j: TSB[:, kc * SEG:(kc + 1) * SEG], epi,
                                  wdtype=BF16, kcw=8, nbuf=1)
                ops = []
                for c in range(32):
                    ops.append(("scalar", lambda a, c=c: a.activation(KC[:, c * SEG:(c + 1) * SEG], KC[:, c * SEG:(c + 1) * SEG], AF.Square, scale=1.0)))
                emit_seq(p, ops)
                p.barrier(("scalar", "tensor", "vector"))

                def epi_vv(a, oc, j, psv):
                    return a.activation(VV[:, oc * SEG:(oc + 1) * SEG], psv, AF.Identity, scale=1.0)
                emit_gemm(p, lambda oc, hh: wvcb[oc, :, hh * 1024:(hh + 1) * 1024], FCH, 32,
                          lambda kc, j: KC[:, kc * SEG:(kc + 1) * SEG], epi_vv,
                          wdtype=BF16, kcw=8, nbuf=1)
                ops = []
                for c in range(FCH):
                    ops.append(("vector", lambda v, c=c: v.tensor_tensor(t0, VV[:, c * SEG:(c + 1) * SEG], RC[:, c * SEG:(c + 1) * SEG], ALU.mult)))
                    ops.append(("vector", lambda v, c=c: v.tensor_tensor(XA[:, c * SEG:(c + 1) * SEG], XA[:, c * SEG:(c + 1) * SEG], t0, ALU.add)))
                emit_seq(p, ops)
                emit_ln(p, XA, acc_into=p.X, ntok=SEG, src_stride=SEG,
                        dst_stride=TOK, dst_tok0=tok0, scratch=cbuf)


def prep_l1attn(ins):
    def tl(W):
        return np.ascontiguousarray(tile_lhsT(np.ascontiguousarray(np.asarray(W).astype(np.float32).T)))
    lnab = np.concatenate([bias_tiles(ins["L1_attn_ln1_g"]), bias_tiles(ins["L1_attn_ln1_b"]),
                           bias_tiles(ins["L1_attn_ln2_g"]), bias_tiles(ins["L1_attn_ln2_b"])], axis=1)
    mixs = np.concatenate([bias_tiles(ins["L1_attn_mix_k"]), bias_tiles(ins["L1_attn_mix_v"]),
                           bias_tiles(ins["L1_attn_mix_r"]), bias_tiles(ins["L1_attn_cmix_k"]),
                           bias_tiles(ins["L1_attn_cmix_r"])], axis=1)
    return {"L1a_wkt": tl(ins["L1_attn_Wk_t"]), "L1a_wvt": tl(ins["L1_attn_Wv_t"]),
            "L1a_wrt": tl(ins["L1_attn_Wr_t"]), "L1a_wkc": tl(ins["L1_attn_Wk_c"]),
            "L1a_wvc": tl(ins["L1_attn_Wv_c"]), "L1a_wrc": tl(ins["L1_attn_Wr_c"]),
            "L1a_lnab": np.ascontiguousarray(lnab), "L1a_mix": np.ascontiguousarray(mixs),
            "L1a_u": bias_tiles(ins["L1_attn_time_first"])}


def decl_l1attn(nc):
    return dict(
        wkt_ap=nc.dram_tensor("L1a_wkt", [8, 128, 1024], F32, kind="ExternalInput").ap(),
        wvt_ap=nc.dram_tensor("L1a_wvt", [8, 128, 1024], F32, kind="ExternalInput").ap(),
        wrt_ap=nc.dram_tensor("L1a_wrt", [8, 128, 1024], F32, kind="ExternalInput").ap(),
        wkc_ap=nc.dram_tensor("L1a_wkc", [32, 128, 1024], F32, kind="ExternalInput").ap(),
        wvc_ap=nc.dram_tensor("L1a_wvc", [8, 128, 4096], F32, kind="ExternalInput").ap(),
        wrc_ap=nc.dram_tensor("L1a_wrc", [8, 128, 1024], F32, kind="ExternalInput").ap(),
        lnab_ap=nc.dram_tensor("L1a_lnab", [128, 32], F32, kind="ExternalInput").ap(),
        mix_ap=nc.dram_tensor("L1a_mix", [128, 40], F32, kind="ExternalInput").ap(),
        u_ap=nc.dram_tensor("L1a_u", [128, 8], F32, kind="ExternalInput").ap(),
    )


def emit_l0ffn(p: Prog, router_ap, w_ap):
    nc = p.nc
    SEG = 512
    NSG2 = TOK // SEG
    FF = 22
    w1b = nc.dram_tensor("l0fw1b", [9, 22, 128, 1024], BF16).ap()
    w3b = nc.dram_tensor("l0fw3b", [9, 22, 128, 1024], BF16).ap()
    w2b = nc.dram_tensor("l0fw2b", [9, 8, 128, 2816], BF16).ap()
    with ExitStack() as es:
        rt = p.sbuf(es, [128, 64], F32)
        rtb = p.sbuf(es, [128, 64], BF16)
        Hseg = p.sbuf(es, [128, FCH * SEG], BF16)
        M1 = p.sbuf(es, [128, FF * SEG], BF16)
        OUT = p.sbuf(es, [128, FCH * SEG], F32)
        sc_tm = p.sbuf(es, [128, 32], F32)
        mx_tm = p.sbuf(es, [128, 4], F32)
        gekm = p.sbuf(es, [8, SEG], F32)
        cbuf = p.sbuf(es, [128, 3584], F32)
        GX = cbuf[:, 1536:3584].bitcast(BF16)
        ge = cbuf[0:1, 0:SEG]
        mxrow = cbuf[0:1, SEG:2 * SEG]
        grow = cbuf[0:1, 2 * SEG:3 * SEG]
        sc_km = cbuf[0:8, 3 * SEG:4 * SEG]
        geq = p.take_dq(1, sw=True)[0]
        p.barrier(("sync",))
        with p.block() as blk:
            @blk.sync
            def _(s):
                p.inc(s.dma_start(out=rt[:].rearrange("p (kc e) -> p kc e", kc=8),
                                  in_=router_ap.rearrange("kc p e -> p kc e")), "dma", 16)
            @blk.vector
            def _(v):
                v.wait_ge(p.S["dma"], p.C["dma"])
                p.inc(v.tensor_copy(rtb[:], rt[:]), "dve")
        pairs = []
        for e in range(9):
            for oc in range(FF):
                pairs.append((w_ap[0][e, oc, :, :], w1b[e, oc, :, :]))
                pairs.append((w_ap[1][e, oc, :, :], w3b[e, oc, :, :]))
            for oc in range(FCH):
                pairs.append((w_ap[2][e, oc, :, :], w2b[e, oc, :, :]))
        emit_precast(p, pairs, obuf=cbuf)
        p.barrier(("sync",))
        for sg in range(NSG2):
            tok0 = sg * SEG
            emit_ln(p, p.X, dst=Hseg, ntok=SEG, src_tok0=tok0,
                    dst_stride=SEG, dst_tok0=0, scratch=cbuf)
            with ExitStack() as es2:
                ps_s = p.psum(es2, [8, 512])
                ps_t = [p.psum(es2, [128, 8], F32) for _ in range(2)]
                ps_r = [p.psum(es2, [1, 128], F32) for _ in range(2)]
                ps_b = p.psum(es2, [8, 512])
                with p.block() as blk:
                    @blk.tensor
                    def _(t):
                        base = dict(p.C)
                        t.wait_ge(p.S["dve"], base["dve"])
                        for kc in range(FCH):
                            ins = t.matmul(ps_s[:], rtb[:, kc * 8:(kc + 1) * 8],
                                           Hseg[:, kc * SEG:(kc + 1) * SEG],
                                           start=(kc == 0), stop=(kc == FCH - 1))
                        p.inc(ins, "pe")
                    @blk.scalar
                    def _(a):
                        a.wait_ge(p.S["pe"], p.C["pe"])
                        p.inc(a.activation(sc_km[:], ps_s[:, 0:SEG], AF.Identity, scale=1.0), "act")
                with p.block() as blk:
                    base = dict(p.C)
                    @blk.tensor
                    def _(t):
                        for j in range(4):
                            if j >= 2:
                                t.wait_ge(p.S["dve"], base["dve"] + 2 * (j - 1))
                            p.inc(t.transpose(ps_t[j % 2][:], sc_km[:, j * 128:(j + 1) * 128], p.eyef[0:8, 0:8]), "pe")
                    @blk.vector
                    def _(v):
                        for j in range(4):
                            v.wait_ge(p.S["pe"], base["pe"] + j + 1)
                            p.fence(v, "dve")
                            p.inc(v.tensor_copy(sc_tm[:, j * 8:(j + 1) * 8], ps_t[j % 2][:]), "dve")
                            p.fence(v, "dve")
                            p.inc(v.tensor_reduce(mx_tm[:, j:j + 1], sc_tm[:, j * 8:(j + 1) * 8], mybir.AxisListType.X, ALU.max), "dve")
                with p.block() as blk:
                    base = dict(p.C)
                    @blk.tensor
                    def _(t):
                        for j in range(4):
                            if j >= 2:
                                t.wait_ge(p.S["act"], base["act"] + j - 1)
                            p.inc(t.transpose(ps_r[j % 2][:], mx_tm[:, j:j + 1], p.eyef[:]), "pe")
                    @blk.scalar
                    def _(a):
                        for j in range(4):
                            a.wait_ge(p.S["pe"], base["pe"] + j + 1)
                            p.inc(a.activation(mxrow[0:1, j * 128:(j + 1) * 128], ps_r[j % 2][:], AF.Identity, scale=1.0), "act")
                with p.block() as blk:
                    @blk.scalar
                    def _(a):
                        p.fence(a, "act")
                        p.inc(a.activation(grow[:], mxrow[:], AF.Sigmoid, scale=1.0), "act")
                    @blk.tensor
                    def _(t):
                        p.inc(t.matmul(ps_b[:, 0:SEG], p.rowsel[0:1, 0:8], mxrow[:], start=True, stop=True), "pe")
                    @blk.vector
                    def _(v):
                        v.wait_ge(p.S["pe"], p.C["pe"])
                        p.fence(v, "dve")
                        p.inc(v.tensor_tensor(gekm[:], sc_km[:], ps_b[:, 0:SEG], ALU.is_ge), "dve")
                with p.block() as blk:
                    @blk.tensor
                    def _(t):
                        p.inc(t.matmul(ps_b[:, 0:SEG], p.rowsel[0:1, 0:8], grow[:], start=True, stop=True), "pe")
                    @blk.vector
                    def _(v):
                        v.wait_ge(p.S["pe"], p.C["pe"])
                        p.fence(v, "dve")
                        p.inc(v.tensor_tensor(gekm[:], gekm[:], ps_b[:, 0:SEG], ALU.mult), "dve")
                p.barrier(("tensor", "vector", "scalar"))
            with ExitStack() as es2:
                ps_bc = p.psum(es2, [128, 512])
                for e in range(9):
                    if e < 8:
                        with p.block() as blk:
                            @blk.gpsimd
                            def _(g):
                                g.wait_ge(p.S["dve"], p.C["dve"])
                                g.wait_ge(p.S["pe"], p.C["pe"])
                                p.inc(g.dma_start(out=ge[:], in_=gekm[e:e + 1, :]), geq, 16)
                        with p.block() as blk:
                            @blk.tensor
                            def _(t):
                                t.wait_ge(p.S[geq], p.C[geq])
                                p.inc(t.matmul(ps_bc[:], p.rowsel[0:1, 0:128], ge[:], start=True, stop=True), "pe")
                            @blk.vector
                            def _(v):
                                v.wait_ge(p.S["pe"], p.C["pe"])
                                for c in range(FCH):
                                    p.fence(v, "dve")
                                    p.inc(v.tensor_tensor(GX[:, c * SEG:(c + 1) * SEG],
                                                          Hseg[:, c * SEG:(c + 1) * SEG], ps_bc[:], ALU.mult), "dve")
                        p.barrier(("tensor", "vector", "gpsimd", "sync"))
                        rhs_x = GX
                    else:
                        rhs_x = Hseg

                    def epi1(a, oc, j, psv):
                        return a.activation(M1[:, oc * SEG:(oc + 1) * SEG], psv, AF.Silu, scale=1.0)
                    emit_gemm(p, lambda oc, hh, e=e: w1b[e, oc, :, :], FF, FCH,
                              lambda kc, j, rhs_x=rhs_x: rhs_x[:, kc * SEG:(kc + 1) * SEG], epi1,
                              wdtype=BF16, kcw=8, nbuf=1)

                    def epi3(v, oc, j, psv):
                        return v.tensor_tensor(M1[:, oc * SEG:(oc + 1) * SEG], psv,
                                               M1[:, oc * SEG:(oc + 1) * SEG], ALU.mult)
                    emit_gemm(p, lambda oc, hh, e=e: w3b[e, oc, :, :], FF, FCH,
                              lambda kc, j, rhs_x=rhs_x: rhs_x[:, kc * SEG:(kc + 1) * SEG], epi3,
                              wdtype=BF16, epi_eng="vector", cast_eng="scalar", kcw=8, nbuf=1)

                    if e == 0:
                        def epi2(v, oc, j, psv):
                            return v.tensor_copy(OUT[:, oc * SEG:(oc + 1) * SEG], psv)
                    else:
                        def epi2(v, oc, j, psv):
                            return v.tensor_tensor(OUT[:, oc * SEG:(oc + 1) * SEG], OUT[:, oc * SEG:(oc + 1) * SEG], psv, ALU.add)
                    emit_gemm(p, lambda oc, hh, e=e: w2b[e, oc, :, hh * 1408:(hh + 1) * 1408], FCH, FF,
                              lambda kc, j: M1[:, kc * SEG:(kc + 1) * SEG], epi2,
                              wdtype=BF16, epi_eng="vector", cast_eng="scalar", kcw=11, nbuf=1)
            emit_ln(p, OUT, acc_into=p.X, ntok=SEG, src_stride=SEG,
                    dst_stride=TOK, dst_tok0=tok0, scratch=cbuf)


def prep_l0ffn(ins):
    r = np.asarray(ins["L0_ffn_router_DE"]).astype(np.float32).reshape(8, 128, 8)
    w1 = np.asarray(ins["L0_ffn_w1_eDF"]).astype(np.float32)
    w3 = np.asarray(ins["L0_ffn_w3_eDF"]).astype(np.float32)
    w2 = np.asarray(ins["L0_ffn_w2_eFD"]).astype(np.float32)
    ws1 = np.asarray(ins["L0_ffn_ws1_DF"]).astype(np.float32)
    ws3 = np.asarray(ins["L0_ffn_ws3_DF"]).astype(np.float32)
    ws2 = np.asarray(ins["L0_ffn_ws2_FD"]).astype(np.float32)
    def t9(mats):
        return np.stack([tile_lhsT(m) for m in mats], axis=0)
    w1t = t9([w1[e] for e in range(8)] + [ws1])
    w3t = t9([w3[e] for e in range(8)] + [ws3])
    w2t = t9([w2[e] for e in range(8)] + [ws2])
    return {"L0f_rt": np.ascontiguousarray(r), "L0f_w1": np.ascontiguousarray(w1t),
            "L0f_w3": np.ascontiguousarray(w3t), "L0f_w2": np.ascontiguousarray(w2t)}


def decl_l0ffn(nc):
    w1 = nc.dram_tensor("L0f_w1", [9, 22, 128, 1024], F32, kind="ExternalInput").ap()
    w3 = nc.dram_tensor("L0f_w3", [9, 22, 128, 1024], F32, kind="ExternalInput").ap()
    w2 = nc.dram_tensor("L0f_w2", [9, 8, 128, 2816], F32, kind="ExternalInput").ap()
    return dict(router_ap=nc.dram_tensor("L0f_rt", [8, 128, 8], F32, kind="ExternalInput").ap(),
                w_ap=(w1, w3, w2))


def emit_l1ffn(p: Prog, wexp_ap, wcon_ap, upc_ap, dnc_ap, bexp_ap, biases_ap):
    nc = p.nc
    w1p = nc.dram_tensor("l1w1p", [32, 128, 1024], BF16).ap()
    w2tmp = nc.dram_tensor("l1w2t", [8, 128, 4096], BF16).ap()
    w2p = nc.dram_tensor("l1w2p", [8, 128, 4096], BF16).ap()
    es0 = ExitStack()
    b1t = p.sbuf(es0, [128, 32], F32)
    b2t = p.sbuf(es0, [128, 8], F32)
    with ExitStack() as es:
        WF = p.sbuf(es, [128, 4096], F32)
        WFb = p.sbuf(es, [128, 4096], BF16)
        brow = p.sbuf(es, [1, 4096], BF16)
        cf = [p.sbuf(es, [128, 2048], BF16) for _ in range(4)]
        t1 = p.sbuf(es, [128, 2048], F32)
        t2 = p.sbuf(es, [128, 2048], F32)
        t3 = WFb[:].bitcast(F32)[:, 0:2048]
        crow = [t1[0:1, 0:2048], t2[0:1, 0:2048]]
        bs = p.sbuf(es, [128, 48], F32)
        wq = p.take_dq(2)
        cq = p.take_dq(1)[0]
        sq = p.take_dq(1, sw=True)[0]
        p.barrier(("sync",))
        with p.block() as blk:
            @blk.sync
            def _(s):
                p.inc(s.dma_start(out=bs[:], in_=biases_ap), "dma", 16)
        for side in range(2):
            w_in = wexp_ap if side == 0 else wcon_ap
            c_in = upc_ap if side == 0 else dnc_ap
            strides = [2 ** t for t in range(12)] if side == 0 else [2 ** (11 - t) for t in range(12)]
            for pp in range(8):
                with p.block() as blk:
                    @blk.sync
                    def _(s):
                        s.wait_ge(p.S["dve"], p.C["dve"])
                        s.wait_ge(p.S[sq], p.C[sq])
                        p.inc(s.dma_start(out=WF[:], in_=w_in[pp]), wq[0], 16)
                        if side == 0 and pp == 0:
                            p.inc(s.dma_start(out=crow[0][:], in_=bexp_ap[0:1, 0:2048]), wq[1], 16)
                            p.inc(s.dma_start(out=crow[1][:], in_=bexp_ap[0:1, 2048:4096]), wq[1], 16)
                    @blk.vector
                    def _(v):
                        v.wait_ge(p.S[wq[0]], p.C[wq[0]])
                        v.wait_ge(p.S[wq[1]], p.C[wq[1]])
                        if side == 0 and pp == 0:
                            p.inc(v.tensor_copy(brow[0:1, 0:2048], crow[0][:]), "dve")
                            p.inc(v.tensor_copy(brow[0:1, 2048:4096], crow[1][:]), "dve")
                for sti, sz in enumerate(strides):
                    with ExitStack() as es2:
                        ps_c = p.psum(es2, [128, 512])
                        base = dict(p.C)
                        with p.block() as blk:
                            @blk.sync
                            def _(s):
                                for i4 in range(4):
                                    if i4 >= 2:
                                        s.wait_ge(p.S["act"], base["act"] + 4 * (i4 - 1))
                                        s.wait_ge(p.S[wq[i4 % 2]], base[wq[i4 % 2]] + 16)
                                    else:
                                        s.wait_ge(p.S["pe"], base["pe"])
                                        s.wait_ge(p.S["dve"], base["dve"])
                                        s.wait_ge(p.S[wq[i4 % 2]], base[wq[i4 % 2]])
                                    p.inc(s.dma_start(out=crow[i4 % 2][:],
                                                      in_=c_in[sti, i4].rearrange("(one n) -> one n", one=1)), wq[i4 % 2], 16)
                            @blk.tensor
                            def _(t):
                                t.wait_ge(p.S["dve"], base["dve"])
                                for i4 in range(4):
                                    t.wait_ge(p.S[wq[i4 % 2]], base[wq[i4 % 2]] + 16 * (i4 // 2 + 1))
                                    for j in range(4):
                                        if i4 * 4 + j >= 1:
                                            t.wait_ge(p.S["act"], base["act"] + i4 * 4 + j)
                                        p.inc(t.matmul(ps_c[:], p.rowsel[0:1, 0:128],
                                                       crow[i4 % 2][0:1, j * 512:(j + 1) * 512],
                                                       start=True, stop=True), "pe")
                            @blk.scalar
                            def _(a):
                                for i4 in range(4):
                                    for j in range(4):
                                        a.wait_ge(p.S["pe"], base["pe"] + i4 * 4 + j + 1)
                                        p.inc(a.activation(cf[i4][:, j * 512:(j + 1) * 512], ps_c[:], AF.Identity, scale=1.0), "act")
                        for dc in range(1):
                            W4 = WF[:, 0:4096]
                            def rs(x):
                                return x.rearrange("q (nb i) -> q nb i", i=sz)
                            av = W4.rearrange("q (nb two i) -> q nb two i", two=2, i=sz)[:, :, 0, :]
                            bv = W4.rearrange("q (nb two i) -> q nb two i", two=2, i=sz)[:, :, 1, :]
                            ops = []
                            ops.append(("vector", lambda v, av=av: v.tensor_tensor(rs(t1[:]), av, rs(cf[0][:]), ALU.mult)))
                            ops.append(("vector", lambda v, av=av: v.tensor_tensor(rs(t2[:]), av, rs(cf[2][:]), ALU.mult)))
                            ops.append(("vector", lambda v, bv=bv: v.tensor_tensor(rs(t3[:]), bv, rs(cf[1][:]), ALU.mult)))
                            ops.append(("vector", lambda v, av=av: v.tensor_tensor(av, rs(t1[:]), rs(t3[:]), ALU.add)))
                            ops.append(("vector", lambda v, bv=bv: v.tensor_tensor(rs(t3[:]), bv, rs(cf[3][:]), ALU.mult)))
                            ops.append(("vector", lambda v, bv=bv: v.tensor_tensor(bv, rs(t2[:]), rs(t3[:]), ALU.add)))
                            if side == 0 and pp == 0 and dc == 0:
                                bav = brow[:].rearrange("q (nb two i) -> q nb two i", two=2, i=sz)[:, :, 0, :]
                                bbv = brow[:].rearrange("q (nb two i) -> q nb two i", two=2, i=sz)[:, :, 1, :]
                                ops.append(("vector", lambda v, bav=bav: v.tensor_tensor(rs(t1[0:1, :]), bav, rs(cf[0][0:1, :]), ALU.mult)))
                                ops.append(("vector", lambda v, bav=bav: v.tensor_tensor(rs(t2[0:1, :]), bav, rs(cf[2][0:1, :]), ALU.mult)))
                                ops.append(("vector", lambda v, bbv=bbv: v.tensor_tensor(rs(t3[0:1, :]), bbv, rs(cf[1][0:1, :]), ALU.mult)))
                                ops.append(("vector", lambda v, bav=bav: v.tensor_tensor(bav, rs(t1[0:1, :]), rs(t3[0:1, :]), ALU.add)))
                                ops.append(("vector", lambda v, bbv=bbv: v.tensor_tensor(rs(t3[0:1, :]), bbv, rs(cf[3][0:1, :]), ALU.mult)))
                                ops.append(("vector", lambda v, bbv=bbv: v.tensor_tensor(bbv, rs(t2[0:1, :]), rs(t3[0:1, :]), ALU.add)))
                            emit_seq(p, ops)
                with p.block() as blk:
                    @blk.vector
                    def _(v):
                        p.fence(v, "dve")
                        p.inc(v.tensor_copy(WFb[:], WF[:]), "dve")
                    @blk.gpsimd
                    def _(g):
                        g.wait_ge(p.S["dve"], p.C["dve"])
                        if side == 0:
                            dst = BassAP(w1p.tensor, pp * 128,
                                         [[1024, 128], [131072, 32], [1, 128]])
                            p.inc(g.dma_start(out=dst, in_=WFb[:].rearrange("q (oc m) -> q oc m", oc=32)), sq, 16)
                        else:
                            p.inc(g.dma_start(out=w2tmp[pp], in_=WFb[:]), sq, 16)
        p.barrier()
        with ExitStack() as es2:
            ps_tb = [p.psum(es2, [128, 1], BF16) for _ in range(2)]
            base = dict(p.C)
            with p.block() as blk:
                @blk.tensor
                def _(t):
                    for oc in range(32):
                        if oc >= 2:
                            t.wait_ge(p.S["act"], base["act"] + oc - 1)
                        p.inc(t.transpose(ps_tb[oc % 2][:], brow[0:1, oc * 128:(oc + 1) * 128], p.eye[0:1, 0:1]), "pe")
                @blk.scalar
                def _(a):
                    for oc in range(32):
                        a.wait_ge(p.S["pe"], base["pe"] + oc + 1)
                        p.inc(a.activation(b1t[:, oc:oc + 1], ps_tb[oc % 2][:], AF.Identity, scale=1.0), "act")
            with p.block() as blk:
                @blk.vector
                def _(v):
                    v.wait_ge(p.S["act"], p.C["act"])
                    v.wait_ge(p.S["dma"], p.C["dma"])
                    p.fence(v, "dve")
                    p.inc(v.tensor_tensor(b1t[:], b1t[:], bs[:, 0:32], ALU.add), "dve")
                    p.inc(v.tensor_tensor(b2t[:], bs[:, 32:40], bs[:, 40:48], ALU.add), "dve")
        p.barrier()
        sqs = p.take_dq(2, sw=True)
        with ExitStack() as es2:
            ps_t2 = [p.psum(es2, [128, 128], BF16) for _ in range(2)]
            tst = [p.sbuf(es2, [128, 128], BF16) for _ in range(2)]
            for ooc in range(8):
                with p.block() as blk:
                    @blk.sync
                    def _(s):
                        s.wait_ge(p.S["pe"], p.C["pe"])
                        s.wait_ge(p.S[sq], p.C[sq])
                        p.inc(s.dma_start(out=WFb[:, 0:4096], in_=w2tmp[ooc]), wq[0], 16)
                    @blk.vector
                    def _(v):
                        v.wait_ge(p.S[wq[0]], p.C[wq[0]])
                base2 = dict(p.C)
                with p.block() as blk:
                    @blk.tensor
                    def _(t):
                        for hc in range(32):
                            if hc >= 2:
                                t.wait_ge(p.S["dve"], base2["dve"] + hc - 1)
                            p.inc(t.transpose(ps_t2[hc % 2][:], WFb[:, hc * 128:(hc + 1) * 128], p.eye[:]), "pe")
                    @blk.vector
                    def _(v):
                        for hc in range(32):
                            v.wait_ge(p.S["pe"], base2["pe"] + hc + 1)
                            if hc >= 2:
                                v.wait_ge(p.S[sqs[hc % 2]], base2[sqs[hc % 2]] + 16 * (hc // 2))
                            elif base2[sqs[hc % 2]] > 0:
                                v.wait_ge(p.S[sqs[hc % 2]], base2[sqs[hc % 2]])
                            p.fence(v, "dve")
                            p.inc(v.tensor_copy(tst[hc % 2][:], ps_t2[hc % 2][:]), "dve")
                    @blk.gpsimd
                    def _(g):
                        for hc in range(32):
                            g.wait_ge(p.S["dve"], base2["dve"] + hc + 1)
                            p.inc(g.dma_start(out=w2p[ooc, :, hc * 128:(hc + 1) * 128], in_=tst[hc % 2][:]), sqs[hc % 2], 16)
        p.barrier()
    SEG = 512
    with ExitStack() as es:
        Hseg = p.sbuf(es, [128, FCH * SEG], BF16)
        G = p.sbuf(es, [128, 32 * SEG], BF16)
        Yseg = p.sbuf(es, [128, FCH * SEG], F32)
        for sg in range(TOK // SEG):
            tok0 = sg * SEG
            emit_ln(p, p.X, dst=Hseg, ntok=SEG, src_tok0=tok0,
                    dst_stride=SEG, dst_tok0=0, scratch=Yseg)

            def epi1(a, oc, j, psv):
                return a.activation(G[:, oc * SEG:(oc + 1) * SEG], psv,
                                    AF.Gelu, bias=b1t[:, oc:oc + 1], scale=1.0)
            emit_gemm(p, lambda oc, hh: w1p[oc, :, :], 32, FCH,
                      lambda kc, j: Hseg[:, kc * SEG:(kc + 1) * SEG], epi1,
                      wdtype=BF16, kcw=8, nbuf=1)

            def epi2(a, oc, j, psv):
                return a.activation(Yseg[:, oc * SEG:(oc + 1) * SEG], psv,
                                    AF.Identity, bias=b2t[:, oc:oc + 1], scale=1.0)
            emit_gemm(p, lambda oc, hh: w2p[oc, :, hh * 1024:(hh + 1) * 1024], FCH, 32,
                      lambda kc, j: G[:, kc * SEG:(kc + 1) * SEG], epi2,
                      wdtype=BF16, kcw=8, nbuf=1)

            emit_ln(p, Yseg, acc_into=p.X, ntok=SEG, src_stride=SEG,
                    dst_stride=TOK, dst_tok0=tok0, scratch=G[:].bitcast(F32))
    es0.close()


def prep_l1ffn(ins):
    wexp = np.ascontiguousarray(np.asarray(ins["L1_ffn_W_exp"]).astype(np.float32).reshape(8, 128, 4096))
    wcon = np.ascontiguousarray(np.asarray(ins["L1_ffn_W_con"]).astype(np.float32).T.reshape(8, 128, 4096))
    up = np.asarray(ins["L1_ffn_up_weights"]).astype(np.float32)
    dn = np.asarray(ins["L1_ffn_down_weights"]).astype(np.float32)
    upc = np.ascontiguousarray(up.transpose(0, 2, 3, 1).reshape(12, 4, 2048))
    dnt = dn.transpose(0, 3, 2, 1).reshape(12, 4, 2048)
    dnc = np.ascontiguousarray(dnt[::-1])
    bexp = np.ascontiguousarray(np.asarray(ins["L1_ffn_b_exp"]).astype(np.float32).reshape(1, 4096))
    biases = np.concatenate([bias_tiles(ins["L1_ffn_up_bias"]),
                             bias_tiles(ins["L1_ffn_b_con"]),
                             bias_tiles(ins["L1_ffn_down_bias"])], axis=1)
    return {"L1f_wexp": wexp, "L1f_wcon": wcon, "L1f_upc": upc, "L1f_dnc": dnc,
            "L1f_bexp": bexp, "L1f_bias": np.ascontiguousarray(biases)}


def decl_l1ffn(nc):
    return dict(
        wexp_ap=nc.dram_tensor("L1f_wexp", [8, 128, 4096], F32, kind="ExternalInput").ap(),
        wcon_ap=nc.dram_tensor("L1f_wcon", [8, 128, 4096], F32, kind="ExternalInput").ap(),
        upc_ap=nc.dram_tensor("L1f_upc", [12, 4, 2048], F32, kind="ExternalInput").ap(),
        dnc_ap=nc.dram_tensor("L1f_dnc", [12, 4, 2048], F32, kind="ExternalInput").ap(),
        bexp_ap=nc.dram_tensor("L1f_bexp", [1, 4096], F32, kind="ExternalInput").ap(),
        biases_ap=nc.dram_tensor("L1f_bias", [128, 48], F32, kind="ExternalInput").ap(),
    )


BLOCKS = [
    ("l0attn", decl_l0attn, emit_l0attn, prep_l0attn, True, False),
    ("l0ffn", decl_l0ffn, emit_l0ffn, prep_l0ffn, True, True),
    ("l1attn", decl_l1attn, emit_l1attn, prep_l1attn, False, False),
    ("l1ffn", decl_l1ffn, emit_l1ffn, prep_l1ffn, True, True),
    ("l2attn", decl_l2attn, emit_l2attn, prep_l2attn, False, False),
    ("l2ffn", decl_l2ffn, emit_l2ffn, prep_l2ffn, False, False),
    ("l3attn", decl_l3attn, emit_l3attn, prep_l3attn, True, False),
    ("l3ffn", decl_l3ffn, emit_l3ffn, prep_l3ffn, False, False),
]

_prog_cache = {}


def build_block_program(idx):
    if idx in _prog_cache:
        return _prog_cache[idx]
    name, decl, emit, _, use_eye, use_eyef = BLOCKS[idx]
    nc = bass.Bass("TRN2", target_bir_lowering=False)
    x_in = nc.dram_tensor("x", [D, TOK], F32, kind="ExternalInput").ap()
    aux_in = nc.dram_tensor("aux", [128, AUXC], F32, kind="ExternalInput").ap()
    sel_in = nc.dram_tensor("sel", [128, 64], BF16, kind="ExternalInput").ap()
    rowsel_in = nc.dram_tensor("rowsel", [8, 512], F32, kind="ExternalInput").ap()
    eye_in = nc.dram_tensor("eye", [128, 128], BF16, kind="ExternalInput").ap() if use_eye else None
    eyef_in = nc.dram_tensor("eyef", [128, 128], F32, kind="ExternalInput").ap() if use_eyef else None
    x_out = nc.dram_tensor("xo", [D, TOK], F32, kind="ExternalOutput").ap()
    daps = decl(nc)
    with ExitStack() as es:
        p = Prog(nc, es)
        with nc.Block() as blk:
            p.blk = blk
            emit_setup(p, x_in, aux_in, sel_in, rowsel_in, eye_in=eye_in, eyef_in=eyef_in)
            emit(p, **daps)
            emit_store_x(p, x_out)
        p.blk = None
    _prog_cache[idx] = nc
    return nc


def kernel(**inputs):
    consts = {"aux": host_aux(), "sel": host_sel(),
              "rowsel": host_rowsel().astype(np.float32),
              "eye": host_eye(), "eyef": np.eye(128, dtype=np.float32)}
    xs = [feat_major(np.asarray(inputs["L0_attn_x"])[2 * c:2 * c + 2])
          for c in range(NCORES)]
    for idx, (name, decl, emit, prep, use_eye, use_eyef) in enumerate(BLOCKS):
        nc = build_block_program(idx)
        wk = prep(inputs)
        in_maps = []
        for c in range(NCORES):
            m = dict(consts)
            if not use_eye:
                m.pop("eye", None)
            if not use_eyef:
                m.pop("eyef", None)
            m = {k: v for k, v in m.items()}
            m.update(wk)
            m["x"] = xs[c]
            in_maps.append(m)
        res = run_bass_kernel_spmd(nc, in_maps, core_ids=list(range(NCORES)))
        xs = [np.asarray(res.results[c]["xo"]) for c in range(NCORES)]
    out = np.empty((B, T, D), dtype=np.float32)
    for c in range(NCORES):
        out[2 * c:2 * c + 2] = xs[c].T.reshape(2, T, D)
    return out


_fused_cache = []


def build_fused_program():
    if _fused_cache:
        return _fused_cache[0]
    nc = bass.Bass("TRN2", target_bir_lowering=False)
    x_in = nc.dram_tensor("x", [D, TOK], F32, kind="ExternalInput").ap()
    aux_in = nc.dram_tensor("aux", [128, AUXC], F32, kind="ExternalInput").ap()
    sel_in = nc.dram_tensor("sel", [128, 64], BF16, kind="ExternalInput").ap()
    rowsel_in = nc.dram_tensor("rowsel", [8, 512], F32, kind="ExternalInput").ap()
    eye_in = nc.dram_tensor("eye", [128, 128], BF16, kind="ExternalInput").ap()
    eyef_in = nc.dram_tensor("eyef", [128, 128], F32, kind="ExternalInput").ap()
    x_out = nc.dram_tensor("xo", [D, TOK], F32, kind="ExternalOutput").ap()
    alldaps = []
    for name, decl, emit, _, _, _ in BLOCKS:
        alldaps.append(decl(nc))
    with ExitStack() as es:
        p = Prog(nc, es)
        with nc.Block() as blk:
            p.blk = blk
            with nc.named_scope("setup"):
                emit_setup(p, x_in, aux_in, sel_in, rowsel_in, eye_in=eye_in, eyef_in=eyef_in)
            for (name, decl, emit, _, _, _), daps in zip(BLOCKS, alldaps):
                with nc.named_scope(name):
                    emit(p, **daps)
            with nc.named_scope("store"):
                emit_store_x(p, x_out)
        p.blk = None
    _fused_cache.append(nc)
    return nc


def kernel_fused(**inputs):
    consts = {"aux": host_aux(), "sel": host_sel(),
              "rowsel": host_rowsel().astype(np.float32),
              "eye": host_eye(), "eyef": np.eye(128, dtype=np.float32)}
    wk = {}
    for name, decl, emit, prep, _, _ in BLOCKS:
        wk.update(prep(inputs))
    nc = build_fused_program()
    xs = [feat_major(np.asarray(inputs["L0_attn_x"])[2 * c:2 * c + 2])
          for c in range(NCORES)]
    in_maps = []
    for c in range(NCORES):
        m = dict(consts)
        m.update(wk)
        m["x"] = xs[c]
        in_maps.append(m)
    res = run_bass_kernel_spmd(nc, in_maps, core_ids=list(range(NCORES)))
    out = np.empty((B, T, D), dtype=np.float32)
    for c in range(NCORES):
        out[2 * c:2 * c + 2] = np.asarray(res.results[c]["xo"]).T.reshape(2, T, D)
    return out


_orig_kernel_unfused = kernel

def kernel(**inputs):
    import os as _os
    if _os.environ.get("MK_UNFUSED", "0") == "1":
        return _orig_kernel_unfused(**inputs)
    return kernel_fused(**inputs)


# version 08 (final)
# speedup vs baseline: 2.1396x; 1.1217x over previous
import numpy as np
import ml_dtypes
from contextlib import ExitStack
import concourse.bass as bass
import concourse.mybir as mybir
from concourse.bass_utils import run_bass_kernel_spmd

F32 = mybir.dt.float32
BF16 = mybir.dt.bfloat16
U8 = mybir.dt.uint8
AF = mybir.ActivationFunctionType
ALU = mybir.AluOpType

NCORES = 8
WRN = 4
WRB = 2816
B, T, D = 16, 2048, 1024
SEQ = 2
TOK = SEQ * T
FCH = D // 128
TSEG = 512
NSEG = TOK // TSEG
MAGIC = 12582912.0

ENGINES = ("sync", "tensor", "vector", "scalar", "gpsimd")
CENG = ("tensor", "vector", "scalar", "gpsimd")



def tile_lhsT(W):
    K, M = W.shape
    assert K % 128 == 0 and M % 128 == 0, (K, M)
    t = W.reshape(K // 128, 128, M // 128, 128).transpose(2, 1, 0, 3)
    return np.ascontiguousarray(t.reshape(M // 128, 128, K))


def bias_tiles(b):
    M = b.shape[0]
    return np.ascontiguousarray(np.asarray(b).reshape(M // 128, 128).T.astype(np.float32))


def feat_major(x):
    x = np.asarray(x).reshape(-1, x.shape[-1])
    return np.ascontiguousarray(x.T.astype(np.float32))


def host_sel():
    a = np.zeros((128, 8, 8), dtype=np.float32)
    for j in range(8):
        a[:, j, j] = 1.0
    return a.reshape(128, 64).astype(ml_dtypes.bfloat16)


def host_rowsel():
    a = np.zeros((8, 4, 128), dtype=np.float32)
    for j in range(4):
        a[j, j, :] = 1.0
    return a.reshape(8, 512)


def host_eye():
    return np.eye(128, dtype=np.float32).astype(ml_dtypes.bfloat16)


AUX_ZERO, AUX_ONE, AUX_EPS5, AUX_EPS3, AUX_WKVEPS, AUX_HALFPI = 0, 1, 2, 3, 4, 5
AUXC = 8

def host_aux():
    a = np.zeros((128, AUXC), dtype=np.float32)
    a[:, AUX_ONE] = 1.0
    a[:, AUX_EPS5] = 1e-5
    a[:, AUX_EPS3] = 1e-3
    a[:, AUX_WKVEPS] = 1e-8
    a[:, AUX_HALFPI] = np.pi / 2
    return a



class Prog:
    def __init__(self, nc, es):
        self.nc, self.es = nc, es
        self.uid = 0
        self.S = {}
        self.C = {}
        for s in ("dma", "dmo", "pe", "dve", "act", "pool", "bar", "sp",
                  "dq0", "dq1", "dq2", "dq3", "dq4", "dq5", "dq6", "dq7",
                  "wr0", "wr1", "wr2", "wr3", "wr4", "wr5"):
            self.S[s] = es.enter_context(nc.semaphore(s))
            self.C[s] = 0
        self.X = None
        self.aux = None
        self.sel = None
        self.rowsel = None
        self.dq_rr = 0
        self.dq_sw_rr = 0
        self.blk = None
        self.lastjoin = {e: {} for e in ENGINES}
        self.wring = None
        self.wr_pos = 0
        self.wr_guard = [None] * WRN

    def wr_init(p):
        if p.wring is None:
            p.wring = [p.sbuf(p.es, [128, WRB], U8) for _ in range(WRN)]

    def wr_plan(p, consume):
        i = p.wr_pos % WRN
        p.wr_pos += 1
        sem = f"wr{i}"
        p.C[sem] += 16
        prev = p.wr_guard[i]
        p.wr_guard[i] = consume
        return i, sem, p.C[sem], prev

    def block(self):
        if self.blk is not None:
            return _SharedBlk(self)
        return self.nc.Block()

    def take_dq(self, k, sw=False):
        if sw:
            out = [f"dq{6 + (self.dq_sw_rr + i) % 2}" for i in range(k)]
            self.dq_sw_rr = (self.dq_sw_rr + k) % 2
            return out
        out = [f"dq{(self.dq_rr + i) % 6}" for i in range(k)]
        self.dq_rr = (self.dq_rr + k) % 6
        return out

    def name(self, pfx):
        self.uid += 1
        return f"{pfx}{self.uid}"

    def sbuf(self, es, shape, dtype):
        return es.enter_context(self.nc.sbuf_tensor(self.name("t"), shape, dtype))

    def psum(self, es, shape, dtype=F32):
        return es.enter_context(self.nc.psum_tensor(self.name("p"), shape, dtype))

    def inc(self, instr, s, n=1):
        instr.then_inc(self.S[s], n)
        self.C[s] += n

    def fence(self, q, s):
        q.wait_ge(self.S[s], self.C[s])

    def barrier(p, engines=ENGINES):
        nc = p.nc
        ALLSEMS = ("sp", "pe", "dve", "act", "pool", "dma", "dmo",
                   "dq0", "dq1", "dq2", "dq3", "dq4", "dq5", "dq6", "dq7")
        snap = {nm: p.C[nm] for nm in ALLSEMS}

        def _emit(blk):
            for eng in engines:
                lj = p.lastjoin[eng]
                todo = [(nm, snap[nm]) for nm in ALLSEMS
                        if snap[nm] > lj.get(nm, 0)]
                if not todo:
                    continue

                def mk(items):
                    def _(q):
                        for nm, v in items:
                            q.wait_ge(p.S[nm], v)
                    return _
                getattr(blk, eng)(mk(todo))
                for nm, v in todo:
                    lj[nm] = v

        if p.blk is not None:
            _emit(p.blk)
        else:
            with nc.Block() as blk:
                _emit(blk)


class _SharedBlk:
    def __init__(self, p):
        self.p = p

    def __enter__(self):
        return self.p.blk

    def __exit__(self, exc_type, exc_val, exc_tb):
        if exc_type is None:
            self.p.barrier(CENG)
        return False


def emit_setup(p: Prog, x_in, aux_in, sel_in, rowsel_in, eye_in=None, eyef_in=None, load_x=True):
    nc = p.nc
    es = p.es
    p.X = p.sbuf(es, [128, FCH * TOK], F32)
    p.aux = p.sbuf(es, [128, AUXC], F32)
    p.sel = p.sbuf(es, [128, 64], BF16)
    p.rowsel = p.sbuf(es, [8, 512], F32)
    p.eye = p.sbuf(es, [128, 128], BF16) if eye_in is not None else None
    p.eyef = p.sbuf(es, [128, 128], F32) if eyef_in is not None else None
    p.wr_init()
    with p.block() as blk:
        @blk.sync
        def _(s):
            if load_x:
                p.inc(s.dma_start(out=p.X[:].rearrange("p (c t) -> p c t", c=FCH), in_=x_in.rearrange("(c p) t -> p c t", p=128)), "dma", 16)
            p.inc(s.dma_start(out=p.aux[:], in_=aux_in), "dma", 16)
            p.inc(s.dma_start(out=p.sel[:], in_=sel_in), "dma", 16)
            p.inc(s.dma_start(out=p.rowsel[:], in_=rowsel_in), "dma", 16)
            if eye_in is not None:
                p.inc(s.dma_start(out=p.eye[:], in_=eye_in), "dma", 16)
            if eyef_in is not None:
                p.inc(s.dma_start(out=p.eyef[:], in_=eyef_in), "dma", 16)
        @blk.vector
        def _(v):
            v.wait_ge(p.S["dma"], p.C["dma"])
    p.barrier()


def emit_store_x(p: Prog, x_out):
    nc = p.nc
    p.barrier()
    with p.block() as blk:
        @blk.sync
        def _(s):
            p.inc(s.dma_start(out=x_out.rearrange("(c p) t -> p c t", p=128), in_=p.X[:].rearrange("p (c t) -> p c t", c=FCH)), "dmo", 16)
            s.wait_ge(p.S["dmo"], p.C["dmo"])



EPS_COL = {1e-5: AUX_EPS5, 1e-3: AUX_EPS3}

def emit_ln(p: Prog, src, dst=None, dst2=None, acc_into=None, eps=1e-5,
            nch=FCH, src_stride=None, src_tok0=0, ntok=TOK,
            dst_stride=None, dst_tok0=0, gamma=None, beta=None, gb_col0=0, beta_col0=None,
            scratch=None):
    nc = p.nc
    sstride = src_stride if src_stride is not None else TOK
    dstride = dst_stride if dst_stride is not None else sstride
    seglen = min(TSEG, ntok)
    nseg = ntok // seglen
    assert ntok % seglen == 0 and nseg <= 4
    invn = 1.0 / (nch * 128)
    epsap = p.aux[0:8, EPS_COL[eps]:EPS_COL[eps] + 1]
    if beta_col0 is None:
        beta_col0 = gb_col0

    plain = (gamma is None and dst2 is None and acc_into is None and dst is not None)
    gsimple = (gamma is not None and dst is not None and dst2 is None and acc_into is None)
    acconly = (gamma is None and dst is None and dst2 is None and acc_into is not None)
    nwr = (0 if (plain or gsimple) else
           (1 if dst is not None else 0) + (1 if dst2 is not None else 0)
           + (1 if acc_into is not None else 0))
    chain_len = 2 + (1 if gamma is not None else 0) + nwr
    if plain:
        napply = 2
    elif gsimple:
        napply = 2 + nch
    elif acconly and nch % 2 == 0:
        napply = 3 * (nch // 2)
    else:
        napply = nch * chain_len

    with ExitStack() as es:
        s1 = p.psum(es, [128, 512])
        s2 = p.psum(es, [128, 512])
        bc = [p.psum(es, [128, 1024]) for _ in range(2)]
        nscr = 1 if plain else (3 if gamma is not None else 2)
        if scratch is not None:
            o = 0
            scr = [scratch[:, o + i * seglen:o + (i + 1) * seglen] for i in range(nscr)]
            scrpair = scratch[:, 0:2 * seglen] if nscr >= 2 else None
            o += nscr * seglen
            rows = scratch[0:8, o:o + 3 * seglen]
            o += 3 * seglen
            h = seglen // 2
            xb = [scratch[:, o + i * h:o + (i + 1) * h].bitcast(BF16) for i in range(2)]
            o += 2 * h
            xq = [scratch[:, o + i * h:o + (i + 1) * h].bitcast(BF16) for i in range(2)]
        else:
            xb = [p.sbuf(es, [128, seglen], BF16) for _ in range(2)]
            xq = [p.sbuf(es, [128, seglen], BF16) for _ in range(2)]
            rows = p.sbuf(es, [8, 1536], F32)
            scrbuf = p.sbuf(es, [128, nscr * seglen], F32)
            scr = [scrbuf[:, i * seglen:(i + 1) * seglen] for i in range(nscr)]
            scrpair = scrbuf[:, 0:2 * seglen] if nscr >= 2 else None
        X1 = rows[0:8, 0:seglen]
        X2 = rows[0:8, seglen:2 * seglen]
        X3 = rows[0:8, 2 * seglen:3 * seglen]
        R_A, R_B = X3, X2
        base = dict(p.C)
        N = nseg * nch

        def src_ap(c, g):
            t0 = src_tok0 + g * seglen
            return src[:, c * sstride + t0: c * sstride + t0 + seglen]

        def dst_ap(buf, c, g):
            t0 = dst_tok0 + g * seglen
            return buf[:, c * dstride + t0: c * dstride + t0 + seglen]

        with p.block() as blk:
            @blk.scalar
            def _(a):
                for i in range(N):
                    if i >= 2:
                        a.wait_ge(p.S["pe"], base["pe"] + i - 1)
                    p.inc(a.activation(xq[i % 2][:], src_ap(i % nch, i // nch), AF.Square), "act")
                a.wait_ge(p.S["dve"], base["dve"] + N + 3)
                p.inc(a.activation(X2, X3, AF.Sqrt, bias=epsap, scale=1.0), "act")

            @blk.tensor
            def _(t):
                for i in range(N):
                    g = i // nch
                    t.wait_ge(p.S["dve"], base["dve"] + i + 1)
                    t.wait_ge(p.S["act"], base["act"] + i + 1)
                    first, last = (i == 0), (i == N - 1)
                    selj = p.sel[:, g * 8:(g + 1) * 8]
                    t.matmul(s1[0:8, 0:seglen], selj, xb[i % 2][:], start=first, stop=last)
                    p.inc(t.matmul(s2[0:8, 0:seglen], selj, xq[i % 2][:], start=first, stop=last), "pe")
                for g in range(nseg):
                    t.wait_ge(p.S["dve"], base["dve"] + N + 5 + (0 if g < 2 else (g - 1) * napply))
                    rj = p.rowsel[0:8, g * 128:(g + 1) * 128]
                    p.inc(t.matmul(bc[g % 2][:, 0:seglen], rj, X3, start=True, stop=True), "pe")
                    p.inc(t.matmul(bc[g % 2][:, 512:512 + seglen], rj, X2, start=True, stop=True), "pe")

            @blk.vector
            def _(v):
                for i in range(N):
                    if i >= 2:
                        v.wait_ge(p.S["pe"], base["pe"] + i - 1)
                    p.inc(v.tensor_copy(xb[i % 2][:], src_ap(i % nch, i // nch)), "dve")
                v.wait_ge(p.S["pe"], base["pe"] + N)
                p.inc(v.tensor_scalar(X1, s1[0:8, 0:seglen], invn, None, ALU.mult), "dve")
                p.fence(v, "dve")
                p.inc(v.tensor_tensor(X2, X1, X1, ALU.mult), "dve")
                p.fence(v, "dve")
                p.inc(v.scalar_tensor_tensor(X3, s2[0:8, 0:seglen], invn, X2, ALU.mult, ALU.subtract), "dve")
                v.wait_ge(p.S["act"], base["act"] + N + 1)
                p.inc(v.reciprocal(X3, X2), "dve")
                p.fence(v, "dve")
                p.inc(v.scalar_tensor_tensor(X2, X1, -1.0, X3, ALU.mult, ALU.mult), "dve")
                def wide3(buf, stride, tok0, g, c0, ncw):
                    t0 = tok0 + g * seglen
                    a3 = buf[:].rearrange("p (c t) -> p c t", t=stride)
                    return a3[:, c0:c0 + ncw, t0:t0 + seglen]

                for g in range(nseg):
                    v.wait_ge(p.S["pe"], base["pe"] + N + 2 * (g + 1))
                    A = bc[g % 2][:, 0:seglen]
                    Bv = bc[g % 2][:, 512:512 + seglen]
                    if plain or gsimple:
                        d3 = wide3(dst, dstride, dst_tok0, g, 0, nch)
                        s3 = wide3(src, sstride, src_tok0, g, 0, nch)
                        A3 = A.unsqueeze(1).broadcast_to([128, nch, seglen])
                        B3 = Bv.unsqueeze(1).broadcast_to([128, nch, seglen])
                        p.fence(v, "dve")
                        p.inc(v.tensor_tensor(d3, s3, A3, ALU.mult), "dve")
                        p.fence(v, "dve")
                        p.inc(v.tensor_tensor(d3, d3, B3, ALU.add), "dve")
                        if gsimple:
                            for c in range(nch):
                                p.fence(v, "dve")
                                p.inc(v.tensor_scalar(dst_ap(dst, c, g), dst_ap(dst, c, g),
                                                      gamma[:, gb_col0 + c:gb_col0 + c + 1],
                                                      beta[:, beta_col0 + c:beta_col0 + c + 1], ALU.mult, ALU.add), "dve")
                        continue
                    if acconly and nch % 2 == 0:
                        s2w = scrpair.rearrange("p (c t) -> p c t", t=seglen)
                        A2 = A.unsqueeze(1).broadcast_to([128, 2, seglen])
                        B2 = Bv.unsqueeze(1).broadcast_to([128, 2, seglen])
                        for cp in range(nch // 2):
                            s3 = wide3(src, sstride, src_tok0, g, 2 * cp, 2)
                            x3 = wide3(acc_into, dstride, dst_tok0, g, 2 * cp, 2)
                            p.fence(v, "dve")
                            p.inc(v.tensor_tensor(s2w, s3, A2, ALU.mult), "dve")
                            p.fence(v, "dve")
                            p.inc(v.tensor_tensor(s2w, s2w, B2, ALU.add), "dve")
                            p.fence(v, "dve")
                            p.inc(v.tensor_tensor(x3, x3, s2w, ALU.add), "dve")
                        continue
                    for c in range(nch):
                        p.fence(v, "dve")
                        p.inc(v.tensor_tensor(scr[0][:], src_ap(c, g), A, ALU.mult), "dve")
                        p.fence(v, "dve")
                        if gamma is not None:
                            p.inc(v.tensor_tensor(scr[1][:], scr[0][:], Bv, ALU.add), "dve")
                            p.fence(v, "dve")
                            r = scr[2]
                            p.inc(v.tensor_scalar(scr[2][:], scr[1][:],
                                                  gamma[:, gb_col0 + c:gb_col0 + c + 1],
                                                  beta[:, beta_col0 + c:beta_col0 + c + 1], ALU.mult, ALU.add), "dve")
                        else:
                            p.inc(v.tensor_tensor(scr[1][:], scr[0][:], Bv, ALU.add), "dve")
                            r = scr[1]
                        p.fence(v, "dve")
                        writes = []
                        if dst is not None:
                            writes.append(("copy", dst))
                        if dst2 is not None:
                            writes.append(("copy", dst2))
                        if acc_into is not None:
                            writes.append(("acc", acc_into))
                        assert writes
                        for wi, (kind, buf) in enumerate(writes):
                            if kind == "copy":
                                ins = v.tensor_copy(dst_ap(buf, c, g), r[:])
                            else:
                                ins = v.tensor_tensor(dst_ap(buf, c, g), dst_ap(buf, c, g), r[:], ALU.add)
                            if wi == len(writes) - 1:
                                p.inc(ins, "dve")



ENG2SEM = {"vector": "dve", "scalar": "act", "gpsimd": "pool", "tensor": "pe"}

def emit_gemm(p: Prog, w_oc_ap, och, kch, rhs_fn, epi, njobs=1,
              wdtype=F32, epi_eng="scalar", cast_eng=None, wcast=None, kcw=8,
              wraw_mult=1, nbuf=2):
    nc = p.nc
    kcw = min(kcw, kch)
    assert kch % kcw == 0
    nw = kch // kcw
    L = rhs_fn(0, 0).shape[-1]
    assert L <= 512 and njobs <= 2
    nocast = (wcast is None and wdtype == BF16)
    tile_elems = wraw_mult * kcw * 128
    tile_bytes = tile_elems * {F32: 4, BF16: 2, U8: 1}[wdtype]
    assert tile_bytes <= WRB, (tile_bytes, WRB)
    if cast_eng is None:
        cast_eng = "vector" if epi_eng != "vector" else "gpsimd"
    assert cast_eng != epi_eng
    episem = ENG2SEM[epi_eng]
    castsem = ENG2SEM[cast_eng]
    ntiles = och * nw
    p.wr_init()
    with ExitStack() as es:
        wbf = None if nocast else [p.sbuf(es, [128, kcw * 128], BF16) for _ in range(nbuf)]
        ps = [p.psum(es, [128, 512]) for _ in range(2)]
        base = dict(p.C)
        plan = []
        for l in range(ntiles):
            consume = (("pe", base["pe"] + l + 1) if nocast
                       else (castsem, base[castsem] + l + 1))
            plan.append(p.wr_plan(consume))

        def wview(l):
            i = plan[l][0]
            return p.wring[i][:].bitcast(wdtype)[:, 0:tile_elems]

        with p.block() as blk:
            @blk.sync
            def _(s):
                for l in range(ntiles):
                    i, sem, cnt, prev = plan[l]
                    if prev is not None:
                        s.wait_ge(p.S[prev[0]], prev[1])
                    s.dma_start(out=wview(l), in_=w_oc_ap(l // nw, l % nw)
                                ).then_inc(p.S[sem], 16)

            def cast_sec(q):
                copyf = q.copy if cast_eng == "scalar" else q.tensor_copy
                for l in range(ntiles):
                    i, sem, cnt, prev = plan[l]
                    q.wait_ge(p.S[sem], cnt)
                    if l >= nbuf:
                        q.wait_ge(p.S["pe"], base["pe"] + l - nbuf + 1)
                    if wcast is not None:
                        p.inc(wcast(q, wbf[l % nbuf][:], wview(l)), castsem)
                    else:
                        p.inc(copyf(wbf[l % nbuf][:], wview(l)), castsem)

            def epi_sec(q):
                for oc in range(och):
                    for j in range(njobs):
                        idx = oc * njobs + j
                        q.wait_ge(p.S["pe"], base["pe"] + (oc + 1) * nw)
                        p.inc(epi(q, oc, j, ps[idx % 2][:, 0:L]), episem)

            if not nocast:
                getattr(blk, cast_eng)(cast_sec)
            getattr(blk, epi_eng)(epi_sec)

            @blk.tensor
            def _(t):
                for oc in range(och):
                    for h in range(nw):
                        l = oc * nw + h
                        if nocast:
                            i, sem, cnt, prev = plan[l]
                            t.wait_ge(p.S[sem], cnt)
                            wt = wview(l)
                        else:
                            t.wait_ge(p.S[castsem], base[castsem] + l + 1)
                            wt = wbf[l % nbuf][:]
                        for j in range(njobs):
                            idx = oc * njobs + j
                            if h == 0 and idx >= 2:
                                t.wait_ge(p.S[episem], base[episem] + idx - 1)
                            for k in range(kcw):
                                kc = h * kcw + k
                                p_ins = t.matmul(ps[idx % 2][:, 0:L],
                                                 wt[:, k * 128:(k + 1) * 128],
                                                 rhs_fn(kc, j),
                                                 start=(kc == 0), stop=(kc == kch - 1))
                        p.inc(p_ins, "pe")


def emit_precast(p: Prog, pairs, obuf=None):
    nc = p.nc
    p.wr_init()
    split = []
    for sap, dap in pairs:
        C = sap.shape[-1]
        for o in range(0, C, 512):
            w = min(512, C - o)
            split.append((sap[:, o:o + w], dap[:, o:o + w]))
    pairs = split
    NOB = 3
    with ExitStack() as es:
        if obuf is not None:
            obf = obuf[:, 0:NOB * 256].bitcast(BF16)
            ob = [obf[:, i * 512:(i + 1) * 512] for i in range(NOB)]
        else:
            ob = [p.sbuf(es, [128, 512], BF16)[:] for _ in range(NOB)]
        oq = p.take_dq(NOB)
        base = dict(p.C)
        n = len(pairs)
        plan = [p.wr_plan(("dve", base["dve"] + l + 1)) for l in range(n)]
        ocnt = []
        for l in range(n):
            p.C[oq[l % NOB]] += 16
            ocnt.append(p.C[oq[l % NOB]])

        with p.block() as blk:
            @blk.sync
            def _(s):
                for l in range(n + NOB):
                    if l < n:
                        i, sem, cnt, prev = plan[l]
                        if prev is not None:
                            s.wait_ge(p.S[prev[0]], prev[1])
                        C = pairs[l][0].shape[-1]
                        s.dma_start(out=p.wring[i][:].bitcast(F32)[:, 0:C],
                                    in_=pairs[l][0]).then_inc(p.S[sem], 16)
                    if l >= NOB:
                        w = l - NOB
                        s.wait_ge(p.S["dve"], base["dve"] + w + 1)
                        C = pairs[w][0].shape[-1]
                        s.dma_start(out=pairs[w][1], in_=ob[w % NOB][:, 0:C]
                                    ).then_inc(p.S[oq[w % NOB]], 16)

            @blk.vector
            def _(v):
                for l in range(n):
                    i, sem, cnt, prev = plan[l]
                    v.wait_ge(p.S[sem], cnt)
                    if l >= NOB:
                        v.wait_ge(p.S[oq[l % NOB]], ocnt[l - NOB])
                    C = pairs[l][0].shape[-1]
                    p.inc(v.tensor_copy(ob[l % NOB][:, 0:C],
                                        p.wring[i][:].bitcast(F32)[:, 0:C]), "dve")



def emit_l3ffn(p: Prog, w1_ap, b1_ap, w2_ap, b2_ap):
    nc = p.nc
    w1b = nc.dram_tensor("l3fw1b", [32, 128, 1024], BF16).ap()
    w2b = nc.dram_tensor("l3fw2b", [8, 128, 4096], BF16).ap()
    with ExitStack() as es:
        b1s = p.sbuf(es, [128, 32], F32)
        b2s = p.sbuf(es, [128, 8], F32)
        Hseg = p.sbuf(es, [128, FCH * TSEG], BF16)
        Mseg = p.sbuf(es, [128, 32 * TSEG], BF16)
        Yseg = p.sbuf(es, [128, FCH * TSEG], F32)
        p.barrier(("sync",))
        with p.block() as blk:
            @blk.sync
            def _(s):
                p.inc(s.dma_start(out=b1s[:], in_=b1_ap), "dma", 16)
                p.inc(s.dma_start(out=b2s[:], in_=b2_ap), "dma", 16)
        pairs = [(w1_ap[oc, :, 0:1024], w1b[oc, :, 0:1024]) for oc in range(32)]
        pairs += [(w2_ap[oc, :, j * 1024:(j + 1) * 1024],
                   w2b[oc, :, j * 1024:(j + 1) * 1024])
                  for oc in range(FCH) for j in range(4)]
        emit_precast(p, pairs)
        p.barrier(("sync",))
        for g in range(NSEG):
            emit_ln(p, p.X, dst=Hseg, ntok=TSEG, src_tok0=g * TSEG,
                    dst_stride=TSEG, dst_tok0=0, scratch=Yseg)

            def epi1(a, oc, j, psv):
                return a.activation(Mseg[:, oc * TSEG:(oc + 1) * TSEG], psv,
                                    AF.Relu, bias=b1s[:, oc:oc + 1], scale=1.0)
            emit_gemm(p, lambda oc, h: w1b[oc, :, h * 1024:(h + 1) * 1024], 32, FCH,
                      lambda kc, j: Hseg[:, kc * TSEG:(kc + 1) * TSEG], epi1, wdtype=BF16)

            def epi2(a, oc, j, psv):
                return a.activation(Yseg[:, oc * TSEG:(oc + 1) * TSEG], psv,
                                    AF.Identity, bias=b2s[:, oc:oc + 1], scale=1.0)
            emit_gemm(p, lambda oc, h: w2b[oc, :, h * 1024:(h + 1) * 1024], FCH, 32,
                      lambda kc, j: Mseg[:, kc * TSEG:(kc + 1) * TSEG], epi2, wdtype=BF16)

            emit_ln(p, Yseg, acc_into=p.X, ntok=TSEG, src_stride=TSEG,
                    dst_stride=TOK, dst_tok0=g * TSEG,
                    scratch=Mseg[:].bitcast(F32))


def prep_l3ffn(ins):
    return {
        "L3f_w1": tile_lhsT(ins["L3_ffn_W1"].astype(np.float32)),
        "L3f_b1": bias_tiles(ins["L3_ffn_b1"]),
        "L3f_w2": tile_lhsT(ins["L3_ffn_W2"].astype(np.float32)),
        "L3f_b2": bias_tiles(ins["L3_ffn_b2"]),
    }


def decl_l3ffn(nc):
    return dict(
        w1_ap=nc.dram_tensor("L3f_w1", [32, 128, 1024], F32, kind="ExternalInput").ap(),
        b1_ap=nc.dram_tensor("L3f_b1", [128, 32], F32, kind="ExternalInput").ap(),
        w2_ap=nc.dram_tensor("L3f_w2", [8, 128, 4096], F32, kind="ExternalInput").ap(),
        b2_ap=nc.dram_tensor("L3f_b2", [128, 8], F32, kind="ExternalInput").ap(),
    )


def emit_l2ffn(p: Prog, k0_ap, k1_ap, k2_ap, s3_ap):
    nc = p.nc
    SEG = 256
    NSG = TOK // SEG
    with ExitStack() as es:
        sbc = p.sbuf(es, [128, 3], F32)
        s3t = p.sbuf(es, [1, 3], F32)
        psb = p.psum(es, [128, 512])
        Hseg = p.sbuf(es, [128, FCH * SEG], BF16)
        A1 = p.sbuf(es, [128, 32 * SEG], BF16)
        A2 = p.sbuf(es, [128, 32 * SEG], BF16)
        Yseg = p.sbuf(es, [128, FCH * SEG], F32)
        base = dict(p.C)
        p.barrier(("sync",))
        with p.block() as blk:
            @blk.sync
            def _(s):
                p.inc(s.dma_start(out=s3t[:], in_=s3_ap), "dma", 16)
            @blk.tensor
            def _(t):
                t.wait_ge(p.S["dma"], p.C["dma"])
                p.inc(t.matmul(psb[:, 0:3], p.rowsel[0:1, 0:128], s3t[:], start=True, stop=True), "pe")
            @blk.vector
            def _(v):
                v.wait_ge(p.S["pe"], p.C["pe"])
                p.inc(v.tensor_scalar(sbc[:, 0:1], psb[:, 0:1], 4.0, None, ALU.mult), "dve")
                p.inc(v.tensor_scalar(sbc[:, 1:2], psb[:, 1:2], 2.0, None, ALU.mult), "dve")
                p.inc(v.tensor_scalar(sbc[:, 2:3], psb[:, 2:3], 1.0, None, ALU.mult), "dve")
        p.barrier()

        def subcast(q, dst, src):
            n = dst.shape[-1]
            return q.tensor_tensor(dst, src[:, 0:n], src[:, n:2 * n], ALU.subtract)

        def affcast(q, dst, src):
            return q.tensor_scalar(dst, src, 2.0, -1.0, ALU.mult, ALU.add)

        for g in range(NSG):
            emit_ln(p, p.X, dst=Hseg, ntok=SEG, src_tok0=g * SEG,
                    dst_stride=SEG, dst_tok0=0)

            def epi0(v, oc, j, psv):
                return v.tensor_scalar(A1[:, oc * SEG:(oc + 1) * SEG], psv,
                                       sbc[:, 0:1], 0.0, ALU.mult, ALU.max)
            emit_gemm(p, lambda oc, h: k0_ap[oc, :, :], 32, FCH,
                      lambda kc, j: Hseg[:, kc * SEG:(kc + 1) * SEG], epi0,
                      wdtype=U8, epi_eng="vector", cast_eng="gpsimd",
                      wcast=subcast, kcw=FCH, wraw_mult=2)

            def epi1(v, oc, j, psv):
                return v.tensor_scalar(A2[:, oc * SEG:(oc + 1) * SEG], psv,
                                       sbc[:, 1:2], 0.0, ALU.mult, ALU.max)
            emit_gemm(p, lambda oc, h: k1_ap[oc, :, h * 1024:(h + 1) * 1024], 32, 32,
                      lambda kc, j: A1[:, kc * SEG:(kc + 1) * SEG], epi1,
                      wdtype=U8, epi_eng="vector", cast_eng="gpsimd", wcast=affcast)

            def epi2(v, oc, j, psv):
                return v.tensor_scalar(Yseg[:, oc * SEG:(oc + 1) * SEG], psv,
                                       sbc[:, 2:3], None, ALU.mult)
            emit_gemm(p, lambda oc, h: k2_ap[oc, :, h * 1024:(h + 1) * 1024], FCH, 32,
                      lambda kc, j: A2[:, kc * SEG:(kc + 1) * SEG], epi2,
                      wdtype=U8, epi_eng="vector", cast_eng="gpsimd", wcast=affcast)

            emit_ln(p, Yseg, acc_into=p.X, ntok=SEG, src_stride=SEG,
                    dst_stride=TOK, dst_tok0=g * SEG)


def prep_l2ffn(ins):
    k0 = np.asarray(ins["L2_ffn_k0"]).astype(np.uint8)
    k1 = np.asarray(ins["L2_ffn_k1"]).astype(np.uint8)
    k2 = np.asarray(ins["L2_ffn_k2"]).astype(np.uint8)
    k0b = np.concatenate([tile_lhsT(k0[:1024]), tile_lhsT(k0[1024:])], axis=2)
    s3 = np.array([[ins["L2_ffn_s0"], ins["L2_ffn_s1"], ins["L2_ffn_s2"]]], dtype=np.float32)
    return {
        "L2f_k0": np.ascontiguousarray(k0b),
        "L2f_k1": tile_lhsT(k1),
        "L2f_k2": tile_lhsT(k2),
        "L2f_s3": s3,
    }


def decl_l2ffn(nc):
    return dict(
        k0_ap=nc.dram_tensor("L2f_k0", [32, 128, 2048], U8, kind="ExternalInput").ap(),
        k1_ap=nc.dram_tensor("L2f_k1", [32, 128, 4096], U8, kind="ExternalInput").ap(),
        k2_ap=nc.dram_tensor("L2f_k2", [8, 128, 4096], U8, kind="ExternalInput").ap(),
        s3_ap=nc.dram_tensor("L2f_s3", [1, 3], F32, kind="ExternalInput").ap(),
    )


from concourse.ap import AP as BassAP


def emit_l3attn(p: Prog, text_ap, wqkv_ap, gb_ap):
    nc = p.nc
    temp = nc.dram_tensor("l3bias", [16, 16, 16, 128, 128], BF16).ap()

    with ExitStack() as es:
        qraw = [p.sbuf(es, [128, 2048], F32) for _ in range(2)]
        qbf = [p.sbuf(es, [128, 2048], BF16) for _ in range(2)]
        pst = [p.psum(es, [128, 128], BF16) for _ in range(2)]
        stage = [p.sbuf(es, [128, 2048], BF16) for _ in range(2)]
        lq = p.take_dq(2)
        sq = p.take_dq(2, sw=True)
        base = dict(p.C)
        NIT = 256

        def qm_src(h, it, phi):
            baseoff = -2048 * h + 32752 - 128 * it - 16 * phi
            return BassAP(text_ap.tensor, baseoff, [[2048, 16], [1, 2048]])

        p.barrier(("sync",))
        with p.block() as blk:
            @blk.sync
            def _(s):
                for i in range(NIT):
                    h, it = i // 16, i % 16
                    if i >= 2:
                        s.wait_ge(p.S["dve"], base["dve"] + (i - 1) * 17 + 1)
                    for phi in range(8):
                        p.inc(s.dma_start(out=qraw[i % 2][phi * 16:(phi + 1) * 16, :],
                                          in_=qm_src(h, it, phi)), lq[i % 2], 16)
            @blk.vector
            def _(v):
                for i in range(NIT):
                    v.wait_ge(p.S[lq[i % 2]], base[lq[i % 2]] + 128 * (i // 2 + 1))
                    if i >= 2:
                        v.wait_ge(p.S["pe"], base["pe"] + 16 * (i - 1))
                    p.inc(v.tensor_copy(qbf[i % 2][:], qraw[i % 2][:]), "dve")
                    for j in range(16):
                        v.wait_ge(p.S["pe"], base["pe"] + 16 * i + j + 1)
                        if i >= 2 and j == 0:
                            v.wait_ge(p.S[sq[i % 2]], base[sq[i % 2]] + 16 * (i // 2))
                        p.inc(v.tensor_copy(stage[i % 2][:, j * 128:(j + 1) * 128], pst[j % 2][:]), "dve")
            @blk.tensor
            def _(t):
                for i in range(NIT):
                    t.wait_ge(p.S["dve"], base["dve"] + i * 17 + 1)
                    for j in range(16):
                        if j >= 2:
                            t.wait_ge(p.S["dve"], base["dve"] + i * 17 + 1 + j - 1)
                        elif i >= 1:
                            t.wait_ge(p.S["dve"], base["dve"] + i * 17)
                        p.inc(t.transpose(pst[j % 2][:], qbf[i % 2][:, j * 128:(j + 1) * 128], p.eye[:]), "pe")
            @blk.gpsimd
            def _(g):
                for i in range(NIT):
                    h, it = i // 16, i % 16
                    g.wait_ge(p.S["dve"], base["dve"] + (i + 1) * 17)
                    p.inc(g.dma_start(out=temp[h, it].rearrange("jt p i -> p jt i"), in_=stage[i % 2][:].rearrange("p (jt i) -> p jt i", jt=16)),
                          sq[i % 2], 16)
        p.barrier()

    ao_dram = nc.dram_tensor("l3ao", [8, 128, 2048], BF16).ap()
    with ExitStack() as es:
        gb = p.sbuf(es, [128, 16], F32)
        Hseq = p.sbuf(es, [128, FCH * 2048], BF16)
        QK = p.sbuf(es, [64, 2 * 2048], BF16)
        Vst = p.sbuf(es, [64, 512], BF16)
        Vkm = p.sbuf(es, [128, 16 * 65], BF16)
        Ptile = [p.sbuf(es, [128, 512], BF16) for _ in range(1)]
        btile = [p.sbuf(es, [128, 512], BF16) for _ in range(1)]
        rz = p.sbuf(es, [1, 512], F32)
        oT = p.sbuf(es, [64, 512], BF16)
        AOst = p.sbuf(es, [64, 512], BF16)
        SEG2 = 128
        AOseg = p.sbuf(es, [128, FCH * SEG2], BF16)
        Yseg = p.sbuf(es, [128, FCH * SEG2], BF16)
        wq_s = p.take_dq(2)
        bq_s = p.take_dq(2)
        aoq = p.take_dq(1, sw=True)[0]
        wqkvb = nc.dram_tensor("l3wqkvb", [16, 3, 128, 512], BF16).ap()
        p.barrier(("sync",))
        with p.block() as blk:
            @blk.sync
            def _(s):
                p.inc(s.dma_start(out=gb[:], in_=gb_ap), "dma", 16)
        emit_precast(p, [(wqkv_ap[h_, t3_], wqkvb[h_, t3_])
                         for h_ in range(16) for t3_ in range(3)],
                     obuf=Hseq[:].bitcast(F32))
        p.barrier()

        for seq in range(SEQ):
            emit_ln(p, p.X, dst=Hseq, ntok=2048, src_tok0=seq * 2048, dst_stride=2048, dst_tok0=0)
            for h in range(16):
                with ExitStack() as es2:
                    ps_kv = [p.psum(es2, [64, 512]) for _ in range(2)]
                    ps_vt = [p.psum(es2, [128, 64], BF16) for _ in range(2)]
                    base2 = dict(p.C)
                    wcons = [("pe", base2["pe"] + 4), ("pe", base2["pe"] + 8),
                             ("pe", base2["pe"] + 28)]
                    wplan = [p.wr_plan(wcons[t3]) for t3 in range(3)]

                    def wv(t3):
                        return p.wring[wplan[t3][0]][:].bitcast(BF16)[:, 0:512]

                    with p.block() as blk:
                        @blk.sync
                        def _(s):
                            for t3 in range(3):
                                i, sem, cnt, prev = wplan[t3]
                                if prev is not None:
                                    s.wait_ge(p.S[prev[0]], prev[1])
                                s.dma_start(out=wv(t3), in_=wqkvb[h, t3]
                                            ).then_inc(p.S[sem], 16)
                        @blk.tensor
                        def _(t):
                            t.wait_ge(p.S["dve"], base2["dve"])
                            for i in range(12):
                                t3, sg = i // 4, i % 4
                                if sg == 0:
                                    t.wait_ge(p.S[wplan[t3][1]], wplan[t3][2])
                                if i >= 2:
                                    t.wait_ge(p.S["act"], base2["act"] + i - 1)
                                for kc in range(FCH):
                                    ins = t.matmul(ps_kv[i % 2][:],
                                                   wv(t3)[:, kc * 64:(kc + 1) * 64],
                                                   Hseq[:, kc * 2048 + sg * 512:kc * 2048 + (sg + 1) * 512],
                                                   start=(kc == 0), stop=(kc == FCH - 1))
                                p.inc(ins, "pe")
                                if t3 == 2:
                                    for j in range(4):
                                        kt = sg * 4 + j
                                        t.wait_ge(p.S["act"], base2["act"] + i + 1)
                                        if kt >= 2:
                                            t.wait_ge(p.S["dve"], base2["dve"] + kt)
                                        p.inc(t.transpose(ps_vt[kt % 2][:], Vst[:, j * 128:(j + 1) * 128], p.eye[0:64, 0:64]), "pe")
                        @blk.scalar
                        def _(a):
                            for i in range(12):
                                t3, sg = i // 4, i % 4
                                npe = i + 1 + max(0, (i - 8)) * 4
                                a.wait_ge(p.S["pe"], base2["pe"] + npe)
                                dst = (Vst[:, 0:512] if t3 == 2 else
                                       QK[0:64, t3 * 2048 + sg * 512:t3 * 2048 + (sg + 1) * 512])
                                p.inc(a.activation(dst, ps_kv[i % 2][:], AF.Identity, scale=1.0), "act")
                        @blk.vector
                        def _(v):
                            p.fence(v, "dve")
                            p.inc(v.memset(Vkm[:].rearrange("p (kt c) -> p kt c", c=65)[:, :, 64:65], 1.0), "dve")
                            for kt in range(16):
                                sg = kt // 4
                                i = 8 + sg
                                npe = i + 1 + sg * 4 + (kt % 4) + 1
                                v.wait_ge(p.S["pe"], base2["pe"] + npe)
                                p.inc(v.tensor_copy(Vkm[:, kt * 65:kt * 65 + 64], ps_vt[kt % 2][:]), "dve")
                    p.barrier(("tensor", "vector", "scalar"))
                with ExitStack() as es2:
                    ps_s = [p.psum(es2, [128, 512]) for _ in range(2)]
                    ps_o = p.psum(es2, [128, 512])
                    ps_o2 = p.psum(es2, [128, 512])
                    ps_z = p.psum(es2, [64, 512])
                    for qs in range(4):
                        base3 = dict(p.C)
                        with p.block() as blk:
                            @blk.sync
                            def _(s):
                                for kt in range(16):
                                    s.wait_ge(p.S["pe"], base3["pe"] + 3 * kt)
                                    p.inc(s.dma_start(out=btile[0][:].rearrange("p (itt i) -> p itt i", itt=4),
                                                      in_=BassAP(temp.tensor,
                                                                 ((h * 16 + 4 * qs) * 16 + kt) * 16384,
                                                                 [[128, 128], [16384 * 16, 4], [1, 128]])), bq_s[0], 16)
                            @blk.tensor
                            def _(t):
                                for kt in range(16):
                                    if kt >= 2:
                                        t.wait_ge(p.S["act"], base3["act"] + kt - 1)
                                    ins = t.matmul(ps_s[kt % 2][:],
                                                   QK[0:64, 1 * 2048 + kt * 128:1 * 2048 + (kt + 1) * 128],
                                                   QK[0:64, 0 * 2048 + qs * 512:0 * 2048 + (qs + 1) * 512],
                                                   start=True, stop=True)
                                    p.inc(ins, "pe")
                                    t.wait_ge(p.S["act"], base3["act"] + kt + 1)
                                    p.inc(t.matmul(ps_o[0:65, :], Vkm[:, kt * 65:(kt + 1) * 65], Ptile[0][:],
                                                   start=(kt == 0), stop=(kt == 15)), "pe")
                                    t.wait_ge(p.S[bq_s[0]], base3[bq_s[0]] + 16 * (kt + 1))
                                    p.inc(t.matmul(ps_o2[0:64, :], Vkm[:, kt * 65:kt * 65 + 64], btile[0][:],
                                                   start=(kt == 0), stop=(kt == 15)), "pe")
                            @blk.scalar
                            def _(a):
                                for kt in range(16):
                                    a.wait_ge(p.S["pe"], base3["pe"] + 3 * kt + 1)
                                    p.inc(a.activation(Ptile[0][:], ps_s[kt % 2][:], AF.Exp, scale=0.03125), "act")
                        base4 = dict(p.C)
                        with p.block() as blk:
                            @blk.vector
                            def _(v):
                                v.wait_ge(p.S["pe"], base4["pe"])
                                p.inc(v.reciprocal(rz[:], ps_o[64:65, :]), "dve")
                            @blk.tensor
                            def _(t):
                                t.wait_ge(p.S["dve"], p.C["dve"])
                                p.inc(t.matmul(ps_z[:], p.rowsel[0:1, 0:64], rz[:], start=True, stop=True), "pe")
                            @blk.scalar
                            def _(a):
                                a.wait_ge(p.S["pe"], base4["pe"])
                                p.inc(a.activation(oT[:], ps_o[0:64, :], AF.Identity, scale=1.0), "act")
                        with p.block() as blk:
                            @blk.vector
                            def _(v):
                                v.wait_ge(p.S["pe"], p.C["pe"])
                                v.wait_ge(p.S["act"], p.C["act"])
                                p.inc(v.tensor_tensor(oT[:], oT[:], ps_z[:], ALU.mult), "dve")
                                p.fence(v, "dve")
                                p.inc(v.tensor_tensor(AOst[:], oT[:], ps_o2[0:64, :], ALU.add), "dve")
                            @blk.gpsimd
                            def _(g):
                                g.wait_ge(p.S["dve"], p.C["dve"])
                                dst = BassAP(ao_dram.tensor,
                                             (h // 2) * 128 * 2048 + (h % 2) * 64 * 2048 + qs * 512,
                                             [[2048, 64], [1, 512]])
                                p.inc(g.dma_start(out=dst, in_=AOst[:]), aoq, 16)
                        p.barrier(("tensor", "vector", "scalar", "sync", "gpsimd"))
            for sg in range(2048 // SEG2):
                base5 = dict(p.C)
                p.barrier(("sync",))
                with p.block() as blk:
                    @blk.sync
                    def _(s):
                        p.inc(s.dma_start(out=AOseg[:].rearrange("p (c t) -> p c t", c=FCH),
                                          in_=ao_dram[:, :, sg * SEG2:(sg + 1) * SEG2].rearrange("c p t -> p c t")),
                              "dma", 16)
                    @blk.vector
                    def _(v):
                        v.wait_ge(p.S["dma"], p.C["dma"])
                emit_ln(p, AOseg, dst=Yseg, ntok=SEG2, src_stride=SEG2,
                        dst_stride=SEG2, dst_tok0=0, eps=1e-3, gamma=gb, beta=gb,
                        gb_col0=0, beta_col0=8)
                emit_ln(p, Yseg, acc_into=p.X, ntok=SEG2, src_stride=SEG2,
                        dst_stride=TOK, dst_tok0=seq * 2048 + sg * SEG2)


def prep_l3attn(ins):
    T = np.asarray(ins["L3_attn_rel_table"]).astype(np.float32)
    T_ext = np.concatenate([T, np.repeat(T[2046:2047], 4096 - 2047, axis=0)], axis=0)
    def headtiles(W):
        return np.ascontiguousarray(
            np.asarray(W).astype(np.float32).reshape(8, 128, 16, 64).transpose(2, 1, 0, 3).reshape(16, 128, 512))
    wq = headtiles(ins["L3_attn_Wq"])
    wk = headtiles(ins["L3_attn_Wk"])
    wv = headtiles(ins["L3_attn_Wv"])
    wqkv = np.ascontiguousarray(np.stack([wq, wk, wv], axis=1))
    gb = np.concatenate([bias_tiles(ins["L3_attn_gamma"]), bias_tiles(ins["L3_attn_beta"])], axis=1)
    return {
        "L3a_text": np.ascontiguousarray(T_ext.reshape(-1)),
        "L3a_wqkv": wqkv,
        "L3a_gb": np.ascontiguousarray(gb),
    }


def decl_l3attn(nc):
    return dict(
        text_ap=nc.dram_tensor("L3a_text", [4096 * 16], F32, kind="ExternalInput").ap(),
        wqkv_ap=nc.dram_tensor("L3a_wqkv", [16, 3, 128, 512], F32, kind="ExternalInput").ap(),
        gb_ap=nc.dram_tensor("L3a_gb", [128, 16], F32, kind="ExternalInput").ap(),
    )


def emit_l0attn(p: Prog, wqkv_ap, wo_ap):
    nc = p.nc
    ao_dram = nc.dram_tensor("l0ao", [8, 128, 2048], BF16).ap()
    wqkvb = nc.dram_tensor("l0wqkvb", [16, 3, 128, 512], BF16).ap()
    wob = nc.dram_tensor("l0wob", [8, 128, 1024], BF16).ap()
    with ExitStack() as es:
        Hseq = p.sbuf(es, [128, FCH * 2048], BF16)
        Qaug = p.sbuf(es, [66, 2048], BF16)
        Kaug = p.sbuf(es, [66, 2048], BF16)
        qnrow = p.sbuf(es, [1, 2048], BF16)
        arow = p.sbuf(es, [1, 2048], BF16)
        Vkm = p.sbuf(es, [128, 16 * 65], BF16)
        cbuf = p.sbuf(es, [128, 3840], F32)
        brow = cbuf[0:1, 3584:3840].bitcast(BF16)
        Vst = cbuf[0:64, 0:256].bitcast(BF16)
        sq2 = [cbuf[0:64, 256:512].bitcast(BF16)]
        c0 = cbuf[:, 0:1024]
        c1 = cbuf[:, 1024:2048]
        c2 = cbuf[:, 2048:3072]
        Ptile = cbuf[:, 3072:3584].bitcast(BF16)
        oT = Ptile[0:64, 512:1024]
        AOst = Ptile[64:128, 512:1024]
        SEG2 = 256
        AOseg = cbuf[:, 2048:3072].bitcast(BF16)
        Yseg = cbuf[:, 0:2048]
        wq_s = p.take_dq(2)
        qnq = p.take_dq(1)[0]
        aoq = p.take_dq(1, sw=True)[0]

        p.barrier(("sync",))
        emit_precast(p, [(wqkv_ap[h_, t3_], wqkvb[h_, t3_])
                         for h_ in range(16) for t3_ in range(3)]
                     + [(wo_ap[oc, :, :], wob[oc, :, :]) for oc in range(FCH)],
                     obuf=Hseq[:].bitcast(F32))
        p.barrier(("sync",))
        for seq in range(SEQ):
            emit_ln(p, p.X, dst=Hseq, ntok=2048, src_tok0=seq * 2048, dst_stride=2048, dst_tok0=0, scratch=cbuf[:, 0:3072])
            for h in range(16):
                basev = dict(p.C)
                wplan = [p.wr_plan(("pe", basev["pe"] + 20 + 4)),
                         p.wr_plan(("pe", basev["pe"] + 20 + 8)),
                         p.wr_plan(("pe", basev["pe"] + 12))]

                def wv(t3):
                    return p.wring[wplan[t3][0]][:].bitcast(BF16)[:, 0:512]

                with p.block() as blk:
                    @blk.sync
                    def _(s):
                        for t3 in range(3):
                            i, sem, cnt, prev = wplan[t3]
                            if prev is not None:
                                s.wait_ge(p.S[prev[0]], prev[1])
                            s.dma_start(out=wv(t3), in_=wqkvb[h, t3]
                                        ).then_inc(p.S[sem], 16)
                with ExitStack() as es2:
                    ps_kv = [p.psum(es2, [64, 512]) for _ in range(2)]
                    ps_vt = [p.psum(es2, [128, 64], BF16) for _ in range(2)]
                    ps_n = p.psum(es2, [1, 2048])
                    base2 = dict(p.C)

                    def pe_T(b):
                        return [6, 11, 16, 20][b]

                    def pe_proj(sg):
                        return [1, 2, 7, 12][sg]

                    with p.block() as blk:
                        @blk.tensor
                        def _(t):
                            t.wait_ge(p.S["dve"], base2["dve"])
                            t.wait_ge(p.S[wplan[2][1]], wplan[2][2])
                            for sg in range(4):
                                if sg >= 2:
                                    t.wait_ge(p.S["act"], base2["act"] + sg - 1)
                                for kc in range(FCH):
                                    ins = t.matmul(ps_kv[sg % 2][:],
                                                   wv(2)[:, kc * 64:(kc + 1) * 64],
                                                   Hseq[:, kc * 2048 + sg * 512:kc * 2048 + (sg + 1) * 512],
                                                   start=(kc == 0), stop=(kc == FCH - 1))
                                p.inc(ins, "pe")
                                if sg >= 1:
                                    b = sg - 1
                                    t.wait_ge(p.S["act"], base2["act"] + b + 1)
                                    for j in range(4):
                                        kt = b * 4 + j
                                        if kt >= 2:
                                            t.wait_ge(p.S["dve"], base2["dve"] + kt)
                                        p.inc(t.transpose(ps_vt[kt % 2][:], Vst[:, j * 128:(j + 1) * 128], p.eye[0:64, 0:64]), "pe")
                            t.wait_ge(p.S["act"], base2["act"] + 4)
                            for j in range(4):
                                kt = 12 + j
                                t.wait_ge(p.S["dve"], base2["dve"] + kt)
                                p.inc(t.transpose(ps_vt[kt % 2][:], Vst[:, j * 128:(j + 1) * 128], p.eye[0:64, 0:64]), "pe")
                        @blk.scalar
                        def _(a):
                            for sg in range(4):
                                a.wait_ge(p.S["pe"], base2["pe"] + pe_proj(sg))
                                if sg >= 1:
                                    a.wait_ge(p.S["pe"], base2["pe"] + pe_T(sg - 1))
                                p.inc(a.activation(Vst[:], ps_kv[sg % 2][:], AF.Identity, scale=1.0), "act")
                        @blk.vector
                        def _(v):
                            p.fence(v, "dve")
                            p.inc(v.memset(Vkm[:].rearrange("p (kt c) -> p kt c", c=65)[:, :, 64:65], 1.0), "dve")
                            for kt in range(16):
                                b, j = kt // 4, kt % 4
                                pos = (3 + b * 5 + j) if b < 3 else (17 + j)
                                v.wait_ge(p.S["pe"], base2["pe"] + pos)
                                p.inc(v.tensor_copy(Vkm[:, kt * 65:kt * 65 + 64], ps_vt[kt % 2][:]), "dve")
                    p.barrier(("tensor", "vector", "scalar"))
                    base2 = dict(p.C)
                    with p.block() as blk:
                        @blk.tensor
                        def _(t):
                            for i in range(8):
                                t3, sg = i // 4, i % 4
                                if sg == 0:
                                    t.wait_ge(p.S[wplan[t3][1]], wplan[t3][2])
                                if i >= 2:
                                    t.wait_ge(p.S["act"], base2["act"] + i - 1)
                                for kc in range(FCH):
                                    ins = t.matmul(ps_kv[i % 2][:],
                                                   wv(t3)[:, kc * 64:(kc + 1) * 64],
                                                   Hseq[:, kc * 2048 + sg * 512:kc * 2048 + (sg + 1) * 512],
                                                   start=(kc == 0), stop=(kc == FCH - 1))
                                p.inc(ins, "pe")
                        @blk.scalar
                        def _(a):
                            for i in range(8):
                                t3, sg = i // 4, i % 4
                                a.wait_ge(p.S["pe"], base2["pe"] + i + 1)
                                dst = (Qaug if t3 == 0 else Kaug)[0:64, sg * 512:(sg + 1) * 512]
                                p.inc(a.activation(dst, ps_kv[i % 2][:], AF.Identity,
                                                   scale=(-2.0 if t3 == 0 else 1.0)), "act")
                    p.barrier(("tensor", "scalar"))
                    base2 = dict(p.C)
                    with p.block() as blk:
                        @blk.vector
                        def _(v):
                            p.fence(v, "dve")
                            p.inc(v.memset(Qaug[64:66, :], 1.0), "dve")
                            p.inc(v.memset(Kaug[64:66, :], 1.0), "dve")
                            for i in range(8):
                                t3, sg = i // 4, i % 4
                                if i >= 1:
                                    v.wait_ge(p.S["pe"], base2["pe"] + i)
                                src = (Qaug if t3 == 0 else Kaug)[0:64, sg * 512:(sg + 1) * 512]
                                p.inc(v.tensor_tensor(sq2[0][:], src, src, ALU.mult), "dve")
                        @blk.tensor
                        def _(t):
                            for i in range(8):
                                t3, sg = i // 4, i % 4
                                t.wait_ge(p.S["dve"], base2["dve"] + 2 + i + 1)
                                if t3 == 1:
                                    t.wait_ge(p.S["act"], base2["act"] + sg + 1)
                                p.inc(t.matmul(ps_n[0:1, sg * 512:(sg + 1) * 512],
                                               p.sel[0:64, 0:1], sq2[0][:],
                                               start=True, stop=True), "pe")
                        @blk.scalar
                        def _(a):
                            for i in range(8):
                                t3, sg = i // 4, i % 4
                                a.wait_ge(p.S["pe"], base2["pe"] + i + 1)
                                if t3 == 0:
                                    p.inc(a.activation(qnrow[0:1, sg * 512:(sg + 1) * 512],
                                                       ps_n[0:1, sg * 512:(sg + 1) * 512],
                                                       AF.Identity, scale=0.25), "act")
                                else:
                                    a.wait_ge(p.S["dve"], base2["dve"] + 1)
                                    p.inc(a.activation(Kaug[64:65, sg * 512:(sg + 1) * 512],
                                                       ps_n[0:1, sg * 512:(sg + 1) * 512],
                                                       AF.Identity, scale=1.0), "act")
                    with p.block() as blk:
                        @blk.sync
                        def _(s):
                            s.wait_ge(p.S["act"], p.C["act"])
                            s.wait_ge(p.S["dve"], p.C["dve"])
                            s.wait_ge(p.S["pe"], p.C["pe"])
                            p.inc(s.dma_start(out=Qaug[65:66, :], in_=qnrow[:]), qnq, 16)
                        @blk.vector
                        def _(v):
                            p.fence(v, "dve")
                            v.wait_ge(p.S["act"], p.C["act"])
                            p.inc(v.tensor_scalar(arow[:], Kaug[64:65, :], -1.0, 1.0, ALU.mult, ALU.add), "dve")
                    p.barrier()
                with ExitStack() as es2:
                    ps_s = [p.psum(es2, [128, 1024]) for _ in range(1)]
                    ps_d = [p.psum(es2, [128, 1024]) for _ in range(1)]
                    ps_o = p.psum(es2, [128, 512])
                    ps_z = p.psum(es2, [64, 512])
                    for qs in range(4):
                        with p.block() as blk:
                            @blk.vector
                            def _(v):
                                p.fence(v, "dve")
                                p.inc(v.tensor_scalar(brow[:], qnrow[0:1, qs * 512:(qs + 1) * 512], -1.0, 1.0, ALU.mult, ALU.add), "dve")
                        base3 = dict(p.C)
                        with p.block() as blk:
                            @blk.tensor
                            def _(t):
                                for kp in range(8):
                                    if kp >= 1:
                                        t.wait_ge(p.S["act"], base3["act"] + 2 * kp - 1)
                                        t.wait_ge(p.S["dve"], base3["dve"] + (kp - 1) * 6 + 1)
                                    for half in range(2):
                                        kt = kp * 2 + half
                                        t.matmul(ps_s[0][:, half * 512:(half + 1) * 512],
                                                 Kaug[:, kt * 128:(kt + 1) * 128],
                                                 Qaug[:, qs * 512:(qs + 1) * 512],
                                                 start=True, stop=True)
                                        p.inc(t.matmul(ps_d[0][:, half * 512:(half + 1) * 512],
                                                       arow[:, kt * 128:(kt + 1) * 128],
                                                       brow[:, 0:512],
                                                       start=True, stop=True), "pe")
                                    t.wait_ge(p.S["dve"], base3["dve"] + (kp + 1) * 6)
                                    for half in range(2):
                                        kt = kp * 2 + half
                                        p.inc(t.matmul(ps_o[0:65, :], Vkm[:, kt * 65:(kt + 1) * 65],
                                                       Ptile[:, half * 512:(half + 1) * 512],
                                                       start=(kt == 0), stop=(kt == 15)), "pe")
                            @blk.scalar
                            def _(a):
                                for kp in range(8):
                                    a.wait_ge(p.S["pe"], base3["pe"] + kp * 4 + 2)
                                    if kp >= 1:
                                        a.wait_ge(p.S["dve"], base3["dve"] + (kp - 1) * 6 + 5)
                                    p.inc(a.activation(c0[:], ps_s[0][:], AF.Relu, scale=1.0), "act")
                                    a.wait_ge(p.S["dve"], base3["dve"] + kp * 6 + 4)
                                    p.inc(a.activation(c0[:], c2[:], AF.Sqrt, scale=4.0), "act")
                            @blk.vector
                            def _(v):
                                for kp in range(8):
                                    v.wait_ge(p.S["act"], base3["act"] + kp * 2 + 1)
                                    p.inc(v.tensor_scalar(c1[:], ps_d[0][:], 1e-6, None, ALU.max), "dve")
                                    p.fence(v, "dve")
                                    p.inc(v.reciprocal(c2[:], c1[:]), "dve")
                                    p.fence(v, "dve")
                                    p.inc(v.tensor_tensor(c1[:], c0[:], c2[:], ALU.mult), "dve")
                                    p.fence(v, "dve")
                                    p.inc(v.scalar_tensor_tensor(c2[:], c1[:], 1.0, c1[:], ALU.add, ALU.mult), "dve")
                                    v.wait_ge(p.S["act"], base3["act"] + kp * 2 + 2)
                                    p.inc(v.scalar_tensor_tensor(c2[:], c1[:], 2.0, c0[:], ALU.mult, ALU.subtract), "dve")
                                    p.fence(v, "dve")
                                    if kp >= 1:
                                        v.wait_ge(p.S["pe"], base3["pe"] + kp * 4)
                                    p.inc(v.tensor_scalar(Ptile[:], c2[:], 1.0, None, ALU.add), "dve")
                        base4 = dict(p.C)
                        with p.block() as blk:
                            @blk.vector
                            def _(v):
                                v.wait_ge(p.S["pe"], base4["pe"])
                                p.inc(v.reciprocal(cbuf[0:1, 2048:2560], ps_o[64:65, :]), "dve")
                            @blk.tensor
                            def _(t):
                                t.wait_ge(p.S["dve"], p.C["dve"])
                                p.inc(t.matmul(ps_z[:], p.rowsel[0:1, 0:64], cbuf[0:1, 2048:2560], start=True, stop=True), "pe")
                            @blk.scalar
                            def _(a):
                                a.wait_ge(p.S["pe"], base4["pe"])
                                p.inc(a.activation(oT[:], ps_o[0:64, :], AF.Identity, scale=1.0), "act")
                        with p.block() as blk:
                            @blk.vector
                            def _(v):
                                v.wait_ge(p.S["pe"], p.C["pe"])
                                v.wait_ge(p.S["act"], p.C["act"])
                                p.inc(v.tensor_tensor(AOst[:], oT[:], ps_z[:], ALU.mult), "dve")
                            @blk.gpsimd
                            def _(g):
                                g.wait_ge(p.S["dve"], p.C["dve"])
                                dst = BassAP(ao_dram.tensor,
                                             (h // 2) * 128 * 2048 + (h % 2) * 64 * 2048 + qs * 512,
                                             [[2048, 64], [1, 512]])
                                p.inc(g.dma_start(out=dst, in_=AOst[:]), aoq, 16)
                        p.barrier(("tensor", "vector", "scalar", "sync", "gpsimd"))
            for sg in range(2048 // SEG2):
                p.barrier(("sync",))
                with p.block() as blk:
                    @blk.sync
                    def _(s):
                        p.inc(s.dma_start(out=AOseg[:].rearrange("p (c t) -> p c t", c=FCH),
                                          in_=ao_dram[:, :, sg * SEG2:(sg + 1) * SEG2].rearrange("c p t -> p c t")),
                              "dma", 16)
                    @blk.vector
                    def _(v):
                        v.wait_ge(p.S["dma"], p.C["dma"])

                def epi(a, oc, j, psv):
                    return a.activation(Yseg[:, oc * SEG2:(oc + 1) * SEG2], psv,
                                        AF.Identity, scale=1.0)
                emit_gemm(p, lambda oc, hh: wob[oc, :, hh * 512:(hh + 1) * 512], FCH, FCH,
                          lambda kc, j: AOseg[:, kc * SEG2:(kc + 1) * SEG2], epi,
                          wdtype=BF16, kcw=4, nbuf=1)
                emit_ln(p, Yseg, acc_into=p.X, ntok=SEG2, src_stride=SEG2,
                        dst_stride=TOK, dst_tok0=seq * 2048 + sg * SEG2,
                        scratch=cbuf[:, 2048:3840])


def prep_l0attn(ins):
    def headtiles_T(W):
        Wt = np.ascontiguousarray(np.asarray(W).astype(np.float32).T)
        return np.ascontiguousarray(Wt.reshape(8, 128, 16, 64).transpose(2, 1, 0, 3).reshape(16, 128, 512))
    wq = headtiles_T(ins["L0_attn_wq"])
    wk = headtiles_T(ins["L0_attn_wk"])
    wv = headtiles_T(ins["L0_attn_wv"])
    wqkv = np.ascontiguousarray(np.stack([wq, wk, wv], axis=1))
    wo_t = tile_lhsT(np.ascontiguousarray(np.asarray(ins["L0_attn_wo"]).astype(np.float32).T))
    return {"L0a_wqkv": wqkv, "L0a_wo": np.ascontiguousarray(wo_t)}


def decl_l0attn(nc):
    return dict(
        wqkv_ap=nc.dram_tensor("L0a_wqkv", [16, 3, 128, 512], F32, kind="ExternalInput").ap(),
        wo_ap=nc.dram_tensor("L0a_wo", [8, 128, 1024], F32, kind="ExternalInput").ap(),
    )


def emit_seq(p: Prog, ops):
    nc = p.nc
    plan = []
    cnt = {"dve": p.C["dve"], "act": p.C["act"], "pool": p.C["pool"]}
    base = dict(cnt)
    for eng, fn in ops:
        plan.append((eng, fn, dict(cnt)))
        cnt[ENG2SEM[eng]] += 1
    secs = {"vector": [], "scalar": [], "gpsimd": []}
    for eng, fn, pre in plan:
        secs[eng].append((eng, fn, pre))
    with p.block() as blk:
        for engname in ("vector", "scalar", "gpsimd"):
            if not secs[engname]:
                continue
            def mk(engname):
                def _(q):
                    own = ENG2SEM[engname]
                    seen = {"dve": -1, "act": -1, "pool": -1}
                    for eng, fn, pre in secs[engname]:
                        for s2 in ("dve", "act", "pool"):
                            want = pre[s2]
                            if s2 == own:
                                if want > base[own] and want > seen[s2]:
                                    q.wait_ge(p.S[s2], want)
                                    seen[s2] = want
                            else:
                                if want > 0 and want > seen[s2]:
                                    q.wait_ge(p.S[s2], want)
                                    seen[s2] = want
                        p.inc(fn(q), own)
                        if ENG2SEM[engname] == own:
                            pass
                return _
            getattr(blk, engname)(mk(engname))



PI = float(np.pi)

def emit_l2attn(p: Prog, wmag_ap, wq_ap, bmq_ap, ramp_ap, phases_ap, lng_ap,
                w1_ap, b1_ap, w2_ap, b2_ap):
    nc = p.nc
    SEG = 256
    NSG = 2048 // SEG
    wmagb = nc.dram_tensor("l2wmagb", [8, 128, 1024], BF16).ap()
    wqb = nc.dram_tensor("l2wqb", [8, 128, 1024], BF16).ap()
    w1b = nc.dram_tensor("l2w1b", [16, 128, 4096], BF16).ap()
    w2b = nc.dram_tensor("l2w2b", [8, 128, 2048], BF16).ap()
    with ExitStack() as es:
        ramp = p.sbuf(es, [1, 256], F32)
        phs = p.sbuf(es, [128, 8], F32)
        bmq = p.sbuf(es, [128, 16], F32)
        lng = p.sbuf(es, [128, 64], F32)
        b1s = p.sbuf(es, [128, 16], F32)
        b2s = p.sbuf(es, [128, 8], F32)
        Hseg = p.sbuf(es, [128, FCH * SEG], BF16)
        mag0 = p.sbuf(es, [128, FCH * SEG], BF16)
        qoff = p.sbuf(es, [128, FCH * SEG], F32)
        CTX = p.sbuf(es, [128, 32 * SEG], BF16)
        G = p.sbuf(es, [128, 16 * SEG], BF16)
        Yseg = p.sbuf(es, [128, FCH * SEG], F32)
        carry = p.sbuf(es, [128, 24], F32)
        cbuf = p.sbuf(es, [128, 4096], F32)

        def cv(i):
            return cbuf[:, i * SEG:(i + 1) * SEG]
        t_phi, t_n, t_m, t_sin, t_cos, t_w0, t_a, t_b, t_mr, t_mi, t_sr, t_si = [cv(i) for i in range(12)]
        p.barrier(("sync",))
        with p.block() as blk:
            @blk.sync
            def _(s):
                for ap_, buf in ((phases_ap, phs), (bmq_ap, bmq),
                                 (lng_ap, lng), (b1_ap, b1s), (b2_ap, b2s)):
                    p.inc(s.dma_start(out=buf[:], in_=ap_), "dma", 16)
            @blk.vector
            def _(v):
                v.wait_ge(p.S["dma"], p.C["dma"])
        pairs = []
        for src, dst, noc in ((wmag_ap, wmagb, 8), (wq_ap, wqb, 8),
                              (w1_ap, w1b, 16), (w2_ap, w2b, 8)):
            for oc in range(noc):
                pairs.append((src[oc, :, :], dst[oc, :, :]))
        emit_precast(p, pairs, obuf=cbuf)
        p.barrier(("sync",))
        for seq in range(SEQ):
            with p.block() as blk:
                @blk.vector
                def _(v):
                    p.fence(v, "dve")
                    p.inc(v.memset(carry[:], 0.0), "dve")
            p.barrier(("vector",))
            for sg in range(NSG):
                tok0 = seq * 2048 + sg * SEG
                emit_ln(p, p.X, dst=Hseg, ntok=SEG, src_tok0=tok0,
                        dst_stride=SEG, dst_tok0=0, scratch=cbuf)

                def epi_mag(a, oc, j, psv):
                    return a.activation(mag0[:, oc * SEG:(oc + 1) * SEG], psv,
                                        AF.Sigmoid, bias=bmq[:, oc:oc + 1], scale=1.0)
                emit_gemm(p, lambda oc, hh: wmagb[oc, :, :], FCH, FCH,
                          lambda kc, j: Hseg[:, kc * SEG:(kc + 1) * SEG], epi_mag,
                          wdtype=BF16, kcw=8, nbuf=1)

                def epi_q(a, oc, j, psv):
                    return a.activation(qoff[:, oc * SEG:(oc + 1) * SEG], psv,
                                        AF.Identity, bias=bmq[:, 8 + oc:9 + oc], scale=1.0)
                emit_gemm(p, lambda oc, hh: wqb[oc, :, :], FCH, FCH,
                          lambda kc, j: Hseg[:, kc * SEG:(kc + 1) * SEG], epi_q,
                          wdtype=BF16, kcw=8, nbuf=1)

                with ExitStack() as es2:
                    ps_r = p.psum(es2, [128, 512])
                    with p.block() as blk:
                        @blk.sync
                        def _(s):
                            s.wait_ge(p.S["pe"], p.C["pe"])
                            p.inc(s.dma_start(out=ramp[:], in_=ramp_ap[0:1, sg * SEG:(sg + 1) * SEG]), "dma", 16)
                        @blk.tensor
                        def _(t):
                            t.wait_ge(p.S["dve"], p.C["dve"])
                            t.wait_ge(p.S["act"], p.C["act"])
                            t.wait_ge(p.S["dma"], p.C["dma"])
                            p.inc(t.matmul(ps_r[:, 0:SEG], p.rowsel[0:1, 0:128],
                                           ramp[0:1, 0:SEG],
                                           start=True, stop=True), "pe")
                    p.barrier(("tensor", "vector", "scalar", "sync"))
                    allops = []
                    for c in range(FCH):
                        H_ = Hseg[:, c * SEG:(c + 1) * SEG]
                        M_ = mag0[:, c * SEG:(c + 1) * SEG]
                        Q_ = qoff[:, c * SEG:(c + 1) * SEG]
                        ops = []
                        ops.append(("vector", lambda v, c=c: v.tensor_scalar(t_phi, ps_r[:, 0:SEG], phs[:, c:c + 1], 0.01, ALU.mult, ALU.mult)))
                        ops.append(("vector", lambda v: v.tensor_scalar(t_n, t_phi, 1.0 / (2 * PI), MAGIC, ALU.mult, ALU.add)))
                        ops.append(("vector", lambda v: v.tensor_scalar(t_n, t_n, MAGIC, None, ALU.subtract)))
                        ops.append(("vector", lambda v: v.scalar_tensor_tensor(t_m, t_n, -2 * PI, t_phi, ALU.mult, ALU.add)))
                        ops.append(("vector", lambda v: v.tensor_scalar(t_m, t_m, -PI, PI, ALU.max, ALU.min)))
                        ops.append(("scalar", lambda a: a.activation(t_sin, t_m, AF.Sin, scale=1.0)))
                        ops.append(("scalar", lambda a: a.activation(t_n, t_m, AF.Abs, scale=1.0)))
                        ops.append(("scalar", lambda a: a.activation(t_cos, t_n, AF.Sin, bias=p.aux[:, AUX_HALFPI:AUX_HALFPI + 1], scale=-1.0)))
                        ops.append(("vector", lambda v, H_=H_, M_=M_: v.scalar_tensor_tensor(t_w0, M_, 5.0, H_, ALU.mult, ALU.mult)))
                        ops.append(("vector", lambda v: v.tensor_tensor(t_a, t_w0, t_cos, ALU.mult)))
                        ops.append(("vector", lambda v: v.tensor_tensor(t_b, t_w0, t_sin, ALU.mult)))
                        ops.append(("vector", lambda v, c=c: v.tensor_tensor_scan(t_sr, t_a, t_a, carry[:, c:c + 1], ALU.add, ALU.bypass)))
                        ops.append(("vector", lambda v, c=c: v.tensor_tensor_scan(t_si, t_b, t_b, carry[:, 8 + c:9 + c], ALU.add, ALU.bypass)))
                        ops.append(("vector", lambda v, c=c, M_=M_: v.tensor_tensor_scan(t_a, M_, M_, carry[:, 16 + c:17 + c], ALU.add, ALU.bypass)))
                        ops.append(("vector", lambda v, c=c: v.tensor_copy(carry[:, c:c + 1], t_sr[:, SEG - 1:SEG])))
                        ops.append(("vector", lambda v, c=c: v.tensor_copy(carry[:, 8 + c:9 + c], t_si[:, SEG - 1:SEG])))
                        ops.append(("vector", lambda v, c=c: v.tensor_copy(carry[:, 16 + c:17 + c], t_a[:, SEG - 1:SEG])))
                        ops.append(("scalar", lambda a: a.activation(t_b, t_a, AF.Sqrt, bias=p.aux[:, AUX_WKVEPS:AUX_WKVEPS + 1], scale=5.0)))
                        ops.append(("vector", lambda v: v.reciprocal(t_b, t_b)))
                        ops.append(("vector", lambda v: v.tensor_tensor(t_mr, t_sr, t_b, ALU.mult)))
                        ops.append(("vector", lambda v: v.tensor_tensor(t_mi, t_si, t_b, ALU.mult)))
                        ops.append(("vector", lambda v, c=c, H_=H_: v.tensor_tensor(CTX[:, c * SEG:(c + 1) * SEG], H_, t_cos, ALU.mult)))
                        ops.append(("vector", lambda v, c=c, H_=H_: v.tensor_tensor(CTX[:, (8 + c) * SEG:(9 + c) * SEG], H_, t_sin, ALU.mult)))
                        ops.append(("vector", lambda v, Q_=Q_: v.tensor_tensor(t_phi, t_phi, Q_, ALU.add)))
                        ops.append(("vector", lambda v: v.tensor_scalar(t_n, t_phi, 1.0 / (2 * PI), MAGIC, ALU.mult, ALU.add)))
                        ops.append(("vector", lambda v: v.tensor_scalar(t_n, t_n, MAGIC, None, ALU.subtract)))
                        ops.append(("vector", lambda v: v.scalar_tensor_tensor(t_m, t_n, -2 * PI, t_phi, ALU.mult, ALU.add)))
                        ops.append(("vector", lambda v: v.tensor_scalar(t_m, t_m, -PI, PI, ALU.max, ALU.min)))
                        ops.append(("scalar", lambda a: a.activation(t_sin, t_m, AF.Sin, scale=1.0)))
                        ops.append(("scalar", lambda a: a.activation(t_n, t_m, AF.Abs, scale=1.0)))
                        ops.append(("scalar", lambda a: a.activation(t_cos, t_n, AF.Sin, bias=p.aux[:, AUX_HALFPI:AUX_HALFPI + 1], scale=-1.0)))
                        ops.append(("vector", lambda v: v.tensor_tensor(t_a, t_mr, t_cos, ALU.mult)))
                        ops.append(("vector", lambda v: v.tensor_tensor(t_b, t_mi, t_sin, ALU.mult)))
                        ops.append(("vector", lambda v, c=c: v.tensor_tensor(CTX[:, (16 + c) * SEG:(17 + c) * SEG], t_a, t_b, ALU.add)))
                        ops.append(("vector", lambda v: v.tensor_tensor(t_a, t_mi, t_cos, ALU.mult)))
                        ops.append(("vector", lambda v: v.tensor_tensor(t_b, t_mr, t_sin, ALU.mult)))
                        ops.append(("vector", lambda v, c=c: v.tensor_tensor(CTX[:, (24 + c) * SEG:(25 + c) * SEG], t_a, t_b, ALU.subtract)))
                        allops.extend(ops)
                    emit_seq(p, allops)
                    p.barrier(("vector", "scalar", "tensor"))
                emit_ln(p, CTX, dst=CTX, ntok=SEG, nch=32, src_stride=SEG,
                        dst_stride=SEG, eps=1e-5, gamma=lng, beta=lng,
                        gb_col0=0, beta_col0=32, scratch=cbuf)

                def epi_w1(a, oc, j, psv):
                    return a.activation(G[:, oc * SEG:(oc + 1) * SEG], psv,
                                        AF.Gelu, bias=b1s[:, oc:oc + 1], scale=1.0)
                emit_gemm(p, lambda oc, hh: w1b[oc, :, hh * 1024:(hh + 1) * 1024], 16, 32,
                          lambda kc, j: CTX[:, kc * SEG:(kc + 1) * SEG], epi_w1,
                          wdtype=BF16, kcw=8, nbuf=1)

                def epi_w2(v, oc, j, psv):
                    return v.scalar_tensor_tensor(Yseg[:, oc * SEG:(oc + 1) * SEG], psv,
                                                  b2s[:, oc:oc + 1],
                                                  Hseg[:, oc * SEG:(oc + 1) * SEG],
                                                  ALU.add, ALU.add)
                emit_gemm(p, lambda oc, hh: w2b[oc, :, hh * 1024:(hh + 1) * 1024], FCH, 16,
                          lambda kc, j: G[:, kc * SEG:(kc + 1) * SEG], epi_w2,
                          wdtype=BF16, epi_eng="vector", cast_eng="scalar", kcw=8, nbuf=1)

                emit_ln(p, Yseg, acc_into=p.X, ntok=SEG, src_stride=SEG,
                        dst_stride=TOK, dst_tok0=tok0, scratch=cbuf)


def prep_l2attn(ins):
    wm = tile_lhsT(np.ascontiguousarray(np.asarray(ins["L2_attn_W_mag"]).astype(np.float32).T))
    wq = tile_lhsT(np.ascontiguousarray(np.asarray(ins["L2_attn_W_q"]).astype(np.float32).T))
    bmq = np.concatenate([bias_tiles(ins["L2_attn_b_mag"]), bias_tiles(ins["L2_attn_b_q"])], axis=1)
    ramp = np.arange(2048, dtype=np.float32).reshape(1, 2048)
    phs = bias_tiles(ins["L2_attn_random_phases"])
    lng = np.concatenate([bias_tiles(ins["L2_attn_ln_g"]), bias_tiles(ins["L2_attn_ln_b"])], axis=1)
    w1 = tile_lhsT(np.ascontiguousarray(np.asarray(ins["L2_attn_W1"]).astype(np.float32).T))
    w2 = tile_lhsT(np.ascontiguousarray(np.asarray(ins["L2_attn_W2"]).astype(np.float32).T))
    return {"L2a_wmag": np.ascontiguousarray(wm), "L2a_wq": np.ascontiguousarray(wq),
            "L2a_bmq": np.ascontiguousarray(bmq), "L2a_ramp": ramp,
            "L2a_phs": np.ascontiguousarray(phs), "L2a_lng": np.ascontiguousarray(lng),
            "L2a_w1": np.ascontiguousarray(w1), "L2a_b1": bias_tiles(ins["L2_attn_b1"]),
            "L2a_w2": np.ascontiguousarray(w2), "L2a_b2": bias_tiles(ins["L2_attn_b2"])}


def decl_l2attn(nc):
    return dict(
        wmag_ap=nc.dram_tensor("L2a_wmag", [8, 128, 1024], F32, kind="ExternalInput").ap(),
        wq_ap=nc.dram_tensor("L2a_wq", [8, 128, 1024], F32, kind="ExternalInput").ap(),
        bmq_ap=nc.dram_tensor("L2a_bmq", [128, 16], F32, kind="ExternalInput").ap(),
        ramp_ap=nc.dram_tensor("L2a_ramp", [1, 2048], F32, kind="ExternalInput").ap(),
        phases_ap=nc.dram_tensor("L2a_phs", [128, 8], F32, kind="ExternalInput").ap(),
        lng_ap=nc.dram_tensor("L2a_lng", [128, 64], F32, kind="ExternalInput").ap(),
        w1_ap=nc.dram_tensor("L2a_w1", [16, 128, 4096], F32, kind="ExternalInput").ap(),
        b1_ap=nc.dram_tensor("L2a_b1", [128, 16], F32, kind="ExternalInput").ap(),
        w2_ap=nc.dram_tensor("L2a_w2", [8, 128, 2048], F32, kind="ExternalInput").ap(),
        b2_ap=nc.dram_tensor("L2a_b2", [128, 8], F32, kind="ExternalInput").ap(),
    )


def emit_l1attn(p: Prog, wkt_ap, wvt_ap, wrt_ap, wkc_ap, wvc_ap, wrc_ap,
                lnab_ap, mix_ap, u_ap):
    nc = p.nc
    SEG = 256
    NSG = 2048 // SEG
    EPSW = 1e-8
    wktb = nc.dram_tensor("l1wktb", [8, 128, 1024], BF16).ap()
    wvtb = nc.dram_tensor("l1wvtb", [8, 128, 1024], BF16).ap()
    wrtb = nc.dram_tensor("l1wrtb", [8, 128, 1024], BF16).ap()
    wkcb = nc.dram_tensor("l1wkcb", [32, 128, 1024], BF16).ap()
    wvcb = nc.dram_tensor("l1wvcb", [8, 128, 4096], BF16).ap()
    wrcb = nc.dram_tensor("l1wrcb", [8, 128, 1024], BF16).ap()
    with ExitStack() as es:
        lnab = p.sbuf(es, [128, 32], F32)
        mixs = p.sbuf(es, [128, 40], F32)
        imix = p.sbuf(es, [128, 40], F32)
        ucol = p.sbuf(es, [128, 8], F32)
        XA = p.sbuf(es, [128, FCH * SEG], F32)
        Hseg = p.sbuf(es, [128, FCH * SEG], BF16)
        HS = p.sbuf(es, [128, FCH * SEG], BF16)
        TSB = p.sbuf(es, [128, FCH * SEG], BF16)
        Kf = p.sbuf(es, [128, FCH * SEG], F32)
        Vf = p.sbuf(es, [128, FCH * SEG], BF16)
        KC = p.sbuf(es, [128, 32 * SEG], BF16)
        RC = p.sbuf(es, [128, FCH * SEG], BF16)
        SSc = p.sbuf(es, [128, SEG + 8], F32)
        SSs = p.sbuf(es, [128, SEG + 8], F32)
        SSd = p.sbuf(es, [128, SEG + 8], F32)
        hprev = p.sbuf(es, [128, 32], F32)
        carry = p.sbuf(es, [128, 32], F32)
        cbuf = p.sbuf(es, [128, 2048], F32)
        Rsig = cbuf[:, 1024:2048].bitcast(BF16)
        VV = TSB

        def cv(i):
            return cbuf[:, i * SEG:(i + 1) * SEG]
        t0, t1, t2, t3 = [cv(i) for i in range(4)]

        p.barrier(("sync",))
        with p.block() as blk:
            @blk.sync
            def _(s):
                for ap_, buf in ((lnab_ap, lnab), (mix_ap, mixs), (u_ap, ucol)):
                    p.inc(s.dma_start(out=buf[:], in_=ap_), "dma", 16)
            @blk.vector
            def _(v):
                v.wait_ge(p.S["dma"], p.C["dma"])
                p.inc(v.tensor_scalar(imix[:], mixs[:], -1.0, 1.0, ALU.mult, ALU.add), "dve")
        pairs = []
        for src, dst, noc in ((wkt_ap, wktb, 8), (wvt_ap, wvtb, 8), (wrt_ap, wrtb, 8),
                              (wkc_ap, wkcb, 32), (wvc_ap, wvcb, 8), (wrc_ap, wrcb, 8)):
            for oc in range(noc):
                pairs.append((src[oc, :, :], dst[oc, :, :]))
        emit_precast(p, pairs, obuf=cbuf)
        p.barrier(("sync",))

        for seq in range(SEQ):
            with p.block() as blk:
                @blk.vector
                def _(v):
                    p.fence(v, "dve")
                    p.inc(v.memset(carry[:], 0.0), "dve")
                    p.inc(v.memset(hprev[:], 0.0), "dve")
            p.barrier(("vector",))
            for sg in range(NSG):
                tok0 = seq * 2048 + sg * SEG
                par, npar = (sg % 2) * 16, ((sg + 1) % 2) * 16
                emit_ln(p, p.X, dst=XA, ntok=SEG, src_tok0=tok0,
                        dst_stride=SEG, dst_tok0=0, scratch=cbuf)
                emit_ln(p, XA, dst=Hseg, ntok=SEG, src_stride=SEG, dst_stride=SEG,
                        gamma=lnab, beta=lnab, gb_col0=0, beta_col0=8, scratch=cbuf)
                ops = []
                for c in range(FCH):
                    ops.append(("vector", lambda v, c=c: v.tensor_copy(HS[:, c * SEG + 1:(c + 1) * SEG], Hseg[:, c * SEG:(c + 1) * SEG - 1])))
                    ops.append(("vector", lambda v, c=c: v.tensor_copy(HS[:, c * SEG:c * SEG + 1], hprev[:, par + c:par + c + 1])))
                    ops.append(("vector", lambda v, c=c: v.tensor_copy(hprev[:, npar + c:npar + c + 1], Hseg[:, (c + 1) * SEG - 1:(c + 1) * SEG])))
                emit_seq(p, ops)
                for i3, (w_ap, mcol) in enumerate(((wktb, 0), (wvtb, 8), (wrtb, 16))):
                    ops = []
                    for c in range(FCH):
                        ops.append(("vector", lambda v, c=c, mcol=mcol: v.tensor_scalar(t0, Hseg[:, c * SEG:(c + 1) * SEG], mixs[:, mcol + c:mcol + c + 1], None, ALU.mult)))
                        ops.append(("vector", lambda v, c=c, mcol=mcol: v.scalar_tensor_tensor(TSB[:, c * SEG:(c + 1) * SEG], HS[:, c * SEG:(c + 1) * SEG], imix[:, mcol + c:mcol + c + 1], t0, ALU.mult, ALU.add)))
                    emit_seq(p, ops)
                    p.barrier(("vector", "tensor", "scalar"))
                    if i3 == 0:
                        def epi(a, oc, j, psv):
                            return a.activation(Kf[:, oc * SEG:(oc + 1) * SEG], psv, AF.Identity, scale=1.0)
                    elif i3 == 1:
                        def epi(a, oc, j, psv):
                            return a.activation(Vf[:, oc * SEG:(oc + 1) * SEG], psv, AF.Identity, scale=1.0)
                    else:
                        def epi(a, oc, j, psv):
                            return a.activation(Rsig[:, oc * SEG:(oc + 1) * SEG], psv, AF.Sigmoid, scale=1.0)
                    emit_gemm(p, lambda oc, hh, w_ap=w_ap: w_ap[oc, :, :], FCH, FCH,
                              lambda kc, j: TSB[:, kc * SEG:(kc + 1) * SEG], epi,
                              wdtype=BF16, kcw=8, nbuf=1)
                allops = []
                for c in range(FCH):
                    K_ = Kf[:, c * SEG:(c + 1) * SEG]
                    V_ = Vf[:, c * SEG:(c + 1) * SEG]
                    Cex = SSc[:, 0:SEG]
                    Cin = SSc[:, 1:SEG + 1]
                    Dex = SSd[:, 0:SEG]
                    Din = SSd[:, 1:SEG + 1]
                    Rex = SSs[:, 0:SEG]
                    ops = []
                    ops.append(("scalar", lambda a, K_=K_: a.activation(t0, K_, AF.Exp, scale=1.0)))
                    ops.append(("vector", lambda v, V_=V_: v.tensor_tensor(t1, t0, V_, ALU.mult)))
                    ops.append(("vector", lambda v, c=c: v.tensor_copy(SSc[:, 0:1], carry[:, c:c + 1])))
                    ops.append(("vector", lambda v, c=c, Cin=Cin: v.tensor_tensor_scan(Cin, t0, t0, carry[:, c:c + 1], ALU.add, ALU.bypass)))
                    ops.append(("vector", lambda v, c=c: v.tensor_copy(carry[:, c:c + 1], SSc[:, SEG:SEG + 1])))
                    ops.append(("vector", lambda v, Cex=Cex: v.tensor_tensor(t2, Cex, t0, ALU.max)))
                    ops.append(("vector", lambda v, c=c, Din=Din: v.tensor_tensor_scan(Din, t2, t2, carry[:, 24 + c:25 + c], ALU.add, ALU.bypass)))
                    ops.append(("vector", lambda v, c=c: v.tensor_copy(carry[:, 24 + c:25 + c], SSd[:, SEG:SEG + 1])))
                    ops.append(("vector", lambda v, c=c: v.tensor_copy(SSd[:, 0:1], carry[:, 16 + c:17 + c])))
                    ops.append(("vector", lambda v, Dex=Dex, Cex=Cex: v.scalar_tensor_tensor(t3, Dex, EPSW, Cex, ALU.mult, ALU.add)))
                    ops.append(("vector", lambda v: v.tensor_tensor(t2, t3, t0, ALU.max)))
                    ops.append(("vector", lambda v, c=c, Din=Din: v.tensor_tensor_scan(Din, t2, t2, carry[:, 16 + c:17 + c], ALU.add, ALU.bypass)))
                    ops.append(("vector", lambda v, c=c: v.tensor_copy(carry[:, 16 + c:17 + c], SSd[:, SEG:SEG + 1])))
                    ops.append(("vector", lambda v, Din=Din, Cin=Cin: v.scalar_tensor_tensor(t3, Din, EPSW, Cin, ALU.mult, ALU.add)))
                    ops.append(("vector", lambda v: v.reciprocal(t2, t2)))
                    ops.append(("vector", lambda v: v.tensor_tensor(t3, t3, t2, ALU.mult)))
                    ops.append(("vector", lambda v, c=c: v.tensor_copy(SSs[:, 0:1], carry[:, 8 + c:9 + c])))
                    ops.append(("vector", lambda v, c=c: v.tensor_tensor_scan(SSs[:, 1:SEG + 1], t1, t3, carry[:, 8 + c:9 + c], ALU.add, ALU.mult)))
                    ops.append(("vector", lambda v, c=c: v.tensor_copy(carry[:, 8 + c:9 + c], SSs[:, SEG:SEG + 1])))
                    ops.append(("scalar", lambda a, K_=K_, c=c: a.activation(t0, K_, AF.Exp, bias=ucol[:, c:c + 1], scale=1.0)))
                    ops.append(("vector", lambda v, Dex=Dex, Cex=Cex: v.scalar_tensor_tensor(t2, Dex, EPSW, Cex, ALU.mult, ALU.add)))
                    ops.append(("vector", lambda v: v.tensor_tensor(t1, t2, t0, ALU.max)))
                    ops.append(("vector", lambda v: v.scalar_tensor_tensor(t1, t1, EPSW, t0, ALU.mult, ALU.add)))
                    ops.append(("vector", lambda v: v.tensor_tensor(t1, t1, t2, ALU.add)))
                    ops.append(("vector", lambda v, V_=V_: v.tensor_tensor(t0, t0, V_, ALU.mult)))
                    ops.append(("vector", lambda v, Rex=Rex: v.tensor_tensor(t0, t0, Rex, ALU.add)))
                    ops.append(("vector", lambda v: v.reciprocal(t1, t1)))
                    ops.append(("vector", lambda v: v.tensor_tensor(t0, t0, t1, ALU.mult)))
                    ops.append(("vector", lambda v, c=c: v.tensor_tensor(t0, t0, Rsig[:, c * SEG:(c + 1) * SEG], ALU.mult)))
                    ops.append(("vector", lambda v, c=c: v.tensor_tensor(XA[:, c * SEG:(c + 1) * SEG], XA[:, c * SEG:(c + 1) * SEG], t0, ALU.add)))
                    allops.extend(ops)
                emit_seq(p, allops)
                emit_ln(p, XA, dst=Hseg, ntok=SEG, src_stride=SEG, dst_stride=SEG,
                        gamma=lnab, beta=lnab, gb_col0=16, beta_col0=24, scratch=cbuf)
                ops = []
                for c in range(FCH):
                    ops.append(("vector", lambda v, c=c: v.tensor_copy(HS[:, c * SEG + 1:(c + 1) * SEG], Hseg[:, c * SEG:(c + 1) * SEG - 1])))
                    ops.append(("vector", lambda v, c=c: v.tensor_copy(HS[:, c * SEG:c * SEG + 1], hprev[:, par + 8 + c:par + 9 + c])))
                    ops.append(("vector", lambda v, c=c: v.tensor_copy(hprev[:, npar + 8 + c:npar + 9 + c], Hseg[:, (c + 1) * SEG - 1:(c + 1) * SEG])))
                emit_seq(p, ops)
                for i3, (w_ap, mcol, och) in enumerate(((wkcb, 24, 32), (wrcb, 32, 8))):
                    ops = []
                    for c in range(FCH):
                        ops.append(("vector", lambda v, c=c, mcol=mcol: v.tensor_scalar(t0, Hseg[:, c * SEG:(c + 1) * SEG], mixs[:, mcol + c:mcol + c + 1], None, ALU.mult)))
                        ops.append(("vector", lambda v, c=c, mcol=mcol: v.scalar_tensor_tensor(TSB[:, c * SEG:(c + 1) * SEG], HS[:, c * SEG:(c + 1) * SEG], imix[:, mcol + c:mcol + c + 1], t0, ALU.mult, ALU.add)))
                    emit_seq(p, ops)
                    p.barrier(("vector", "tensor", "scalar"))
                    if i3 == 0:
                        def epi(v, oc, j, psv):
                            return v.tensor_scalar(KC[:, oc * SEG:(oc + 1) * SEG], psv, 0.0, None, ALU.max)
                        emit_gemm(p, lambda oc, hh, w_ap=w_ap: w_ap[oc, :, :], och, FCH,
                                  lambda kc, j: TSB[:, kc * SEG:(kc + 1) * SEG], epi,
                                  wdtype=BF16, epi_eng="vector", cast_eng="scalar", kcw=8, nbuf=1)
                    else:
                        def epi(a, oc, j, psv):
                            return a.activation(RC[:, oc * SEG:(oc + 1) * SEG], psv, AF.Sigmoid, scale=1.0)
                        emit_gemm(p, lambda oc, hh, w_ap=w_ap: w_ap[oc, :, :], och, FCH,
                                  lambda kc, j: TSB[:, kc * SEG:(kc + 1) * SEG], epi,
                                  wdtype=BF16, kcw=8, nbuf=1)
                ops = []
                for c in range(32):
                    ops.append(("scalar", lambda a, c=c: a.activation(KC[:, c * SEG:(c + 1) * SEG], KC[:, c * SEG:(c + 1) * SEG], AF.Square, scale=1.0)))
                emit_seq(p, ops)
                p.barrier(("scalar", "tensor", "vector"))

                def epi_vv(a, oc, j, psv):
                    return a.activation(VV[:, oc * SEG:(oc + 1) * SEG], psv, AF.Identity, scale=1.0)
                emit_gemm(p, lambda oc, hh: wvcb[oc, :, hh * 1024:(hh + 1) * 1024], FCH, 32,
                          lambda kc, j: KC[:, kc * SEG:(kc + 1) * SEG], epi_vv,
                          wdtype=BF16, kcw=8, nbuf=1)
                ops = []
                for c in range(FCH):
                    ops.append(("vector", lambda v, c=c: v.tensor_tensor(t0, VV[:, c * SEG:(c + 1) * SEG], RC[:, c * SEG:(c + 1) * SEG], ALU.mult)))
                    ops.append(("vector", lambda v, c=c: v.tensor_tensor(XA[:, c * SEG:(c + 1) * SEG], XA[:, c * SEG:(c + 1) * SEG], t0, ALU.add)))
                emit_seq(p, ops)
                emit_ln(p, XA, acc_into=p.X, ntok=SEG, src_stride=SEG,
                        dst_stride=TOK, dst_tok0=tok0, scratch=cbuf)


def prep_l1attn(ins):
    def tl(W):
        return np.ascontiguousarray(tile_lhsT(np.ascontiguousarray(np.asarray(W).astype(np.float32).T)))
    lnab = np.concatenate([bias_tiles(ins["L1_attn_ln1_g"]), bias_tiles(ins["L1_attn_ln1_b"]),
                           bias_tiles(ins["L1_attn_ln2_g"]), bias_tiles(ins["L1_attn_ln2_b"])], axis=1)
    mixs = np.concatenate([bias_tiles(ins["L1_attn_mix_k"]), bias_tiles(ins["L1_attn_mix_v"]),
                           bias_tiles(ins["L1_attn_mix_r"]), bias_tiles(ins["L1_attn_cmix_k"]),
                           bias_tiles(ins["L1_attn_cmix_r"])], axis=1)
    return {"L1a_wkt": tl(ins["L1_attn_Wk_t"]), "L1a_wvt": tl(ins["L1_attn_Wv_t"]),
            "L1a_wrt": tl(ins["L1_attn_Wr_t"]), "L1a_wkc": tl(ins["L1_attn_Wk_c"]),
            "L1a_wvc": tl(ins["L1_attn_Wv_c"]), "L1a_wrc": tl(ins["L1_attn_Wr_c"]),
            "L1a_lnab": np.ascontiguousarray(lnab), "L1a_mix": np.ascontiguousarray(mixs),
            "L1a_u": bias_tiles(ins["L1_attn_time_first"])}


def decl_l1attn(nc):
    return dict(
        wkt_ap=nc.dram_tensor("L1a_wkt", [8, 128, 1024], F32, kind="ExternalInput").ap(),
        wvt_ap=nc.dram_tensor("L1a_wvt", [8, 128, 1024], F32, kind="ExternalInput").ap(),
        wrt_ap=nc.dram_tensor("L1a_wrt", [8, 128, 1024], F32, kind="ExternalInput").ap(),
        wkc_ap=nc.dram_tensor("L1a_wkc", [32, 128, 1024], F32, kind="ExternalInput").ap(),
        wvc_ap=nc.dram_tensor("L1a_wvc", [8, 128, 4096], F32, kind="ExternalInput").ap(),
        wrc_ap=nc.dram_tensor("L1a_wrc", [8, 128, 1024], F32, kind="ExternalInput").ap(),
        lnab_ap=nc.dram_tensor("L1a_lnab", [128, 32], F32, kind="ExternalInput").ap(),
        mix_ap=nc.dram_tensor("L1a_mix", [128, 40], F32, kind="ExternalInput").ap(),
        u_ap=nc.dram_tensor("L1a_u", [128, 8], F32, kind="ExternalInput").ap(),
    )


def emit_l0ffn(p: Prog, router_ap, w_ap):
    nc = p.nc
    SEG = 512
    NSG2 = TOK // SEG
    FF = 22
    w1b = nc.dram_tensor("l0fw1b", [9, 22, 128, 1024], BF16).ap()
    w3b = nc.dram_tensor("l0fw3b", [9, 22, 128, 1024], BF16).ap()
    w2b = nc.dram_tensor("l0fw2b", [9, 8, 128, 2816], BF16).ap()
    with ExitStack() as es:
        rt = p.sbuf(es, [128, 64], F32)
        rtb = p.sbuf(es, [128, 64], BF16)
        Hseg = p.sbuf(es, [128, FCH * SEG], BF16)
        M1 = p.sbuf(es, [128, FF * SEG], BF16)
        OUT = p.sbuf(es, [128, FCH * SEG], F32)
        sc_tm = p.sbuf(es, [128, 32], F32)
        mx_tm = p.sbuf(es, [128, 4], F32)
        gekm = p.sbuf(es, [8, SEG], F32)
        cbuf = p.sbuf(es, [128, 3584], F32)
        GX = cbuf[:, 1536:3584].bitcast(BF16)
        ge = cbuf[0:1, 0:SEG]
        mxrow = cbuf[0:1, SEG:2 * SEG]
        grow = cbuf[0:1, 2 * SEG:3 * SEG]
        sc_km = cbuf[0:8, 3 * SEG:4 * SEG]
        geq = p.take_dq(1, sw=True)[0]
        p.barrier(("sync",))
        with p.block() as blk:
            @blk.sync
            def _(s):
                p.inc(s.dma_start(out=rt[:].rearrange("p (kc e) -> p kc e", kc=8),
                                  in_=router_ap.rearrange("kc p e -> p kc e")), "dma", 16)
            @blk.vector
            def _(v):
                v.wait_ge(p.S["dma"], p.C["dma"])
                p.inc(v.tensor_copy(rtb[:], rt[:]), "dve")
        pairs = []
        for e in range(9):
            for oc in range(FF):
                pairs.append((w_ap[0][e, oc, :, :], w1b[e, oc, :, :]))
                pairs.append((w_ap[1][e, oc, :, :], w3b[e, oc, :, :]))
            for oc in range(FCH):
                pairs.append((w_ap[2][e, oc, :, :], w2b[e, oc, :, :]))
        emit_precast(p, pairs, obuf=cbuf)
        p.barrier(("sync",))
        for sg in range(NSG2):
            tok0 = sg * SEG
            emit_ln(p, p.X, dst=Hseg, ntok=SEG, src_tok0=tok0,
                    dst_stride=SEG, dst_tok0=0, scratch=cbuf)
            with ExitStack() as es2:
                ps_s = p.psum(es2, [8, 512])
                ps_t = [p.psum(es2, [128, 8], F32) for _ in range(2)]
                ps_r = [p.psum(es2, [1, 128], F32) for _ in range(2)]
                ps_b = p.psum(es2, [8, 512])
                with p.block() as blk:
                    @blk.tensor
                    def _(t):
                        base = dict(p.C)
                        t.wait_ge(p.S["dve"], base["dve"])
                        for kc in range(FCH):
                            ins = t.matmul(ps_s[:], rtb[:, kc * 8:(kc + 1) * 8],
                                           Hseg[:, kc * SEG:(kc + 1) * SEG],
                                           start=(kc == 0), stop=(kc == FCH - 1))
                        p.inc(ins, "pe")
                    @blk.scalar
                    def _(a):
                        a.wait_ge(p.S["pe"], p.C["pe"])
                        p.inc(a.activation(sc_km[:], ps_s[:, 0:SEG], AF.Identity, scale=1.0), "act")
                with p.block() as blk:
                    base = dict(p.C)
                    @blk.tensor
                    def _(t):
                        for j in range(4):
                            if j >= 2:
                                t.wait_ge(p.S["dve"], base["dve"] + 2 * (j - 1))
                            p.inc(t.transpose(ps_t[j % 2][:], sc_km[:, j * 128:(j + 1) * 128], p.eyef[0:8, 0:8]), "pe")
                    @blk.vector
                    def _(v):
                        for j in range(4):
                            v.wait_ge(p.S["pe"], base["pe"] + j + 1)
                            p.fence(v, "dve")
                            p.inc(v.tensor_copy(sc_tm[:, j * 8:(j + 1) * 8], ps_t[j % 2][:]), "dve")
                            p.fence(v, "dve")
                            p.inc(v.tensor_reduce(mx_tm[:, j:j + 1], sc_tm[:, j * 8:(j + 1) * 8], mybir.AxisListType.X, ALU.max), "dve")
                with p.block() as blk:
                    base = dict(p.C)
                    @blk.tensor
                    def _(t):
                        for j in range(4):
                            if j >= 2:
                                t.wait_ge(p.S["act"], base["act"] + j - 1)
                            p.inc(t.transpose(ps_r[j % 2][:], mx_tm[:, j:j + 1], p.eyef[:]), "pe")
                    @blk.scalar
                    def _(a):
                        for j in range(4):
                            a.wait_ge(p.S["pe"], base["pe"] + j + 1)
                            p.inc(a.activation(mxrow[0:1, j * 128:(j + 1) * 128], ps_r[j % 2][:], AF.Identity, scale=1.0), "act")
                with p.block() as blk:
                    @blk.scalar
                    def _(a):
                        p.fence(a, "act")
                        p.inc(a.activation(grow[:], mxrow[:], AF.Sigmoid, scale=1.0), "act")
                    @blk.tensor
                    def _(t):
                        p.inc(t.matmul(ps_b[:, 0:SEG], p.rowsel[0:1, 0:8], mxrow[:], start=True, stop=True), "pe")
                    @blk.vector
                    def _(v):
                        v.wait_ge(p.S["pe"], p.C["pe"])
                        p.fence(v, "dve")
                        p.inc(v.tensor_tensor(gekm[:], sc_km[:], ps_b[:, 0:SEG], ALU.is_ge), "dve")
                with p.block() as blk:
                    @blk.tensor
                    def _(t):
                        p.inc(t.matmul(ps_b[:, 0:SEG], p.rowsel[0:1, 0:8], grow[:], start=True, stop=True), "pe")
                    @blk.vector
                    def _(v):
                        v.wait_ge(p.S["pe"], p.C["pe"])
                        p.fence(v, "dve")
                        p.inc(v.tensor_tensor(gekm[:], gekm[:], ps_b[:, 0:SEG], ALU.mult), "dve")
                p.barrier(("tensor", "vector", "scalar"))
            with ExitStack() as es2:
                ps_bc = p.psum(es2, [128, 512])
                for e in range(9):
                    if e < 8:
                        with p.block() as blk:
                            @blk.gpsimd
                            def _(g):
                                g.wait_ge(p.S["dve"], p.C["dve"])
                                g.wait_ge(p.S["pe"], p.C["pe"])
                                p.inc(g.dma_start(out=ge[:], in_=gekm[e:e + 1, :]), geq, 16)
                        with p.block() as blk:
                            @blk.tensor
                            def _(t):
                                t.wait_ge(p.S[geq], p.C[geq])
                                p.inc(t.matmul(ps_bc[:], p.rowsel[0:1, 0:128], ge[:], start=True, stop=True), "pe")
                            @blk.vector
                            def _(v):
                                v.wait_ge(p.S["pe"], p.C["pe"])
                                for c in range(FCH):
                                    p.fence(v, "dve")
                                    p.inc(v.tensor_tensor(GX[:, c * SEG:(c + 1) * SEG],
                                                          Hseg[:, c * SEG:(c + 1) * SEG], ps_bc[:], ALU.mult), "dve")
                        p.barrier(("tensor", "vector", "gpsimd", "sync"))
                        rhs_x = GX
                    else:
                        rhs_x = Hseg

                    def epi1(a, oc, j, psv):
                        return a.activation(M1[:, oc * SEG:(oc + 1) * SEG], psv, AF.Silu, scale=1.0)
                    emit_gemm(p, lambda oc, hh, e=e: w1b[e, oc, :, :], FF, FCH,
                              lambda kc, j, rhs_x=rhs_x: rhs_x[:, kc * SEG:(kc + 1) * SEG], epi1,
                              wdtype=BF16, kcw=8, nbuf=1)

                    def epi3(v, oc, j, psv):
                        return v.tensor_tensor(M1[:, oc * SEG:(oc + 1) * SEG], psv,
                                               M1[:, oc * SEG:(oc + 1) * SEG], ALU.mult)
                    emit_gemm(p, lambda oc, hh, e=e: w3b[e, oc, :, :], FF, FCH,
                              lambda kc, j, rhs_x=rhs_x: rhs_x[:, kc * SEG:(kc + 1) * SEG], epi3,
                              wdtype=BF16, epi_eng="vector", cast_eng="scalar", kcw=8, nbuf=1)

                    if e == 0:
                        def epi2(v, oc, j, psv):
                            return v.tensor_copy(OUT[:, oc * SEG:(oc + 1) * SEG], psv)
                    else:
                        def epi2(v, oc, j, psv):
                            return v.tensor_tensor(OUT[:, oc * SEG:(oc + 1) * SEG], OUT[:, oc * SEG:(oc + 1) * SEG], psv, ALU.add)
                    emit_gemm(p, lambda oc, hh, e=e: w2b[e, oc, :, hh * 1408:(hh + 1) * 1408], FCH, FF,
                              lambda kc, j: M1[:, kc * SEG:(kc + 1) * SEG], epi2,
                              wdtype=BF16, epi_eng="vector", cast_eng="scalar", kcw=11, nbuf=1)
            emit_ln(p, OUT, acc_into=p.X, ntok=SEG, src_stride=SEG,
                    dst_stride=TOK, dst_tok0=tok0, scratch=cbuf)


def prep_l0ffn(ins):
    r = np.asarray(ins["L0_ffn_router_DE"]).astype(np.float32).reshape(8, 128, 8)
    w1 = np.asarray(ins["L0_ffn_w1_eDF"]).astype(np.float32)
    w3 = np.asarray(ins["L0_ffn_w3_eDF"]).astype(np.float32)
    w2 = np.asarray(ins["L0_ffn_w2_eFD"]).astype(np.float32)
    ws1 = np.asarray(ins["L0_ffn_ws1_DF"]).astype(np.float32)
    ws3 = np.asarray(ins["L0_ffn_ws3_DF"]).astype(np.float32)
    ws2 = np.asarray(ins["L0_ffn_ws2_FD"]).astype(np.float32)
    def t9(mats):
        return np.stack([tile_lhsT(m) for m in mats], axis=0)
    w1t = t9([w1[e] for e in range(8)] + [ws1])
    w3t = t9([w3[e] for e in range(8)] + [ws3])
    w2t = t9([w2[e] for e in range(8)] + [ws2])
    return {"L0f_rt": np.ascontiguousarray(r), "L0f_w1": np.ascontiguousarray(w1t),
            "L0f_w3": np.ascontiguousarray(w3t), "L0f_w2": np.ascontiguousarray(w2t)}


def decl_l0ffn(nc):
    w1 = nc.dram_tensor("L0f_w1", [9, 22, 128, 1024], F32, kind="ExternalInput").ap()
    w3 = nc.dram_tensor("L0f_w3", [9, 22, 128, 1024], F32, kind="ExternalInput").ap()
    w2 = nc.dram_tensor("L0f_w2", [9, 8, 128, 2816], F32, kind="ExternalInput").ap()
    return dict(router_ap=nc.dram_tensor("L0f_rt", [8, 128, 8], F32, kind="ExternalInput").ap(),
                w_ap=(w1, w3, w2))


def emit_l1ffn(p: Prog, wexp_ap, wcon_ap, upc_ap, dnc_ap, bexp_ap, biases_ap):
    nc = p.nc
    w1p = nc.dram_tensor("l1w1p", [32, 128, 1024], BF16).ap()
    w2tmp = nc.dram_tensor("l1w2t", [8, 128, 4096], BF16).ap()
    w2p = nc.dram_tensor("l1w2p", [8, 128, 4096], BF16).ap()
    es0 = ExitStack()
    b1t = p.sbuf(es0, [128, 32], F32)
    b2t = p.sbuf(es0, [128, 8], F32)
    with ExitStack() as es:
        WF = p.sbuf(es, [128, 4096], F32)
        WFb = p.sbuf(es, [128, 4096], BF16)
        brow = p.sbuf(es, [1, 4096], BF16)
        cf = [p.sbuf(es, [128, 2048], BF16) for _ in range(4)]
        t1 = p.sbuf(es, [128, 2048], F32)
        t2 = p.sbuf(es, [128, 2048], F32)
        t3 = WFb[:].bitcast(F32)[:, 0:2048]
        crow = [t1[0:1, 0:2048], t2[0:1, 0:2048]]
        bs = p.sbuf(es, [128, 48], F32)
        wq = p.take_dq(2)
        cq = p.take_dq(1)[0]
        sq = p.take_dq(1, sw=True)[0]
        xspill = nc.dram_tensor("l1xsp", [128, FCH * TOK], F32).ap()
        p.barrier(("sync",))
        with p.block() as blk:
            @blk.sync
            def _(s):
                p.inc(s.dma_start(out=xspill, in_=p.X[:]), "dmo", 16)
                p.inc(s.dma_start(out=bs[:], in_=biases_ap), "dma", 16)

        def WF8(pp):
            return p.X[:, pp * 4096:(pp + 1) * 4096]

        for side in range(2):
            w_in = wexp_ap if side == 0 else wcon_ap
            c_in = upc_ap if side == 0 else dnc_ap
            strides = [2 ** t for t in range(12)] if side == 0 else [2 ** (11 - t) for t in range(12)]
            with p.block() as blk:
                @blk.sync
                def _(s):
                    s.wait_ge(p.S["dmo"], p.C["dmo"])
                    s.wait_ge(p.S["dve"], p.C["dve"])
                    s.wait_ge(p.S[sq], p.C[sq])
                    s.wait_ge(p.S[wq[0]], p.C[wq[0]])
                    for pp in range(8):
                        p.inc(s.dma_start(out=WF8(pp), in_=w_in[pp]), wq[0], 16)
                    if side == 0:
                        s.wait_ge(p.S[wq[1]], p.C[wq[1]])
                        p.inc(s.dma_start(out=crow[0][:], in_=bexp_ap[0:1, 0:2048]), wq[1], 16)
                        p.inc(s.dma_start(out=crow[1][:], in_=bexp_ap[0:1, 2048:4096]), wq[1], 16)
                @blk.vector
                def _(v):
                    v.wait_ge(p.S[wq[0]], p.C[wq[0]])
                    v.wait_ge(p.S[wq[1]], p.C[wq[1]])
                    if side == 0:
                        p.inc(v.tensor_copy(brow[0:1, 0:2048], crow[0][:]), "dve")
                        p.inc(v.tensor_copy(brow[0:1, 2048:4096], crow[1][:]), "dve")
            for sti, sz in enumerate(strides):
                with ExitStack() as es2:
                    ps_c = p.psum(es2, [128, 512])
                    base = dict(p.C)
                    with p.block() as blk:
                        @blk.sync
                        def _(s):
                            for i4 in range(4):
                                if i4 >= 2:
                                    s.wait_ge(p.S["act"], base["act"] + 4 * (i4 - 1))
                                    s.wait_ge(p.S[wq[i4 % 2]], base[wq[i4 % 2]] + 16)
                                else:
                                    s.wait_ge(p.S["pe"], base["pe"])
                                    s.wait_ge(p.S["dve"], base["dve"])
                                    s.wait_ge(p.S[wq[i4 % 2]], base[wq[i4 % 2]])
                                p.inc(s.dma_start(out=crow[i4 % 2][:],
                                                  in_=c_in[sti, i4].rearrange("(one n) -> one n", one=1)), wq[i4 % 2], 16)
                        @blk.tensor
                        def _(t):
                            t.wait_ge(p.S["dve"], base["dve"])
                            for i4 in range(4):
                                t.wait_ge(p.S[wq[i4 % 2]], base[wq[i4 % 2]] + 16 * (i4 // 2 + 1))
                                for j in range(4):
                                    if i4 * 4 + j >= 1:
                                        t.wait_ge(p.S["act"], base["act"] + i4 * 4 + j)
                                    p.inc(t.matmul(ps_c[:], p.rowsel[0:1, 0:128],
                                                   crow[i4 % 2][0:1, j * 512:(j + 1) * 512],
                                                   start=True, stop=True), "pe")
                        @blk.scalar
                        def _(a):
                            for i4 in range(4):
                                for j in range(4):
                                    a.wait_ge(p.S["pe"], base["pe"] + i4 * 4 + j + 1)
                                    p.inc(a.activation(cf[i4][:, j * 512:(j + 1) * 512], ps_c[:], AF.Identity, scale=1.0), "act")
                    def rs(x):
                        return x.rearrange("q (nb i) -> q nb i", i=sz)
                    ops = []
                    for pp in range(8):
                        W4 = WF8(pp)
                        av = W4.rearrange("q (nb two i) -> q nb two i", two=2, i=sz)[:, :, 0, :]
                        bv = W4.rearrange("q (nb two i) -> q nb two i", two=2, i=sz)[:, :, 1, :]
                        ops.append(("vector", lambda v, av=av: v.tensor_tensor(rs(t1[:]), av, rs(cf[0][:]), ALU.mult)))
                        ops.append(("vector", lambda v, av=av: v.tensor_tensor(rs(t2[:]), av, rs(cf[2][:]), ALU.mult)))
                        ops.append(("vector", lambda v, bv=bv: v.tensor_tensor(rs(t3[:]), bv, rs(cf[1][:]), ALU.mult)))
                        ops.append(("vector", lambda v, av=av: v.tensor_tensor(av, rs(t1[:]), rs(t3[:]), ALU.add)))
                        ops.append(("vector", lambda v, bv=bv: v.tensor_tensor(rs(t3[:]), bv, rs(cf[3][:]), ALU.mult)))
                        ops.append(("vector", lambda v, bv=bv: v.tensor_tensor(bv, rs(t2[:]), rs(t3[:]), ALU.add)))
                    if side == 0:
                        bav = brow[:].rearrange("q (nb two i) -> q nb two i", two=2, i=sz)[:, :, 0, :]
                        bbv = brow[:].rearrange("q (nb two i) -> q nb two i", two=2, i=sz)[:, :, 1, :]
                        ops.append(("vector", lambda v, bav=bav: v.tensor_tensor(rs(t1[0:1, :]), bav, rs(cf[0][0:1, :]), ALU.mult)))
                        ops.append(("vector", lambda v, bav=bav: v.tensor_tensor(rs(t2[0:1, :]), bav, rs(cf[2][0:1, :]), ALU.mult)))
                        ops.append(("vector", lambda v, bbv=bbv: v.tensor_tensor(rs(t3[0:1, :]), bbv, rs(cf[1][0:1, :]), ALU.mult)))
                        ops.append(("vector", lambda v, bav=bav: v.tensor_tensor(bav, rs(t1[0:1, :]), rs(t3[0:1, :]), ALU.add)))
                        ops.append(("vector", lambda v, bbv=bbv: v.tensor_tensor(rs(t3[0:1, :]), bbv, rs(cf[3][0:1, :]), ALU.mult)))
                        ops.append(("vector", lambda v, bbv=bbv: v.tensor_tensor(bbv, rs(t2[0:1, :]), rs(t3[0:1, :]), ALU.add)))
                    emit_seq(p, ops)
            for pp in range(8):
                with p.block() as blk:
                    @blk.vector
                    def _(v, pp=pp):
                        p.fence(v, "dve")
                        v.wait_ge(p.S[sq], p.C[sq])
                        p.inc(v.tensor_copy(WFb[:], WF8(pp)), "dve")
                    @blk.gpsimd
                    def _(g, pp=pp):
                        g.wait_ge(p.S["dve"], p.C["dve"])
                        if side == 0:
                            dst = BassAP(w1p.tensor, pp * 128,
                                         [[1024, 128], [131072, 32], [1, 128]])
                            p.inc(g.dma_start(out=dst, in_=WFb[:].rearrange("q (oc m) -> q oc m", oc=32)), sq, 16)
                        else:
                            p.inc(g.dma_start(out=w2tmp[pp], in_=WFb[:]), sq, 16)
        p.barrier(("sync",))
        with p.block() as blk:
            @blk.sync
            def _(s):
                p.inc(s.dma_start(out=p.X[:], in_=xspill), "dma", 16)
        p.barrier()
        with ExitStack() as es2:
            ps_tb = [p.psum(es2, [128, 1], BF16) for _ in range(2)]
            base = dict(p.C)
            with p.block() as blk:
                @blk.tensor
                def _(t):
                    for oc in range(32):
                        if oc >= 2:
                            t.wait_ge(p.S["act"], base["act"] + oc - 1)
                        p.inc(t.transpose(ps_tb[oc % 2][:], brow[0:1, oc * 128:(oc + 1) * 128], p.eye[0:1, 0:1]), "pe")
                @blk.scalar
                def _(a):
                    for oc in range(32):
                        a.wait_ge(p.S["pe"], base["pe"] + oc + 1)
                        p.inc(a.activation(b1t[:, oc:oc + 1], ps_tb[oc % 2][:], AF.Identity, scale=1.0), "act")
            with p.block() as blk:
                @blk.vector
                def _(v):
                    v.wait_ge(p.S["act"], p.C["act"])
                    v.wait_ge(p.S["dma"], p.C["dma"])
                    p.fence(v, "dve")
                    p.inc(v.tensor_tensor(b1t[:], b1t[:], bs[:, 0:32], ALU.add), "dve")
                    p.inc(v.tensor_tensor(b2t[:], bs[:, 32:40], bs[:, 40:48], ALU.add), "dve")
        p.barrier()
        sqs = p.take_dq(2, sw=True)
        with ExitStack() as es2:
            ps_t2 = [p.psum(es2, [128, 128], BF16) for _ in range(2)]
            tst = [p.sbuf(es2, [128, 128], BF16) for _ in range(2)]
            for ooc in range(8):
                with p.block() as blk:
                    @blk.sync
                    def _(s):
                        s.wait_ge(p.S["pe"], p.C["pe"])
                        s.wait_ge(p.S[sq], p.C[sq])
                        p.inc(s.dma_start(out=WFb[:, 0:4096], in_=w2tmp[ooc]), wq[0], 16)
                    @blk.vector
                    def _(v):
                        v.wait_ge(p.S[wq[0]], p.C[wq[0]])
                base2 = dict(p.C)
                with p.block() as blk:
                    @blk.tensor
                    def _(t):
                        for hc in range(32):
                            if hc >= 2:
                                t.wait_ge(p.S["dve"], base2["dve"] + hc - 1)
                            p.inc(t.transpose(ps_t2[hc % 2][:], WFb[:, hc * 128:(hc + 1) * 128], p.eye[:]), "pe")
                    @blk.vector
                    def _(v):
                        for hc in range(32):
                            v.wait_ge(p.S["pe"], base2["pe"] + hc + 1)
                            if hc >= 2:
                                v.wait_ge(p.S[sqs[hc % 2]], base2[sqs[hc % 2]] + 16 * (hc // 2))
                            elif base2[sqs[hc % 2]] > 0:
                                v.wait_ge(p.S[sqs[hc % 2]], base2[sqs[hc % 2]])
                            p.fence(v, "dve")
                            p.inc(v.tensor_copy(tst[hc % 2][:], ps_t2[hc % 2][:]), "dve")
                    @blk.gpsimd
                    def _(g):
                        for hc in range(32):
                            g.wait_ge(p.S["dve"], base2["dve"] + hc + 1)
                            p.inc(g.dma_start(out=w2p[ooc, :, hc * 128:(hc + 1) * 128], in_=tst[hc % 2][:]), sqs[hc % 2], 16)
        p.barrier()
    SEG = 512
    with ExitStack() as es:
        Hseg = p.sbuf(es, [128, FCH * SEG], BF16)
        G = p.sbuf(es, [128, 32 * SEG], BF16)
        Yseg = p.sbuf(es, [128, FCH * SEG], F32)
        for sg in range(TOK // SEG):
            tok0 = sg * SEG
            emit_ln(p, p.X, dst=Hseg, ntok=SEG, src_tok0=tok0,
                    dst_stride=SEG, dst_tok0=0, scratch=Yseg)

            def epi1(a, oc, j, psv):
                return a.activation(G[:, oc * SEG:(oc + 1) * SEG], psv,
                                    AF.Gelu, bias=b1t[:, oc:oc + 1], scale=1.0)
            emit_gemm(p, lambda oc, hh: w1p[oc, :, :], 32, FCH,
                      lambda kc, j: Hseg[:, kc * SEG:(kc + 1) * SEG], epi1,
                      wdtype=BF16, kcw=8, nbuf=1)

            def epi2(a, oc, j, psv):
                return a.activation(Yseg[:, oc * SEG:(oc + 1) * SEG], psv,
                                    AF.Identity, bias=b2t[:, oc:oc + 1], scale=1.0)
            emit_gemm(p, lambda oc, hh: w2p[oc, :, hh * 1024:(hh + 1) * 1024], FCH, 32,
                      lambda kc, j: G[:, kc * SEG:(kc + 1) * SEG], epi2,
                      wdtype=BF16, kcw=8, nbuf=1)

            emit_ln(p, Yseg, acc_into=p.X, ntok=SEG, src_stride=SEG,
                    dst_stride=TOK, dst_tok0=tok0, scratch=G[:].bitcast(F32))
    es0.close()


def prep_l1ffn(ins):
    wexp = np.ascontiguousarray(np.asarray(ins["L1_ffn_W_exp"]).astype(np.float32).reshape(8, 128, 4096))
    wcon = np.ascontiguousarray(np.asarray(ins["L1_ffn_W_con"]).astype(np.float32).T.reshape(8, 128, 4096))
    up = np.asarray(ins["L1_ffn_up_weights"]).astype(np.float32)
    dn = np.asarray(ins["L1_ffn_down_weights"]).astype(np.float32)
    upc = np.ascontiguousarray(up.transpose(0, 2, 3, 1).reshape(12, 4, 2048))
    dnt = dn.transpose(0, 3, 2, 1).reshape(12, 4, 2048)
    dnc = np.ascontiguousarray(dnt[::-1])
    bexp = np.ascontiguousarray(np.asarray(ins["L1_ffn_b_exp"]).astype(np.float32).reshape(1, 4096))
    biases = np.concatenate([bias_tiles(ins["L1_ffn_up_bias"]),
                             bias_tiles(ins["L1_ffn_b_con"]),
                             bias_tiles(ins["L1_ffn_down_bias"])], axis=1)
    return {"L1f_wexp": wexp, "L1f_wcon": wcon, "L1f_upc": upc, "L1f_dnc": dnc,
            "L1f_bexp": bexp, "L1f_bias": np.ascontiguousarray(biases)}


def decl_l1ffn(nc):
    return dict(
        wexp_ap=nc.dram_tensor("L1f_wexp", [8, 128, 4096], F32, kind="ExternalInput").ap(),
        wcon_ap=nc.dram_tensor("L1f_wcon", [8, 128, 4096], F32, kind="ExternalInput").ap(),
        upc_ap=nc.dram_tensor("L1f_upc", [12, 4, 2048], F32, kind="ExternalInput").ap(),
        dnc_ap=nc.dram_tensor("L1f_dnc", [12, 4, 2048], F32, kind="ExternalInput").ap(),
        bexp_ap=nc.dram_tensor("L1f_bexp", [1, 4096], F32, kind="ExternalInput").ap(),
        biases_ap=nc.dram_tensor("L1f_bias", [128, 48], F32, kind="ExternalInput").ap(),
    )


BLOCKS = [
    ("l0attn", decl_l0attn, emit_l0attn, prep_l0attn, True, False),
    ("l0ffn", decl_l0ffn, emit_l0ffn, prep_l0ffn, True, True),
    ("l1attn", decl_l1attn, emit_l1attn, prep_l1attn, False, False),
    ("l1ffn", decl_l1ffn, emit_l1ffn, prep_l1ffn, True, True),
    ("l2attn", decl_l2attn, emit_l2attn, prep_l2attn, False, False),
    ("l2ffn", decl_l2ffn, emit_l2ffn, prep_l2ffn, False, False),
    ("l3attn", decl_l3attn, emit_l3attn, prep_l3attn, True, False),
    ("l3ffn", decl_l3ffn, emit_l3ffn, prep_l3ffn, False, False),
]

_prog_cache = {}


def build_block_program(idx):
    if idx in _prog_cache:
        return _prog_cache[idx]
    name, decl, emit, _, use_eye, use_eyef = BLOCKS[idx]
    nc = bass.Bass("TRN2", target_bir_lowering=False)
    x_in = nc.dram_tensor("x", [D, TOK], F32, kind="ExternalInput").ap()
    aux_in = nc.dram_tensor("aux", [128, AUXC], F32, kind="ExternalInput").ap()
    sel_in = nc.dram_tensor("sel", [128, 64], BF16, kind="ExternalInput").ap()
    rowsel_in = nc.dram_tensor("rowsel", [8, 512], F32, kind="ExternalInput").ap()
    eye_in = nc.dram_tensor("eye", [128, 128], BF16, kind="ExternalInput").ap() if use_eye else None
    eyef_in = nc.dram_tensor("eyef", [128, 128], F32, kind="ExternalInput").ap() if use_eyef else None
    x_out = nc.dram_tensor("xo", [D, TOK], F32, kind="ExternalOutput").ap()
    daps = decl(nc)
    with ExitStack() as es:
        p = Prog(nc, es)
        with nc.Block() as blk:
            p.blk = blk
            emit_setup(p, x_in, aux_in, sel_in, rowsel_in, eye_in=eye_in, eyef_in=eyef_in)
            emit(p, **daps)
            emit_store_x(p, x_out)
        p.blk = None
    _prog_cache[idx] = nc
    return nc


def kernel(**inputs):
    consts = {"aux": host_aux(), "sel": host_sel(),
              "rowsel": host_rowsel().astype(np.float32),
              "eye": host_eye(), "eyef": np.eye(128, dtype=np.float32)}
    xs = [feat_major(np.asarray(inputs["L0_attn_x"])[2 * c:2 * c + 2])
          for c in range(NCORES)]
    for idx, (name, decl, emit, prep, use_eye, use_eyef) in enumerate(BLOCKS):
        nc = build_block_program(idx)
        wk = prep(inputs)
        in_maps = []
        for c in range(NCORES):
            m = dict(consts)
            if not use_eye:
                m.pop("eye", None)
            if not use_eyef:
                m.pop("eyef", None)
            m = {k: v for k, v in m.items()}
            m.update(wk)
            m["x"] = xs[c]
            in_maps.append(m)
        res = run_bass_kernel_spmd(nc, in_maps, core_ids=list(range(NCORES)))
        xs = [np.asarray(res.results[c]["xo"]) for c in range(NCORES)]
    out = np.empty((B, T, D), dtype=np.float32)
    for c in range(NCORES):
        out[2 * c:2 * c + 2] = xs[c].T.reshape(2, T, D)
    return out


_fused_cache = []


def build_fused_program():
    if _fused_cache:
        return _fused_cache[0]
    nc = bass.Bass("TRN2", target_bir_lowering=False)
    x_in = nc.dram_tensor("x", [D, TOK], F32, kind="ExternalInput").ap()
    aux_in = nc.dram_tensor("aux", [128, AUXC], F32, kind="ExternalInput").ap()
    sel_in = nc.dram_tensor("sel", [128, 64], BF16, kind="ExternalInput").ap()
    rowsel_in = nc.dram_tensor("rowsel", [8, 512], F32, kind="ExternalInput").ap()
    eye_in = nc.dram_tensor("eye", [128, 128], BF16, kind="ExternalInput").ap()
    eyef_in = nc.dram_tensor("eyef", [128, 128], F32, kind="ExternalInput").ap()
    x_out = nc.dram_tensor("xo", [D, TOK], F32, kind="ExternalOutput").ap()
    alldaps = []
    for name, decl, emit, _, _, _ in BLOCKS:
        alldaps.append(decl(nc))
    with ExitStack() as es:
        p = Prog(nc, es)
        with nc.Block() as blk:
            p.blk = blk
            with nc.named_scope("setup"):
                emit_setup(p, x_in, aux_in, sel_in, rowsel_in, eye_in=eye_in, eyef_in=eyef_in)
            for (name, decl, emit, _, _, _), daps in zip(BLOCKS, alldaps):
                with nc.named_scope(name):
                    emit(p, **daps)
            with nc.named_scope("store"):
                emit_store_x(p, x_out)
        p.blk = None
    _fused_cache.append(nc)
    return nc


def kernel_fused(**inputs):
    consts = {"aux": host_aux(), "sel": host_sel(),
              "rowsel": host_rowsel().astype(np.float32),
              "eye": host_eye(), "eyef": np.eye(128, dtype=np.float32)}
    wk = {}
    for name, decl, emit, prep, _, _ in BLOCKS:
        wk.update(prep(inputs))
    nc = build_fused_program()
    xs = [feat_major(np.asarray(inputs["L0_attn_x"])[2 * c:2 * c + 2])
          for c in range(NCORES)]
    in_maps = []
    for c in range(NCORES):
        m = dict(consts)
        m.update(wk)
        m["x"] = xs[c]
        in_maps.append(m)
    res = run_bass_kernel_spmd(nc, in_maps, core_ids=list(range(NCORES)))
    out = np.empty((B, T, D), dtype=np.float32)
    for c in range(NCORES):
        out[2 * c:2 * c + 2] = np.asarray(res.results[c]["xo"]).T.reshape(2, T, D)
    return out


_orig_kernel_unfused = kernel

def kernel(**inputs):
    import os as _os
    if _os.environ.get("MK_UNFUSED", "0") == "1":
        return _orig_kernel_unfused(**inputs)
    return kernel_fused(**inputs)
